# version 16
# baseline (speedup 1.0000x reference)
import numpy as np
from contextlib import ExitStack

import concourse.bass as bass
import concourse.mybir as mybir
from concourse.alu_op_type import AluOpType
from concourse.bass_utils import run_bass_kernel_spmd

F32 = mybir.dt.float32
BF16 = mybir.dt.bfloat16
AX = mybir.AxisListType.X
AF = mybir.ActivationFunctionType
OP = AluOpType

B, S, D = 16, 2048, 1024
NC_ = 8
BPC = B // NC_
CH = 256
NCH = S // CH
EC = D // 128
C16 = 16
NN = 256
HG = 512
LN_EPS = 1e-5

PHASES = ("a", "f")
DEBUG_S = False
NSEQ = BPC
NF_TILES = NCH
NA_CHUNKS = NCH

PASSES = ((0, 0), (0, 1), (1, 0))


class Prog:
    def __init__(self, nc):
        self.nc = nc
        self.h = {}
        self.v = {}

    def sem(self, es, *names):
        for n in names:
            self.h[n] = es.enter_context(self.nc.semaphore(n))
            self.v[n] = 0

    def inc(self, instr, name, n=1):
        instr.then_inc(self.h[name], n)
        self.v[name] += n
        return (name, self.v[name])

    def tok(self, name):
        return (name, self.v[name])

    def wait(self, eng, *deps):
        for d in deps:
            if d is None:
                continue
            name, val = d
            if val > 0:
                eng.wait_ge(self.h[name], val)


class Buffers:
    pass


class ScalarQ:

    def __init__(self, p, blk):
        self.p = p
        self.blk = blk
        self.ops = []

    def wait(self, *deps):
        deps = tuple(d for d in deps if d is not None)
        if deps:
            self.ops.append(("w", deps))

    def op(self, make, sem="sa", n=1):
        tok = None
        if sem is not None:
            self.p.v[sem] += n
            tok = (sem, self.p.v[sem])
        self.ops.append(("i", make, sem, n))
        return tok

    def flush(self):
        if not self.ops:
            return
        ops, self.ops = self.ops, []
        p = self.p

        def body(sc):
            for it in ops:
                if it[0] == "w":
                    p.wait(sc, *it[1])
                else:
                    _, make, sem, n = it
                    ins = make(sc)
                    if sem is not None:
                        ins.then_inc(p.h[sem], n)
        self.blk.scalar(body)


def _ln_tile(p, blk, sq, x_in, xn_out, bf, pre_deps=(), gamma=None, beta=None):
    def _(v):
        p.wait(v, *pre_deps)
        v.bn_stats(bf.stats[:, 0:6], x_in[:, 0:512])
        v.bn_stats(bf.stats[:, 6:12], x_in[:, 512:1024])
        v.drain()
        v.bn_aggr(bf.mv[:, :], bf.stats[:, :])
        v.drain()
        p.inc(v.tensor_scalar_add(bf.tmp1[:, :], bf.mv[:, 1:2], LN_EPS), "sv", 1)
    blk.vector(_)
    var_done = p.tok("sv")

    sq.wait(var_done)
    sq.op(lambda sc: sc.activation(bf.rstd[:, :], bf.tmp1[:, :], AF.Ln), sem=None)
    sq.op(lambda sc: sc.drain(), sem=None)
    rstd_done = sq.op(lambda sc: sc.activation(bf.rstd[:, :], bf.rstd[:, :],
                                               AF.Exp, scale=-0.5))

    def _(v):
        p.wait(v, rstd_done)
        ins = v.tensor_scalar(xn_out, x_in, bf.mv[:, 0:1], bf.rstd[:, :],
                              op0=OP.subtract, op1=OP.mult)
        if gamma is not None:
            v.drain()
            v.tensor_mul(xn_out, xn_out, gamma[:, :])
            v.drain()
            ins = v.tensor_add(xn_out, xn_out, beta[:, :])
        p.inc(ins, "sv", 1)
    blk.vector(_)
    return p.tok("sv")


def _evac_hilo(p, blk, sq, bf, items):
    hi_toks = []
    for stok, ps, hi, lo, bias in items:
        sq.wait(stok)
        if bias is None:
            hi_toks.append(sq.op(lambda sc, ps=ps, hi=hi: sc.copy(hi, ps)))
        else:
            hi_toks.append(sq.op(lambda sc, ps=ps, hi=hi, bias=bias: sc.activation(
                hi, ps, AF.Identity, bias=bias)))

    def _(v):
        for i, (stok, ps, hi, lo, bias) in enumerate(items):
            p.wait(v, hi_toks[i])
            if bias is None:
                ins = v.tensor_sub(lo, ps, hi)
            else:
                ins = v.scalar_tensor_tensor(lo, ps, bias, hi,
                                             op0=OP.add, op1=OP.subtract)
            p.inc(ins, "sv", 1)
            v.drain()
    blk.vector(_)
    return p.tok("sv")


def build_core_program(phases=PHASES, debug_s=DEBUG_S):
    nc = bass.Bass("TRN2", target_bir_lowering=False)
    p = Prog(nc)

    x_d = nc.dram_tensor("x", [BPC, S, D], F32, kind="ExternalInput").ap()
    wq_d = nc.dram_tensor("wq", [128, EC * D], F32, kind="ExternalInput").ap()
    wk_d = nc.dram_tensor("wk", [128, EC * D], F32, kind="ExternalInput").ap()
    wv_d = nc.dram_tensor("wv", [128, EC * D], F32, kind="ExternalInput").ap()
    wo_d = nc.dram_tensor("wo", [128, EC * D], F32, kind="ExternalInput").ap()
    bq_d = nc.dram_tensor("bq", [128, EC], F32, kind="ExternalInput").ap()
    bk_d = nc.dram_tensor("bk", [128, EC], F32, kind="ExternalInput").ap()
    bo_d = nc.dram_tensor("bo", [128, EC], F32, kind="ExternalInput").ap()
    pb_d = nc.dram_tensor("pjb", [128, EC], F32, kind="ExternalInput").ap()
    bvb_d = nc.dram_tensor("bvb", [128, D], F32, kind="ExternalInput").ap()
    lng_d = nc.dram_tensor("lng", [128, D], F32, kind="ExternalInput").ap()
    lnb_d = nc.dram_tensor("lnb", [128, D], F32, kind="ExternalInput").ap()
    ce_d = nc.dram_tensor("ce", [C16, D], F32, kind="ExternalInput").ap()
    gw1_d = nc.dram_tensor("gw1", [128, EC * HG], F32, kind="ExternalInput").ap()
    gb1_d = nc.dram_tensor("gb1", [128, HG // 128], F32, kind="ExternalInput").ap()
    gw2_d = nc.dram_tensor("gw2", [128, (HG // 128) * C16], F32, kind="ExternalInput").ap()
    gb2_d = nc.dram_tensor("gb2", [C16, 1], F32, kind="ExternalInput").ap()
    wc_d = nc.dram_tensor("wc", [C16, 128, EC * NN], F32, kind="ExternalInput").ap()
    bc_d = nc.dram_tensor("bc", [128, C16 * 2], F32, kind="ExternalInput").ap()
    pj_d = nc.dram_tensor("pj", [EC, 128, 32 * 128], F32, kind="ExternalInput").ap()
    id_d = nc.dram_tensor("ident", [128, 128], F32, kind="ExternalInput").ap()
    oh_d = nc.dram_tensor("oh", [C16, C16 * 128], F32, kind="ExternalInput").ap()
    out_d = nc.dram_tensor("out", [BPC, S, D], F32, kind="ExternalOutput").ap()
    skind = {"kind": "ExternalOutput"} if debug_s else {}
    s_d = nc.dram_tensor("s_scr", [BPC, 128, EC, S], F32, **skind).ap()
    xnTh_d = nc.dram_tensor("xnTh_scr", [BPC, 128, EC, S], BF16).ap()
    xnTl_d = nc.dram_tensor("xnTl_scr", [BPC, 128, EC, S], BF16).ap()
    qTh_d = nc.dram_tensor("qTh_scr", [BPC, 128, EC, S], BF16).ap()
    qTl_d = nc.dram_tensor("qTl_scr", [BPC, 128, EC, S], BF16).ap()
    vTh_d = nc.dram_tensor("vTh_scr", [BPC, S // 128, 128, D], BF16).ap()
    vTl_d = nc.dram_tensor("vTl_scr", [BPC, S // 128, 128, D], BF16).ap()
    rn_d = nc.dram_tensor("rn_scr", [BPC, 1, S], F32).ap()
    wk2_d = nc.dram_tensor("wk2_scr", [128, 2 * EC * D], BF16).ap()
    wq2_d = nc.dram_tensor("wq2_scr", [128, 2 * EC * D], BF16).ap()
    wv2_d = nc.dram_tensor("wv2_scr", [128, 2 * EC * D], BF16).ap()
    wo2_d = nc.dram_tensor("wo2_scr", [128, 2 * EC * D], BF16).ap()
    gw12_d = nc.dram_tensor("gw12_scr", [128, 2 * EC * HG], BF16).ap()
    wc2_d = nc.dram_tensor("wc2_scr", [C16, 128, 2 * EC * NN], BF16).ap()
    pj2_d = nc.dram_tensor("pj2_scr", [EC, 128, 2 * 32 * 128], BF16).ap()

    es = ExitStack()
    with es:
        bf = Buffers()
        sb = lambda name, shape, dt=F32: es.enter_context(nc.sbuf_tensor(name, shape, dt))
        bf.bq_t = sb("bq_t", [128, EC])
        bf.bk_t = sb("bk_t", [128, EC])
        bf.bo_t = sb("bo_t", [128, EC])
        bf.pjb_t = sb("pjb_t", [128, EC])
        bf.gb1_t = sb("gb1_t", [128, HG // 128])
        bf.gb2_t = sb("gb2_t", [C16, 1])
        bf.bc_t = sb("bc_t", [128, C16 * 2])
        bf.id_t = sb("id_t", [128, 128])
        bf.ones1 = sb("ones1", [1, C16])
        bf.cnTh = sb("cnTh", [128, EC * C16], BF16)
        bf.cnTl = sb("cnTl", [128, EC * C16], BF16)
        bf.stats = sb("stats", [128, 12])
        bf.mv = sb("mv", [128, 2])
        bf.tmp1 = sb("tmp1", [128, 1])
        bf.rstd = sb("rstd", [128, 1])
        bf.nmx = sb("nmx", [128, 1])
        bf.mxv = sb("mxv", [128, 1])
        bf.sme = sb("sme", [128, 1])
        bf.rcp = sb("rcp", [128, 1])
        bf.rcp2 = sb("rcp2", [128, 1])
        bf.ss_t = sb("ss_t", [128, 1])
        bf.pbk = [es.enter_context(nc.psum_tensor(f"pb{i}", [128, 512], F32))
                  for i in range(8)]

        p.sem(es, "sdk", "sdx0", "sdx1", "sdwA", "sdwB", "sdc0", "sdc1",
              "sdp0", "sdp1", "sdsi", "sdt0", "sdt1", "sds0", "sds1",
              "st", "sa", "sv", "sgc", "sdso", "sdo", "sdsq", "sdsv", "sdsr")

        with nc.Block() as blk:
            sq = ScalarQ(p, blk)
            pro_es = ExitStack()
            psb = lambda name, shape, dt=F32: pro_es.enter_context(
                nc.sbuf_tensor(name, shape, dt))
            ce_t = psb("ce_t", [C16, D])
            cesc = psb("cesc", [C16, D])
            e16a = psb("e16a", [C16, 1])
            e16b = psb("e16b", [C16, 1])
            cnT32 = psb("cnT32", [128, EC * C16])
            w32 = [psb(f"prep32_{i}", [128, EC * D]) for i in range(2)]
            w16 = [psb(f"prep16_{i}", [128, 2 * EC * D], BF16) for i in range(2)]

            def _(sy):
                for dst, src in ((bf.bq_t, bq_d), (bf.bk_t, bk_d), (bf.bo_t, bo_d),
                                 (bf.pjb_t, pb_d), (bf.gb1_t, gb1_d),
                                 (bf.gb2_t, gb2_d), (bf.bc_t, bc_d),
                                 (bf.id_t, id_d), (ce_t, ce_d)):
                    p.inc(sy.dma_start(out=dst[:, :], in_=src), "sdk", 16)
            blk.sync(_)
            kload = p.tok("sdk")

            def _(g):
                p.inc(g.memset(bf.ones1[:, :], 1.0), "sgc", 1)
            blk.gpsimd(_)

            units = [("wk", wk_d, wk2_d, EC * D, None),
                     ("wq", wq_d, wq2_d, EC * D, 1.0 / 32.0),
                     ("wv", wv_d, wv2_d, EC * D, None),
                     ("wo", wo_d, wo2_d, EC * D, None),
                     ("gw1", gw1_d, gw12_d, EC * HG, None)]
            for c in range(C16):
                units.append((f"wc{c}", wc_d[c, :, :], wc2_d[c, :, :], EC * NN, None))
            for m in range(EC):
                units.append((f"pj{m}", pj_d[m, :, :], pj2_d[m, :, :], 32 * 128, None))

            done_toks = [None, None]
            in_sems = ("sdwB", "sdsi")
            out_sems = ("sdso", "sdo")
            for ui, (nm, src, dst, width, scale) in enumerate(units):
                slot = ui % 2

                def _(sy, src=src, width=width, slot=slot, dep=done_toks[slot]):
                    p.wait(sy, dep)
                    p.inc(sy.dma_start(out=w32[slot][:, 0:width], in_=src),
                          in_sems[slot], 16)
                blk.sync(_)
                in_tok = p.tok(in_sems[slot])

                if scale is not None:
                    def _(v, slot=slot, width=width, dep=in_tok, scale=scale):
                        p.wait(v, dep)
                        p.inc(v.tensor_scalar_mul(w32[slot][:, 0:width],
                                                  w32[slot][:, 0:width], scale),
                              "sv", 1)
                    blk.vector(_)
                    in_tok = p.tok("sv")

                sq.wait(in_tok)
                hi_tok = sq.op(lambda sc, slot=slot, width=width: sc.copy(
                    w16[slot][:, 0:width], w32[slot][:, 0:width]))

                def _(v, slot=slot, width=width, dep=hi_tok):
                    p.wait(v, dep)
                    p.inc(v.tensor_sub(w16[slot][:, width:2 * width],
                                       w32[slot][:, 0:width],
                                       w16[slot][:, 0:width]), "sv", 1)
                blk.vector(_)
                lo_tok = p.tok("sv")

                def _(g, dst=dst, width=width, slot=slot, dep=lo_tok):
                    p.wait(g, dep)
                    p.inc(g.dma_start(out=dst, in_=w16[slot][:, 0:2 * width]),
                          out_sems[slot], 16)
                blk.gpsimd(_)
                done_toks[slot] = p.tok(out_sems[slot])

            sq.wait(kload)
            sq_done = sq.op(lambda sc: sc.activation(cesc[:, :], ce_t[:, :],
                                                     AF.Square, accum_out=e16a[:, :]))
            sq.flush()

            def _(v):
                p.wait(v, sq_done)
                p.inc(v.tensor_scalar_max(e16a[:, :], e16a[:, :], 1e-24), "sv", 1)
            blk.vector(_)
            mx_done = p.tok("sv")

            sq.wait(mx_done)
            sq.op(lambda sc: sc.activation(e16b[:, :], e16a[:, :], AF.Ln), sem=None)
            sq.op(lambda sc: sc.drain(), sem=None)
            nrm_done = sq.op(lambda sc: sc.activation(e16a[:, :], e16b[:, :],
                                                      AF.Exp, scale=-0.5))
            sq.flush()

            def _(v):
                p.wait(v, nrm_done)
                v.tensor_scalar(cesc[:, :], ce_t[:, :], e16a[:, :], None, op0=OP.mult)
                v.drain()
                v.tensor_scalar_mul(bf.bq_t[:, :], bf.bq_t[:, :], 1.0 / 32.0)
                v.drain()
                p.inc(v.tensor_scalar_mul(bf.gb2_t[:, :], bf.gb2_t[:, :], 0.5), "sv", 1)
            blk.vector(_)
            cn_done = p.tok("sv")

            sa_base = p.v["sa"]

            def _(t):
                p.wait(t, cn_done)
                for e in range(EC):
                    if e >= 2:
                        p.wait(t, ("sa", sa_base + e - 1))
                    p.inc(t.transpose(bf.pbk[e % 2][0:128, 0:C16],
                                      cesc[:, e * 128:(e + 1) * 128],
                                      bf.id_t[0:C16, 0:C16]), "st", 1)
            blk.tensor(_)
            st_base = p.v["st"] - EC

            for e in range(EC):
                sq.wait(("st", st_base + e + 1))
                sq.op(lambda sc, e=e: sc.copy(cnT32[:, e * C16:(e + 1) * C16],
                                              bf.pbk[e % 2][0:128, 0:C16]))
            sq.op(lambda sc: sc.drain(), sem=None)
            cnh_tok = sq.op(lambda sc: sc.copy(bf.cnTh[:, :], cnT32[:, :]))
            sq.flush()

            def _(v):
                p.wait(v, cnh_tok)
                p.inc(v.tensor_sub(bf.cnTl[:, :], cnT32[:, :], bf.cnTh[:, :]), "sv", 1)
            blk.vector(_)

            def _(sy):
                p.wait(sy, p.tok("sdwB"), p.tok("sdsi"), p.tok("sdk"))
            blk.sync(_)

            def _(g):
                p.wait(g, p.tok("sdso"), p.tok("sdo"))
            blk.gpsimd(_)
        pro_es.close()

        for sqn in range(NSEQ):
            if "a" in phases:
                _phase_a(nc, p, es, sqn, bf, x_d, s_d, bvb_d,
                         wk2_d, wq2_d, wv2_d, wo2_d,
                         xnTh_d, xnTl_d, qTh_d, qTl_d, vTh_d, vTl_d, rn_d)
            if "f" in phases:
                _phase_f(nc, p, es, sqn, bf, x_d, s_d, out_d, gw12_d, gw2_d,
                         wc2_d, pj2_d, lng_d, lnb_d, oh_d,
                         xnTh_d, xnTl_d, rn_d)

        with nc.Block() as blk:
            def _(sy):
                for s in ("sdk", "sdx0", "sdx1", "sdwA", "sdwB", "sdc0", "sdc1",
                          "sdp0", "sdp1", "sdsi", "sdt0", "sdt1", "sds0", "sds1"):
                    p.wait(sy, p.tok(s))
            blk.sync(_)

            def _(g):
                p.wait(g, p.tok("sdso"), p.tok("sdo"))
            blk.gpsimd(_)
    return nc


def _phase_a(nc, p, es, sqn, bf, x_d, s_d, bvb_d, wk2_d, wq2_d, wv2_d, wo2_d,
             xnTh_d, xnTl_d, qTh_d, qTl_d, vTh_d, vTl_d, rn_d):
    with ExitStack() as pes:
        sbX = lambda name, shape, dt=F32: pes.enter_context(
            nc.sbuf_tensor(name, shape, dt))
        KTh = [sbX(f"KTh{e}_s{sqn}", [128, S], BF16) for e in range(EC)]
        KTl = [sbX(f"KTl{e}_s{sqn}", [128, S], BF16) for e in range(EC)]

        with ExitStack() as a1es:
            sb1 = lambda name, shape, dt=F32: a1es.enter_context(
                nc.sbuf_tensor(name, shape, dt))
            wk16 = sb1(f"wk16_{sqn}", [128, 2 * EC * D], BF16)
            wq16 = sb1(f"wq16_{sqn}", [128, 2 * EC * D], BF16)
            wv16 = sb1(f"wv16_{sqn}", [128, 2 * EC * D], BF16)
            bvb_t = sb1(f"bvb_s{sqn}", [128, D])
            xa = sb1(f"xa_s{sqn}", [128, 2 * D])
            xnTh = sb1(f"xnTh_s{sqn}", [128, EC * CH], BF16)
            xnTl = sb1(f"xnTl_s{sqn}", [128, EC * CH], BF16)
            QTh = sb1(f"QTh_s{sqn}", [128, EC * CH], BF16)
            QTl = sb1(f"QTl_s{sqn}", [128, EC * CH], BF16)
            VTh = sb1(f"VTh_s{sqn}", [128, 2 * D], BF16)
            VTl = sb1(f"VTl_s{sqn}", [128, 2 * D], BF16)
            V32 = sb1(f"V32_s{sqn}", [128, 2 * D])
            rn_row = sb1(f"rn_row_s{sqn}", [1, CH])
            rn_park = sb1(f"rn_park_s{sqn}", [128, 4])

            def wof(w16buf, e, m, base):
                return (w16buf[:, e * D + m * 128:e * D + (m + 1) * 128],
                        w16buf[:, base + e * D + m * 128:base + e * D + (m + 1) * 128])

            with nc.Block() as blk:
                sq = ScalarQ(p, blk)

                def _(sy):
                    p.inc(sy.dma_start(out=wk16[:, :], in_=wk2_d), "sdwA", 16)
                    p.inc(sy.dma_start(out=wq16[:, :], in_=wq2_d), "sdwA", 16)
                    p.inc(sy.dma_start(out=wv16[:, :], in_=wv2_d), "sdwA", 16)
                    p.inc(sy.dma_start(out=bvb_t[:, :], in_=bvb_d), "sdwA", 16)
                blk.sync(_)
                w_tok = p.tok("sdwA")

                lnst = {}

                def emit_ln(c, xfree):
                    t0c = c * CH
                    sx = "sdx0" if c % 2 == 0 else "sdx1"
                    rb = (c % 2) * 2

                    def _(sy, t0=t0c, sx=sx, fdep=xfree):
                        p.wait(sy, fdep)
                        p.inc(sy.dma_start(
                            out=xa[:, :].rearrange("p (a d) -> p a d", a=2),
                            in_=x_d[sqn, t0:t0 + CH, :].rearrange(
                                "(a p) d -> p a d", p=128)), sx, 16)
                    blk.sync(_)
                    x_tok = p.tok(sx)
                    lnst["x_tok"] = x_tok

                    ln_toks = []
                    rn_parked = []
                    for tt in range(2):
                        ln_toks.append(_ln_tile(p, blk, sq,
                                                xa[:, tt * D:(tt + 1) * D],
                                                xa[:, tt * D:(tt + 1) * D], bf,
                                                pre_deps=(x_tok, p.tok("sa"))))
                        def _(v, dep=ln_toks[-1]):
                            p.wait(v, dep)
                            v.tensor_mul(bf.rcp2[:, :], bf.mv[:, 1:2], bf.rstd[:, :])
                            v.drain()
                            v.tensor_mul(bf.rcp2[:, :], bf.rcp2[:, :], bf.rstd[:, :])
                            v.drain()
                            p.inc(v.tensor_scalar(bf.rcp2[:, :], bf.rcp2[:, :],
                                                  float(D), 1e-24,
                                                  op0=OP.mult, op1=OP.max), "sv", 1)
                        blk.vector(_)
                        ss_tok = p.tok("sv")

                        sq.wait(ss_tok)
                        sq.op(lambda sc: sc.activation(bf.rcp2[:, :], bf.rcp2[:, :],
                                                       AF.Ln), sem=None)
                        sq.op(lambda sc: sc.drain(), sem=None)
                        rcp_tok = sq.op(lambda sc: sc.activation(
                            bf.rcp2[:, :], bf.rcp2[:, :], AF.Exp, scale=-0.5))

                        def _(v, dep=rcp_tok, tt=tt, rb=rb):
                            p.wait(v, dep)
                            p.inc(v.tensor_copy(rn_park[:, rb + tt:rb + tt + 1],
                                                bf.rcp2[:, :]), "sv", 1)
                        blk.vector(_)
                        rn_parked.append(p.tok("sv"))
                    lnst["ln"] = ln_toks[-1]
                    lnst["rnp"] = rn_parked[-1]

                emit_ln(0, None)
                spp = {"x": None, "q": None, "v": None, "r": None}
                spn = {}
                for n in range(NA_CHUNKS):
                    t0 = n * CH
                    ln_dep = lnst["ln"]
                    rnp_dep = lnst["rnp"]
                    rn_done = None

                    sv_tb = p.v["sv"]

                    def _(t, dep=ln_dep):
                        p.wait(t, dep, p.tok("sv"))
                        for e in range(EC):
                            if e >= 2:
                                p.wait(t, ("sv", sv_tb + 2 * (e - 1)))
                            for tt in range(2):
                                p.inc(t.transpose(
                                    bf.pbk[6 + (e % 2)][:, tt * 128:tt * 128 + 128],
                                    xa[:, tt * D + e * 128:tt * D + (e + 1) * 128],
                                    bf.id_t[:, :]), "st", 1)
                    blk.tensor(_)
                    tr_base = p.v["st"] - 16

                    sq.wait(spp["x"])
                    xl_tok = _evac_hilo(p, blk, sq, bf,
                        [(("st", tr_base + 2 * e + tt + 1),
                          bf.pbk[6 + (e % 2)][:, tt * 128:tt * 128 + 128],
                          xnTh[:, e * CH + tt * 128:e * CH + tt * 128 + 128],
                          xnTl[:, e * CH + tt * 128:e * CH + tt * 128 + 128],
                          None)
                         for e in range(EC) for tt in range(2)])

                    def _(g, t0=t0, dep=xl_tok):
                        p.wait(g, dep)
                        for dst, srcb in ((xnTh_d, xnTh), (xnTl_d, xnTl)):
                            p.inc(g.dma_start(
                                out=dst[sqn, :, :, t0:t0 + CH],
                                in_=srcb[:, :].rearrange("p (e q) -> p e q", e=EC)),
                                "sdso", 16)
                    blk.gpsimd(_)
                    spn["x"] = p.tok("sdso")

                    sv_kb = p.v["sv"]

                    def _(t, dep=xl_tok):
                        p.wait(t, dep, w_tok)
                        for m in range(EC):
                            if m >= 2:
                                p.wait(t, ("sv", sv_kb + m - 1))
                            bank = bf.pbk[m % 2]
                            for e in range(EC):
                                wh, wl = wof(wk16, e, m, EC * D)
                                xh = xnTh[:, e * CH:(e + 1) * CH]
                                xl = xnTl[:, e * CH:(e + 1) * CH]
                                for pi, (ws, xs) in enumerate(PASSES):
                                    mm = t.matmul(bank[:, 0:CH],
                                                  wh if ws == 0 else wl,
                                                  xh if xs == 0 else xl,
                                                  start=(e == 0 and pi == 0),
                                                  stop=(e == EC - 1 and pi == 2))
                            p.inc(mm, "st", 1)
                    blk.tensor(_)
                    k_base = p.v["st"] - EC

                    _evac_hilo(p, blk, sq, bf,
                               [(("st", k_base + m + 1), bf.pbk[m % 2][:, 0:CH],
                                 KTh[m][:, t0:t0 + CH], KTl[m][:, t0:t0 + CH],
                                 bf.bk_t[:, m:m + 1]) for m in range(EC)])

                    sq.wait(spp["q"])
                    sv_qb = p.v["sv"]

                    def _(t, dep=xl_tok):
                        p.wait(t, dep, w_tok, ("sv", sv_qb))
                        for m in range(EC):
                            if m >= 2:
                                p.wait(t, ("sv", sv_qb + m - 1))
                            bank = bf.pbk[2 + (m % 2)]
                            for e in range(EC):
                                wh, wl = wof(wq16, e, m, EC * D)
                                xh = xnTh[:, e * CH:(e + 1) * CH]
                                xl = xnTl[:, e * CH:(e + 1) * CH]
                                for pi, (ws, xs) in enumerate(PASSES):
                                    mm = t.matmul(bank[:, 0:CH],
                                                  wh if ws == 0 else wl,
                                                  xh if xs == 0 else xl,
                                                  start=(e == 0 and pi == 0),
                                                  stop=(e == EC - 1 and pi == 2))
                            p.inc(mm, "st", 1)
                    blk.tensor(_)
                    q_base = p.v["st"] - EC

                    qe_tok = _evac_hilo(p, blk, sq, bf,
                        [(("st", q_base + m + 1), bf.pbk[2 + (m % 2)][:, 0:CH],
                          QTh[:, m * CH:(m + 1) * CH], QTl[:, m * CH:(m + 1) * CH],
                          bf.bq_t[:, m:m + 1]) for m in range(EC)])

                    def _(g, t0=t0, dep=qe_tok):
                        p.wait(g, dep)
                        for dst, srcb in ((qTh_d, QTh), (qTl_d, QTl)):
                            p.inc(g.dma_start(
                                out=dst[sqn, :, :, t0:t0 + CH],
                                in_=srcb[:, :].rearrange("p (e q) -> p e q", e=EC)),
                                "sdsq", 16)
                    blk.gpsimd(_)
                    spn["q"] = p.tok("sdsq")

                    v_guard = p.tok("sv")
                    if n + 1 < NA_CHUNKS:
                        emit_ln(n + 1, ("st", tr_base + 16))

                    sv_vb = p.v["sv"]

                    def _(t, dep=xl_tok):
                        p.wait(t, dep, v_guard)
                        idx = 0
                        for tt in range(2):
                            for hf in range(2):
                                idx += 1
                                if idx >= 3:
                                    p.wait(t, ("sv", sv_vb + idx - 2))
                                bank = bf.pbk[4 + hf]
                                for e in range(EC):
                                    xh = xnTh[:, e * CH + tt * 128:e * CH + (tt + 1) * 128]
                                    xl = xnTl[:, e * CH + tt * 128:e * CH + (tt + 1) * 128]
                                    whh = wv16[:, e * D + hf * 512:e * D + (hf + 1) * 512]
                                    wll = wv16[:, EC * D + e * D + hf * 512:
                                               EC * D + e * D + (hf + 1) * 512]
                                    for pi, (ws, xs) in enumerate(PASSES):
                                        mm = t.matmul(bank[:, 0:512],
                                                      xh if xs == 0 else xl,
                                                      whh if ws == 0 else wll,
                                                      start=(e == 0 and pi == 0),
                                                      stop=(e == EC - 1 and pi == 2))
                                p.inc(mm, "st", 1)
                    blk.tensor(_)
                    v_base = p.v["st"] - 4

                    v32_toks = []

                    def _(v, v_base=v_base, sp=spp["v"]):
                        p.wait(v, sp)
                        for idx in range(4):
                            tt, hf = divmod(idx, 2)
                            sl = slice(tt * D + hf * 512, tt * D + (hf + 1) * 512)
                            p.wait(v, ("st", v_base + idx + 1))
                            p.inc(v.tensor_add(V32[:, sl], bf.pbk[4 + hf][:, 0:512],
                                               bvb_t[:, hf * 512:(hf + 1) * 512]),
                                  "sv", 1)
                            v.drain()
                    blk.vector(_)
                    v32_base = p.v["sv"] - 4
                    for idx in range(4):
                        v32_toks.append(("sv", v32_base + idx + 1))

                    vh_toks = []
                    for idx in range(4):
                        tt, hf = divmod(idx, 2)
                        sl = slice(tt * D + hf * 512, tt * D + (hf + 1) * 512)
                        sq.wait(v32_toks[idx])
                        vh_toks.append(sq.op(lambda sc, sl=sl: sc.copy(
                            VTh[:, sl], V32[:, sl])))

                    def _(v):
                        for idx in range(4):
                            tt, hf = divmod(idx, 2)
                            sl = slice(tt * D + hf * 512, tt * D + (hf + 1) * 512)
                            p.wait(v, vh_toks[idx])
                            ins = v.tensor_sub(VTl[:, sl], V32[:, sl], VTh[:, sl])
                            v.drain()
                        p.inc(ins, "sv", 1)
                    blk.vector(_)
                    ve_tok = p.tok("sv")

                    def _(g, n=n, dep=ve_tok):
                        p.wait(g, dep)
                        for dst, srcb in ((vTh_d, VTh), (vTl_d, VTl)):
                            p.inc(g.dma_start(
                                out=dst[sqn, 2 * n:2 * n + 2, :, :].rearrange(
                                    "a p d -> p a d"),
                                in_=srcb[:, :].rearrange("p (a d) -> p a d", a=2)),
                                "sdsv", 16)
                    blk.gpsimd(_)
                    spn["v"] = p.tok("sdsv")

                    def _(t, dep=rnp_dep, rb=(n % 2) * 2):
                        p.wait(t, dep, p.tok("sv"))
                        for tt in range(2):
                            p.inc(t.transpose(bf.pbk[2 + tt][0:1, 0:128],
                                              rn_park[:, rb + tt:rb + tt + 1],
                                              bf.id_t[:, :]),
                                  "st", 1)
                    blk.tensor(_)
                    rt_base = p.v["st"] - 2

                    sq.wait(("st", rt_base + 2), spp["r"])
                    for tt in range(2):
                        rn_done = sq.op(lambda sc, tt=tt: sc.copy(
                            rn_row[0:1, tt * 128:(tt + 1) * 128],
                            bf.pbk[2 + tt][0:1, 0:128]))
                    sq.flush()

                    def _(g, t0=t0, dep=rn_done):
                        p.wait(g, dep)
                        p.inc(g.dma_start(out=rn_d[sqn, 0:1, t0:t0 + CH],
                                          in_=rn_row[0:1, :]), "sdsr", 16)
                    blk.gpsimd(_)
                    spn["r"] = p.tok("sdsr")
                    spp = dict(spn)

                def _(g):
                    p.wait(g, p.tok("sdso"), p.tok("sdsq"), p.tok("sdsv"),
                           p.tok("sdsr"))
                blk.gpsimd(_)

                def _(sy):
                    p.wait(sy, p.tok("sdwA"), p.tok("sdx0"), p.tok("sdx1"))
                blk.sync(_)

        with ExitStack() as a2es:
            sb2 = lambda name, shape, dt=F32: a2es.enter_context(
                nc.sbuf_tensor(name, shape, dt))
            wo16 = sb2(f"wo16_{sqn}", [128, 2 * EC * D], BF16)
            QThb = [sb2(f"aQTh{i}_s{sqn}", [128, EC * CH], BF16) for i in range(2)]
            QTlb = [sb2(f"aQTl{i}_s{sqn}", [128, EC * CH], BF16) for i in range(2)]
            VThb = [sb2(f"aVTh{i}_s{sqn}", [128, 2 * D], BF16) for i in range(2)]
            VTlb = [sb2(f"aVTl{i}_s{sqn}", [128, 2 * D], BF16) for i in range(2)]
            ET = sb2(f"ET_s{sqn}", [128, 2 * CH])
            ASb = [sb2(f"AS{i}_s{sqn}", [128, 2 * CH]) for i in range(2)]
            ATTh = sb2(f"ATTh_s{sqn}", [128, 2 * CH], BF16)
            ATTl = sb2(f"ATTl_s{sqn}", [128, 2 * CH], BF16)
            OTh = sb2(f"OTh_s{sqn}", [128, EC * CH], BF16)
            OTl = sb2(f"OTl_s{sqn}", [128, EC * CH], BF16)
            sT = sb2(f"sT_s{sqn}", [128, EC * CH])
            nmx8 = sb2(f"nmx8_s{sqn}", [128, NCH])
            sme8 = sb2(f"sme8_s{sqn}", [128, NCH])

            with nc.Block() as blk:
                sq = ScalarQ(p, blk)

                def _(sy):
                    p.inc(sy.dma_start(out=wo16[:, :], in_=wo2_d), "sdwB", 16)
                    for dst, src in ((QThb[0], qTh_d), (QTlb[0], qTl_d)):
                        p.inc(sy.dma_start(
                            out=dst[:, :].rearrange("p (e q) -> p e q", e=EC),
                            in_=src[sqn, :, :, 0:CH]), "sdc0", 16)
                    for dst, src in ((VThb[0], vTh_d), (VTlb[0], vTl_d)):
                        p.inc(sy.dma_start(
                            out=dst[:, :].rearrange("p (a d) -> p a d", a=2),
                            in_=src[sqn, 0:2, :, :].rearrange("a p d -> p a d")),
                            "sdc0", 16)
                blk.sync(_)
                wo_tok = p.tok("sdwB")
                q_toks = [p.tok("sdc0"), None]
                v_toks = [q_toks[0], None]
                q_use = [None, None]
                atg_hist = [None, None]
                as_prev = None
                prev = None

                def emit_atgroup(pv):
                    def _(t, dep=pv["as_tok"], ASp=pv["AS"]):
                        p.wait(t, dep, p.tok("sv"))
                        for qt in range(2):
                            for kt in range(2):
                                p.inc(t.transpose(
                                    bf.pbk[6 + qt][:, kt * 128:(kt + 1) * 128],
                                    ASp[:, qt * CH + kt * 128:qt * CH + (kt + 1) * 128],
                                    bf.id_t[:, :]), "st", 1)
                    blk.tensor(_)
                    atg_tok = p.tok("st")
                    atg_hist[pv["n"] % 2] = atg_tok
                    pv["att_tok"] = _evac_hilo(p, blk, sq, bf,
                        [(atg_tok, bf.pbk[6 + qt][:, kt * 128:(kt + 1) * 128],
                          ATTh[:, kt * CH + qt * 128:kt * CH + qt * 128 + 128],
                          ATTl[:, kt * CH + qt * 128:kt * CH + qt * 128 + 128],
                          None) for qt in range(2) for kt in range(2)])

                def emit_av(pv):
                    npv = pv["n"]
                    parp = npv % 2
                    VThp, VTlp = VThb[parp], VTlb[parp]
                    sv_av = p.v["sv"]

                    def _(t, dep=pv["att_tok"], vdep=v_toks[parp],
                          VTh=VThp, VTl=VTlp):
                        p.wait(t, dep, vdep)
                        for e in range(EC):
                            if e >= 2:
                                p.wait(t, ("sv", sv_av + e - 1))
                            bank = bf.pbk[6 + (e % 2)]
                            first = True
                            for kt in range(2):
                                vh = VTh[:, kt * D + e * 128:kt * D + (e + 1) * 128]
                                vl = VTl[:, kt * D + e * 128:kt * D + (e + 1) * 128]
                                ah = ATTh[:, kt * CH:(kt + 1) * CH]
                                al = ATTl[:, kt * CH:(kt + 1) * CH]
                                for ws, xs in PASSES:
                                    mm = t.matmul(bank[:, 0:CH],
                                                  vh if ws == 0 else vl,
                                                  ah if xs == 0 else al,
                                                  start=first,
                                                  stop=(kt == 1 and (ws, xs) == PASSES[-1]))
                                    first = False
                            p.inc(mm, "st", 1)
                    blk.tensor(_)
                    av_base = p.v["st"] - EC
                    av_tok = p.tok("st")

                    ot_tok = _evac_hilo(p, blk, sq, bf,
                        [(("st", av_base + e + 1), bf.pbk[6 + (e % 2)][:, 0:CH],
                          OTh[:, e * CH:(e + 1) * CH], OTl[:, e * CH:(e + 1) * CH],
                          None) for e in range(EC)])
                    return av_tok, ot_tok

                def emit_wo(pv, ot_tok):
                    npv = pv["n"]
                    t0p = npv * CH
                    sa_wo = p.v["sa"]
                    sdso_prev = p.tok("sdso")

                    def _(t, dep=ot_tok, wdep=wo_tok):
                        p.wait(t, dep, wdep)
                        for m in range(EC):
                            if m >= 2:
                                p.wait(t, ("sa", sa_wo + m - 1))
                            bank = bf.pbk[4 + (m % 2)]
                            for e in range(EC):
                                wh = wo16[:, e * D + m * 128:e * D + (m + 1) * 128]
                                wl = wo16[:, EC * D + e * D + m * 128:
                                          EC * D + e * D + (m + 1) * 128]
                                oh_ = OTh[:, e * CH:(e + 1) * CH]
                                ol_ = OTl[:, e * CH:(e + 1) * CH]
                                for pi, (ws, xs) in enumerate(PASSES):
                                    mm = t.matmul(bank[:, 0:CH],
                                                  wh if ws == 0 else wl,
                                                  oh_ if xs == 0 else ol_,
                                                  start=(e == 0 and pi == 0),
                                                  stop=(e == EC - 1 and pi == 2))
                            p.inc(mm, "st", 1)
                    blk.tensor(_)
                    wo_base = p.v["st"] - EC
                    wo_done = p.tok("st")

                    se_tok = None
                    sq.wait(sdso_prev)
                    for m in range(EC):
                        sq.wait(("st", wo_base + m + 1))
                        se_tok = sq.op(lambda sc, m=m: sc.activation(
                            sT[:, m * CH:(m + 1) * CH], bf.pbk[4 + (m % 2)][:, 0:CH],
                            AF.Identity, bias=bf.bo_t[:, m:m + 1]))
                    sq.flush()

                    def _(g, t0=t0p, dep=se_tok):
                        p.wait(g, dep)
                        p.inc(g.dma_start(out=s_d[sqn, :, :, t0:t0 + CH],
                                          in_=sT[:, :].rearrange("p (e q) -> p e q",
                                                                 e=EC)), "sdso", 16)
                    blk.gpsimd(_)
                    return wo_done

                for n in range(NA_CHUNKS):
                    t0 = n * CH
                    par = n % 2
                    QTh, QTl = QThb[par], QTlb[par]
                    AS = ASb[par]

                    def _(t, qdep=q_toks[par], adep=as_prev, QTh=QTh, QTl=QTl):
                        p.wait(t, qdep, adep)
                        for mp in range(4):
                            bank = bf.pbk[mp]
                            for e in range(EC):
                                qh = QTh[:, e * CH:e * CH + 128]
                                ql = QTl[:, e * CH:e * CH + 128]
                                kh = KTh[e][:, mp * 512:(mp + 1) * 512]
                                kl = KTl[e][:, mp * 512:(mp + 1) * 512]
                                for pi, (ws, xs) in enumerate(PASSES):
                                    mm = t.matmul(bank[:, 0:512],
                                                  qh if ws == 0 else ql,
                                                  kh if xs == 0 else kl,
                                                  start=(e == 0 and pi == 0),
                                                  stop=(e == EC - 1 and pi == 2))
                            p.inc(mm, "st", 1)
                    blk.tensor(_)
                    sc_base0 = p.v["st"] - 4

                    exq = [[], []]
                    as_toks = [None, None]

                    def emit_reduce(qt, sc_base):
                        def _(v, sc_base=sc_base):
                            for m in range(NCH):
                                mp, half = divmod(m, 2)
                                p.wait(v, ("st", sc_base + mp + 1))
                                p.inc(v.reduce_max(
                                    nmx8[:, m:m + 1],
                                    bf.pbk[mp][:, half * CH:(half + 1) * CH],
                                    axis=AX, negate=True), "sv", 1)
                        blk.vector(_)
                        return p.v["sv"] - NCH

                    def emit_expacc(qt, nm_base, n=n, AS=AS):
                        acc0_base = p.v["sv"]
                        for m in range(NCH):
                            mp, half = divmod(m, 2)
                            sq.wait(("sv", nm_base + m + 1))
                            if m >= 2:
                                sq.wait(("sv", acc0_base + m - 1))
                            exq[qt].append(sq.op(
                                lambda sc, mp=mp, half=half, m=m: sc.activation(
                                    ET[:, (m % 2) * CH:(m % 2 + 1) * CH],
                                    bf.pbk[mp][:, half * CH:(half + 1) * CH],
                                    AF.Exp, bias=nmx8[:, m:m + 1],
                                    accum_out=sme8[:, m:m + 1])))
                        ex_base = p.v["sa"] - NCH

                        def _(v, qt=qt, n=n, ex_base=ex_base, AS=AS,
                              gdep=(atg_hist[n % 2] if qt == 0 else None)):
                            p.wait(v, gdep)
                            for m in range(NCH):
                                p.wait(v, ("sa", ex_base + m + 1))
                                v.reciprocal(bf.rcp[:, :], sme8[:, m:m + 1])
                                v.drain()
                                r = bf.rcp
                                if m == n:
                                    v.tensor_scalar_mul(bf.rcp2[:, :],
                                                        bf.rcp[:, :], 2.0)
                                    v.drain()
                                    r = bf.rcp2
                                dst = AS[:, qt * CH:(qt + 1) * CH]
                                esl = ET[:, (m % 2) * CH:(m % 2 + 1) * CH]
                                if m == 0:
                                    ins = v.tensor_scalar(dst, esl, r[:, :], None,
                                                          op0=OP.mult)
                                else:
                                    ins = v.scalar_tensor_tensor(
                                        dst, esl, r[:, :], dst,
                                        op0=OP.mult, op1=OP.add)
                                p.inc(ins, "sv", 1)
                                if m < NCH - 1:
                                    v.drain()
                        blk.vector(_)
                        as_toks[qt] = p.tok("sv")

                    emit_expacc(0, emit_reduce(0, sc_base0))

                    if prev is not None:
                        emit_atgroup(prev)

                    def _(t, qdep=q_toks[par], QTh=QTh, QTl=QTl,
                          ex0=list(exq[0])):
                        p.wait(t, qdep)
                        for mp in range(4):
                            p.wait(t, ex0[2 * mp + 1])
                            bank = bf.pbk[mp]
                            for e in range(EC):
                                qh = QTh[:, e * CH + 128:e * CH + 256]
                                ql = QTl[:, e * CH + 128:e * CH + 256]
                                kh = KTh[e][:, mp * 512:(mp + 1) * 512]
                                kl = KTl[e][:, mp * 512:(mp + 1) * 512]
                                for pi, (ws, xs) in enumerate(PASSES):
                                    mm = t.matmul(bank[:, 0:512],
                                                  qh if ws == 0 else ql,
                                                  kh if xs == 0 else kl,
                                                  start=(e == 0 and pi == 0),
                                                  stop=(e == EC - 1 and pi == 2))
                            p.inc(mm, "st", 1)
                    blk.tensor(_)
                    sc_base1 = p.v["st"] - 4
                    q_use[par] = p.tok("st")

                    nm_base1 = emit_reduce(1, sc_base1)

                    if prev is not None:
                        av_tok, ot_tok = emit_av(prev)
                    emit_expacc(1, nm_base1)
                    as_prev = as_toks[1]

                    if prev is not None:
                        emit_wo(prev, ot_tok)
                        if n + 1 < NA_CHUNKS:
                            def _(sy, n=n, qfree=q_use[(n + 1) % 2], vfree=av_tok):
                                pr = (n + 1) % 2
                                sem_q = "sdc0" if pr == 0 else "sdc1"
                                p.wait(sy, qfree)
                                for dst, srcd in ((QThb[pr], qTh_d),
                                                  (QTlb[pr], qTl_d)):
                                    p.inc(sy.dma_start(
                                        out=dst[:, :].rearrange("p (e q) -> p e q",
                                                                e=EC),
                                        in_=srcd[sqn, :, :,
                                                 (n + 1) * CH:(n + 2) * CH]),
                                        sem_q, 16)
                                p.wait(sy, vfree)
                                for dst, srcd in ((VThb[pr], vTh_d),
                                                  (VTlb[pr], vTl_d)):
                                    p.inc(sy.dma_start(
                                        out=dst[:, :].rearrange("p (a d) -> p a d",
                                                                a=2),
                                        in_=srcd[sqn, 2 * n + 2:2 * n + 4, :, :]
                                        .rearrange("a p d -> p a d")), sem_q, 16)
                            blk.sync(_)
                            q_toks[(n + 1) % 2] = p.tok(
                                "sdc0" if (n + 1) % 2 == 0 else "sdc1")
                            v_toks[(n + 1) % 2] = q_toks[(n + 1) % 2]
                    elif n + 1 < NA_CHUNKS:
                        def _(sy, n=n):
                            pr = (n + 1) % 2
                            sem_q = "sdc0" if pr == 0 else "sdc1"
                            for dst, srcd in ((QThb[pr], qTh_d), (QTlb[pr], qTl_d)):
                                p.inc(sy.dma_start(
                                    out=dst[:, :].rearrange("p (e q) -> p e q",
                                                            e=EC),
                                    in_=srcd[sqn, :, :,
                                             (n + 1) * CH:(n + 2) * CH]),
                                    sem_q, 16)
                            for dst, srcd in ((VThb[pr], vTh_d), (VTlb[pr], vTl_d)):
                                p.inc(sy.dma_start(
                                    out=dst[:, :].rearrange("p (a d) -> p a d",
                                                            a=2),
                                    in_=srcd[sqn, 2 * n + 2:2 * n + 4, :, :]
                                    .rearrange("a p d -> p a d")), sem_q, 16)
                        blk.sync(_)
                        q_toks[(n + 1) % 2] = p.tok(
                            "sdc0" if (n + 1) % 2 == 0 else "sdc1")
                        v_toks[(n + 1) % 2] = q_toks[(n + 1) % 2]

                    prev = {"n": n, "par": par, "AS": AS, "as_tok": as_toks[1]}

                emit_atgroup(prev)
                av_tok, ot_tok = emit_av(prev)
                emit_wo(prev, ot_tok)

                def _(g):
                    p.wait(g, p.tok("sdso"))
                blk.gpsimd(_)

                def _(sy):
                    p.wait(sy, p.tok("sdwB"), p.tok("sdc0"), p.tok("sdc1"))
                blk.sync(_)


def _phase_f(nc, p, es, sqn, bf, x_d, s_d, out_d, gw12_d, gw2_d, wc2_d, pj2_d,
             lng_d, lnb_d, oh_d, xnTh_d, xnTl_d, rn_d):
    with ExitStack() as pes:
        sbX = lambda name, shape, dt=F32: pes.enter_context(
            nc.sbuf_tensor(name, shape, dt))
        gw16 = sbX(f"gw16_{sqn}", [128, 2 * EC * HG], BF16)
        gw2_t = sbX(f"gw2_s{sqn}", [128, (HG // 128) * C16])
        lng_t = sbX(f"lng_s{sqn}", [128, D])
        lnb_t = sbX(f"lnb_s{sqn}", [128, D])
        oh_t = sbX(f"oh_s{sqn}", [C16, C16 * 128])
        wcb = [sbX(f"wcb{i}_s{sqn}", [128, 2 * EC * NN], BF16) for i in range(2)]
        pjb = [sbX(f"pjbuf{i}_s{sqn}", [128, 2 * 32 * 128], BF16) for i in range(2)]
        xt = sbX(f"xt_s{sqn}", [128, 2 * D])
        xnThb = [sbX(f"fxnTh{i}_s{sqn}", [128, EC * CH], BF16) for i in range(2)]
        xnTlb = [sbX(f"fxnTl{i}_s{sqn}", [128, EC * CH], BF16) for i in range(2)]
        sTf = sbX(f"sTf_s{sqn}", [128, EC * CH])
        lgT = sbX(f"lgT_s{sqn}", [C16, CH])
        gvT = sbX(f"gvT_s{sqn}", [C16, CH])
        rnbS = sbX(f"rnbS_s{sqn}", [C16, CH])
        ltok = sbX(f"ltok_s{sqn}", [128, 2 * C16])
        wtok = sbX(f"wtok_s{sqn}", [128, 2 * C16])
        mskT = sbX(f"mskT_s{sqn}", [128, 2 * C16])
        exT = sbX(f"exT_s{sqn}", [128, 2 * C16])
        wT16 = sbX(f"wT16_s{sqn}", [C16, CH])
        colsg = sbX(f"colsg_s{sqn}", [128, 32 * CH])
        colh = sbX(f"colh_s{sqn}", [128, 32 * CH], BF16)
        coll = sbX(f"coll_s{sqn}", [128, 32 * CH], BF16)
        ysT = sbX(f"ysT_s{sqn}", [128, EC * CH])
        hT = colsg[:, 24 * CH:32 * CH]
        ytok = sbX(f"ytok_s{sqn}", [128, 2 * D])
        stok = sbX(f"stok_s{sqn}", [128, 2 * D])
        rn_fc = sbX(f"rn_fc_s{sqn}", [1, CH])

        def gwof(e, m):
            return (gw16[:, e * HG + m * 128:e * HG + (m + 1) * 128],
                    gw16[:, EC * HG + e * HG + m * 128:EC * HG + e * HG + (m + 1) * 128])

        with nc.Block() as blk:
            sq = ScalarQ(p, blk)

            def _(sy):
                p.inc(sy.dma_start(out=gw16[:, :], in_=gw12_d), "sdwA", 16)
                p.inc(sy.dma_start(out=gw2_t[:, :], in_=gw2_d), "sdwA", 16)
                p.inc(sy.dma_start(out=lng_t[:, :], in_=lng_d), "sdwA", 16)
                p.inc(sy.dma_start(out=lnb_t[:, :], in_=lnb_d), "sdwA", 16)
                p.inc(sy.dma_start(out=oh_t[:, :], in_=oh_d), "sdwA", 16)
            blk.sync(_)
            gw_tok = p.tok("sdwA")

            xn_use = [None, None]
            xt_use = [None]
            st_use = [None]
            rn_use = [None]
            cols_prev = [None, None]
            pj_prev = [None]
            hfree = [None]
            hbank_prev = [None, None]
            xnp_toks = [None, None]
            for stile in range(NF_TILES):
                t0 = stile * CH
                par = stile % 2
                sdo_prev = p.tok("sdo")
                xnTh, xnTl = xnThb[par], xnTlb[par]

                if stile == 0:
                    def _(sy):
                        p.inc(sy.dma_start(
                            out=xnThb[0][:, :].rearrange("p (e q) -> p e q", e=EC),
                            in_=xnTh_d[sqn, :, :, 0:CH]), "sdt0", 16)
                        p.inc(sy.dma_start(
                            out=xnTlb[0][:, :].rearrange("p (e q) -> p e q", e=EC),
                            in_=xnTl_d[sqn, :, :, 0:CH]), "sdt1", 16)
                        p.wait(sy, p.tok("sdsr"))
                        p.inc(sy.dma_start(out=rn_fc[0:1, :],
                                           in_=rn_d[sqn, 0:1, 0:CH]), "sdt0", 16)
                    blk.sync(_)
                    xnp_toks[0] = p.tok("sdt0")
                    xnp_toks[1] = p.tok("sdt1")

                def _(sy, t0=t0, par=par, xdep=xt_use[0], sdep=st_use[0]):
                    sx = "sdx0" if par == 0 else "sdx1"
                    ss_ = "sds0" if par == 0 else "sds1"
                    p.wait(sy, sdep)
                    p.inc(sy.dma_start(
                        out=sTf[:, :].rearrange("p (e q) -> p e q", e=EC),
                        in_=s_d[sqn, :, :, t0:t0 + CH]), ss_, 16)
                    p.wait(sy, xdep)
                    p.inc(sy.dma_start(
                        out=xt[:, :].rearrange("p (a d) -> p a d", a=2),
                        in_=x_d[sqn, t0:t0 + CH, :].rearrange(
                            "(a p) d -> p a d", p=128)), sx, 16)
                blk.sync(_)
                x_tok = p.tok("sdx0" if par == 0 else "sdx1")
                xnTh_tok = xnp_toks[0]
                xnTl_tok = xnp_toks[1]
                s_tok_in = p.tok("sds0" if par == 0 else "sds1")

                sa_base = p.v["sa"]

                def _(t, dep0=xnTh_tok, dep1=xnTl_tok, wdep=gw_tok,
                      b0=hbank_prev[0], b1=hbank_prev[1],
                      xnTh=xnTh, xnTl=xnTl):
                    p.wait(t, dep0, dep1, wdep, b0, b1)
                    for m in range(HG // 128):
                        if m >= 2:
                            p.wait(t, ("sa", sa_base + m - 1))
                        bank = bf.pbk[m % 2]
                        for e in range(EC):
                            wh, wl = gwof(e, m)
                            xh = xnTh[:, e * CH:(e + 1) * CH]
                            xl = xnTl[:, e * CH:(e + 1) * CH]
                            for pi, (ws, xs) in enumerate(PASSES):
                                mm = t.matmul(bank[:, 0:CH],
                                              wh if ws == 0 else wl,
                                              xh if xs == 0 else xl,
                                              start=(e == 0 and pi == 0),
                                              stop=(e == EC - 1 and pi == 2))
                        p.inc(mm, "st", 1)
                blk.tensor(_)
                h_base = p.v["st"] - HG // 128

                hT_tok = None
                sq.wait(hfree[0])
                for m in range(HG // 128):
                    sq.wait(("st", h_base + m + 1))
                    hT_tok = sq.op(lambda sc, m=m: sc.activation(
                        hT[:, m * CH:(m + 1) * CH], bf.pbk[m % 2][:, 0:CH],
                        AF.Gelu, bias=bf.gb1_t[:, m:m + 1]))

                def _(t, dep=hT_tok, rdep=xnTh_tok, xnTh=xnTh, xnTl=xnTl):
                    p.wait(t, dep, rdep, p.tok("sv"))
                    for k in range(HG // 128):
                        mm = t.matmul(bf.pbk[2][0:C16, 0:CH],
                                      gw2_t[:, k * C16:(k + 1) * C16],
                                      hT[:, k * CH:(k + 1) * CH],
                                      start=(k == 0), stop=(k == HG // 128 - 1))
                    p.inc(mm, "st", 1)
                    first = True
                    for e in range(EC):
                        ch_ = bf.cnTh[:, e * C16:(e + 1) * C16]
                        cl_ = bf.cnTl[:, e * C16:(e + 1) * C16]
                        xh = xnTh[:, e * CH:(e + 1) * CH]
                        xl = xnTl[:, e * CH:(e + 1) * CH]
                        for ws, xs in PASSES:
                            mm = t.matmul(bf.pbk[3][0:C16, 0:CH],
                                          ch_ if ws == 0 else cl_,
                                          xh if xs == 0 else xl,
                                          start=first,
                                          stop=(e == EC - 1 and (ws, xs) == PASSES[-1]))
                            first = False
                    p.inc(mm, "st", 1)
                    p.wait(t, p.tok("sgc"))
                    p.inc(t.matmul(bf.pbk[3][0:C16, CH:2 * CH], bf.ones1[0:1, :],
                                   rn_fc[0:1, :], start=True, stop=True), "st", 1)
                blk.tensor(_)
                gv_tok = ("st", p.v["st"] - 2)
                rnb_tok = p.tok("st")
                rn_use[0] = rnb_tok

                sq.wait(gv_tok)
                gvtanh_tok = sq.op(lambda sc: sc.activation(
                    gvT[:, :], bf.pbk[2][0:C16, 0:CH],
                    AF.Tanh, bias=bf.gb2_t[:, :], scale=0.5))
                sq.wait(rnb_tok)
                gv_sb_tok = sq.op(lambda sc: sc.copy(rnbS[:, :],
                                                     bf.pbk[3][0:C16, CH:2 * CH]))

                def _(v, dep=gv_sb_tok):
                    p.wait(v, dep)
                    v.tensor_scalar(gvT[:, :], gvT[:, :], 0.5, 0.5,
                                    op0=OP.mult, op1=OP.add)
                    v.drain()
                    v.tensor_mul(lgT[:, :], bf.pbk[3][0:C16, 0:CH], rnbS[:, :])
                    v.drain()
                    p.inc(v.tensor_add(lgT[:, :], lgT[:, :], gvT[:, :]), "sv", 1)
                blk.vector(_)
                lg_tok = p.tok("sv")

                def _(sy, dep0=xnTh_tok, dep1=xnTl_tok):
                    p.wait(sy, dep0, dep1)
                    p.inc(sy.dma_start(out=wcb[0][:, :], in_=wc2_d[0, :, :]),
                          "sdc0", 16)
                blk.sync(_)
                wc_toks = [p.tok("sdc0"), None]

                def _(sy, dep=pj_prev[0]):
                    p.wait(sy, dep)
                    p.inc(sy.dma_start(out=pjb[0][:, :], in_=pj2_d[0, :, :]),
                          "sdp0", 16)
                blk.sync(_)
                pj_toks = [p.tok("sdp0"), None]

                split_toks = []
                rt_state = {}

                def emit_bcast(cp):
                    def _(t, cp=cp):
                        p.wait(t, rt_state["wT_tok"])
                        if cp >= 2:
                            p.wait(t, split_toks[2 * (cp - 2) + 1])
                        p.inc(t.matmul(bf.pbk[2 + (cp % 2)][:, 0:CH],
                                       oh_t[:, cp * 128:(cp + 1) * 128],
                                       wT16[:, :], start=True, stop=True), "st", 1)
                    blk.tensor(_)
                    bc_tok = p.tok("st")
                    for cm in range(2):
                        ti = cp * 2 + cm
                        csl = colsg[:, ti * CH:(ti + 1) * CH]
                        hsl = colh[:, ti * CH:(ti + 1) * CH]
                        lsl = coll[:, ti * CH:(ti + 1) * CH]

                        def _(v, cp=cp, csl=csl, dep1=gelu_toks[ti], dep2=bc_tok):
                            p.wait(v, dep1, dep2)
                            p.inc(v.tensor_mul(csl, csl,
                                               bf.pbk[2 + (cp % 2)][:, 0:CH]),
                                  "sv", 1)
                        blk.vector(_)
                        mul_tok = p.tok("sv")

                        sq.wait(mul_tok)
                        h_tok = sq.op(lambda sc, hsl=hsl, csl=csl: sc.copy(hsl, csl))

                        def _(v, csl=csl, hsl=hsl, lsl=lsl, dep=h_tok):
                            p.wait(v, dep)
                            p.inc(v.tensor_sub(lsl, csl, hsl), "sv", 1)
                        blk.vector(_)
                        split_toks.append(p.tok("sv"))

                cols_mm_toks = []
                gelu_toks = []
                for c in range(C16):
                    def _(sy, c=c):
                        if c + 1 < C16:
                            sem = "sdc0" if (c + 1) % 2 == 0 else "sdc1"
                            if c >= 1:
                                p.wait(sy, cols_mm_toks[c - 1])
                            p.inc(sy.dma_start(out=wcb[(c + 1) % 2][:, :],
                                               in_=wc2_d[c + 1, :, :]), sem, 16)
                    blk.sync(_)
                    if c + 1 < C16:
                        wc_toks[(c + 1) % 2] = p.tok("sdc0" if (c + 1) % 2 == 0 else "sdc1")

                    def _(t, c=c, dep=wc_toks[c % 2], xnTh=xnTh, xnTl=xnTl,
                          b4=(cols_prev[0] if c == 0 else gelu_toks[2 * (c - 1)]),
                          b5=(cols_prev[1] if c == 0 else gelu_toks[2 * (c - 1) + 1])):
                        p.wait(t, dep, b4, b5)
                        wb16 = wcb[c % 2]
                        for cm in range(2):
                            bank = bf.pbk[4 + cm]
                            for e in range(EC):
                                wh = wb16[:, e * NN + cm * 128:e * NN + (cm + 1) * 128]
                                wl = wb16[:, EC * NN + e * NN + cm * 128:
                                          EC * NN + e * NN + (cm + 1) * 128]
                                xh = xnTh[:, e * CH:(e + 1) * CH]
                                xl = xnTl[:, e * CH:(e + 1) * CH]
                                for pi, (ws, xs) in enumerate(PASSES):
                                    mm = t.matmul(bank[:, 0:CH],
                                                  wh if ws == 0 else wl,
                                                  xh if xs == 0 else xl,
                                                  start=(e == 0 and pi == 0),
                                                  stop=(e == EC - 1 and pi == 2))
                            p.inc(mm, "st", 1)
                    blk.tensor(_)
                    cc_base = p.v["st"] - 2
                    cols_mm_toks.append(p.tok("st"))

                    for cm in range(2):
                        sq.wait(("st", cc_base + cm + 1))
                        gelu_toks.append(sq.op(lambda sc, c=c, cm=cm: sc.activation(
                            colsg[:, (c * 2 + cm) * CH:(c * 2 + cm + 1) * CH],
                            bf.pbk[4 + cm][:, 0:CH], AF.Gelu,
                            bias=bf.bc_t[:, c * 2 + cm:c * 2 + cm + 1])))

                    if c == 0:
                        def _(t, dep=lg_tok, gdep=gvtanh_tok):
                            p.wait(t, dep, gdep)
                            for tt in range(2):
                                p.inc(t.transpose(
                                    bf.pbk[2][0:128, tt * C16:(tt + 1) * C16],
                                    lgT[0:C16, tt * 128:(tt + 1) * 128],
                                    bf.id_t[0:C16, 0:C16]), "st", 1)
                        blk.tensor(_)
                        lt_tok = p.tok("st")
                        sq.wait(lt_tok)
                        ltok_tok = None
                        for tt in range(2):
                            ltok_tok = sq.op(lambda sc, tt=tt: sc.copy(
                                ltok[:, tt * C16:(tt + 1) * C16],
                                bf.pbk[2][0:128, tt * C16:(tt + 1) * C16]))
                        rt_state["ltok_tok"] = ltok_tok
                    elif c == 1:
                        sq.flush()
                        w_toks = []
                        for tt in range(2):
                            sl = ltok[:, tt * C16:(tt + 1) * C16]
                            wl = wtok[:, tt * C16:(tt + 1) * C16]

                            def _(v, sl=sl, tt=tt, dep=rt_state["ltok_tok"]):
                                p.wait(v, dep)
                                v.reduce_max(bf.mxv[:, :], sl, axis=AX)
                                v.reduce_max(bf.nmx[:, :], sl, axis=AX, negate=True)
                                v.drain()
                                p.inc(v.tensor_scalar(mskT[:, tt * C16:(tt + 1) * C16],
                                                      sl, bf.mxv[:, :], None,
                                                      op0=OP.is_ge), "sv", 1)
                            blk.vector(_)
                            msk_tok = p.tok("sv")

                            sq.wait(msk_tok)
                            ex_tok = sq.op(lambda sc, sl=sl, tt=tt: sc.activation(
                                exT[:, tt * C16:(tt + 1) * C16], sl, AF.Exp,
                                bias=bf.nmx[:, :], accum_out=bf.sme[:, :]))

                            def _(v, wl=wl, tt=tt, dep=ex_tok):
                                p.wait(v, dep)
                                v.reciprocal(bf.rcp[:, :], bf.sme[:, :])
                                v.drain()
                                p.inc(v.scalar_tensor_tensor(
                                    wl, exT[:, tt * C16:(tt + 1) * C16], bf.rcp[:, :],
                                    mskT[:, tt * C16:(tt + 1) * C16],
                                    op0=OP.mult, op1=OP.mult), "sv", 1)
                            blk.vector(_)
                            w_toks.append(p.tok("sv"))
                        rt_state["w_tok"] = w_toks[-1]
                        sq.flush()
                    elif c == 2:
                        def _(t, dep=rt_state["w_tok"], ldep=rt_state["ltok_tok"]):
                            p.wait(t, dep, ldep)
                            for tt in range(2):
                                p.inc(t.transpose(
                                    bf.pbk[2][0:C16, tt * 128:(tt + 1) * 128],
                                    wtok[:, tt * C16:(tt + 1) * C16],
                                    bf.id_t[:, :]), "st", 1)
                        blk.tensor(_)
                        wt_tok = p.tok("st")
                        sq.wait(wt_tok)
                        wT_tok = None
                        for tt in range(2):
                            wT_tok = sq.op(lambda sc, tt=tt: sc.copy(
                                wT16[:, tt * 128:(tt + 1) * 128],
                                bf.pbk[2][0:C16, tt * 128:(tt + 1) * 128]))
                        rt_state["wT_tok"] = wT_tok
                    elif c == 3:
                        emit_bcast(0)
                        emit_bcast(1)
                    else:
                        emit_bcast(c - 2)
                        if c == C16 - 1:
                            emit_bcast(14)
                            emit_bcast(15)
                cols_tok = split_toks[-1]
                xn_use[par] = cols_tok
                hfree[0] = cols_tok
                cols_prev[0] = gelu_toks[-2]
                cols_prev[1] = gelu_toks[-1]
                sq.flush()

                if stile + 1 < NF_TILES:
                    def _(sy, t1=(stile + 1) * CH, npar=(stile + 1) % 2,
                          ndep=xn_use[(stile + 1) % 2], rdep=rn_use[0]):
                        p.wait(sy, ndep)
                        p.inc(sy.dma_start(
                            out=xnThb[npar][:, :].rearrange("p (e q) -> p e q",
                                                            e=EC),
                            in_=xnTh_d[sqn, :, :, t1:t1 + CH]), "sdt0", 16)
                        p.inc(sy.dma_start(
                            out=xnTlb[npar][:, :].rearrange("p (e q) -> p e q",
                                                            e=EC),
                            in_=xnTl_d[sqn, :, :, t1:t1 + CH]), "sdt1", 16)
                        p.wait(sy, rdep)
                        p.inc(sy.dma_start(out=rn_fc[0:1, :],
                                           in_=rn_d[sqn, 0:1, t1:t1 + CH]),
                              "sdt0", 16)
                    blk.sync(_)
                    xnp_toks[0] = p.tok("sdt0")
                    xnp_toks[1] = p.tok("sdt1")

                ye_toks = []
                for m in range(EC):
                    def _(sy, m=m):
                        if m + 1 < EC:
                            sem = "sdp0" if (m + 1) % 2 == 0 else "sdp1"
                            if m >= 1:
                                p.wait(sy, ye_toks[m - 1])
                            p.inc(sy.dma_start(out=pjb[(m + 1) % 2][:, :],
                                               in_=pj2_d[m + 1, :, :]), sem, 16)
                    blk.sync(_)
                    if m + 1 < EC:
                        pj_toks[(m + 1) % 2] = p.tok("sdp0" if (m + 1) % 2 == 0 else "sdp1")

                    def _(t, m=m, dep=pj_toks[m % 2], cdep=cols_tok,
                          hdep=hT_tok):
                        if m == 0:
                            p.wait(t, dep, hdep)
                        else:
                            p.wait(t, dep, cdep, p.tok("sa"), p.tok("sv"))
                        bank = bf.pbk[m % 4]
                        pb16 = pjb[m % 2]
                        for k in range(32):
                            if m == 0:
                                p.wait(t, split_toks[k])
                            wh = pb16[:, k * 128:(k + 1) * 128]
                            wlo = pb16[:, 32 * 128 + k * 128:32 * 128 + (k + 1) * 128]
                            xh = colh[:, k * CH:(k + 1) * CH]
                            xl = coll[:, k * CH:(k + 1) * CH]
                            for pi, (ws, xs) in enumerate(PASSES):
                                mm = t.matmul(bank[:, 0:CH],
                                              wh if ws == 0 else wlo,
                                              xh if xs == 0 else xl,
                                              start=(k == 0 and pi == 0),
                                              stop=(k == 31 and pi == 2))
                        p.inc(mm, "st", 1)
                    blk.tensor(_)
                    y_tok = p.tok("st")

                    if m == 0:
                        sq.wait(xt_use[0])
                    sq.wait(y_tok)
                    ye_toks.append(sq.op(lambda sc, m=m: sc.activation(
                        ysT[:, m * CH:(m + 1) * CH], bf.pbk[m % 4][:, 0:CH],
                        AF.Identity, bias=bf.pjb_t[:, m:m + 1])))
                if True:
                    pj_prev[0] = ye_toks[-2]
                    hbank_prev[0] = ye_toks[4]
                    hbank_prev[1] = ye_toks[5]

                def _(v, dep=ye_toks[-1], xnTh=xnTh, xnTl=xnTl):
                    p.wait(v, dep)
                    for e in range(EC):
                        v.tensor_add(ysT[:, e * CH:(e + 1) * CH],
                                     ysT[:, e * CH:(e + 1) * CH],
                                     xnTh[:, e * CH:(e + 1) * CH])
                        v.drain()
                        ins = v.tensor_add(ysT[:, e * CH:(e + 1) * CH],
                                           ysT[:, e * CH:(e + 1) * CH],
                                           xnTl[:, e * CH:(e + 1) * CH])
                        if e < EC - 1:
                            v.drain()
                    p.inc(ins, "sv", 1)
                blk.vector(_)
                yres_tok = p.tok("sv")

                sa_base4 = p.v["sa"]

                def _(t, dep=yres_tok):
                    p.wait(t, dep, p.tok("sa"))
                    for e in range(EC):
                        if e >= 2:
                            p.wait(t, ("sa", sa_base4 + 2 * (e - 1)))
                        for tt in range(2):
                            p.inc(t.transpose(bf.pbk[6 + (e % 2)][:, tt * 128:tt * 128 + 128],
                                              ysT[:, e * CH + tt * 128:e * CH + tt * 128 + 128],
                                              bf.id_t[:, :]), "st", 1)
                blk.tensor(_)
                ytr_base = p.v["st"] - 16

                ytok_tok = None
                sq.wait(sdo_prev)
                for e in range(EC):
                    sq.wait(("st", ytr_base + 2 * (e + 1)))
                    for tt in range(2):
                        ytok_tok = sq.op(lambda sc, e=e, tt=tt: sc.copy(
                            ytok[:, tt * D + e * 128:tt * D + (e + 1) * 128],
                            bf.pbk[6 + (e % 2)][:, tt * 128:tt * 128 + 128]))

                sa_base5 = p.v["sa"]

                def _(t, dep=ytok_tok, sdep=s_tok_in):
                    p.wait(t, dep, sdep)
                    for e in range(EC):
                        if e >= 2:
                            p.wait(t, ("sa", sa_base5 + 2 * (e - 1)))
                        for tt in range(2):
                            p.inc(t.transpose(bf.pbk[6 + (e % 2)][:, tt * 128:tt * 128 + 128],
                                              sTf[:, e * CH + tt * 128:e * CH + tt * 128 + 128],
                                              bf.id_t[:, :]), "st", 1)
                blk.tensor(_)
                str_base = p.v["st"] - 16

                stok_tok = None
                for e in range(EC):
                    sq.wait(("st", str_base + 2 * (e + 1)))
                    for tt in range(2):
                        stok_tok = sq.op(lambda sc, e=e, tt=tt: sc.copy(
                            stok[:, tt * D + e * 128:tt * D + (e + 1) * 128],
                            bf.pbk[6 + (e % 2)][:, tt * 128:tt * 128 + 128]))
                st_use[0] = stok_tok

                lnts = []
                for tt in range(2):
                    lnts.append(_ln_tile(p, blk, sq, ytok[:, tt * D:(tt + 1) * D],
                                         ytok[:, tt * D:(tt + 1) * D], bf,
                                         pre_deps=(ytok_tok, p.tok("sa")),
                                         gamma=lng_t, beta=lnb_t))
                for tt in range(2):
                    gsl = ytok[:, tt * D:(tt + 1) * D]
                    sq.wait(lnts[tt])
                    g_tok = sq.op(lambda sc, tt=tt, gsl=gsl: sc.activation(
                        ysT[:, tt * D:(tt + 1) * D], gsl, AF.Tanh, scale=0.5))

                    def _(v, tt=tt, gsl=gsl, dep=g_tok, sdep=stok_tok, xdep=x_tok):
                        p.wait(v, dep, sdep, xdep)
                        v.tensor_scalar(ysT[:, tt * D:(tt + 1) * D],
                                        ysT[:, tt * D:(tt + 1) * D], 0.5, 0.5,
                                        op0=OP.mult, op1=OP.add)
                        v.drain()
                        v.tensor_mul(gsl, gsl, ysT[:, tt * D:(tt + 1) * D])
                        v.drain()
                        v.tensor_mul(gsl, gsl, stok[:, tt * D:(tt + 1) * D])
                        v.drain()
                        p.inc(v.tensor_add(gsl, gsl, xt[:, tt * D:(tt + 1) * D]),
                              "sv", 1)
                    blk.vector(_)
                out_tok = p.tok("sv")
                xt_use[0] = out_tok
                sq.flush()

                def _(g, t0=t0, dep=out_tok):
                    p.wait(g, dep)
                    p.inc(g.dma_start(out=out_d[sqn, t0:t0 + CH, :].rearrange(
                        "(a p) d -> p a d", p=128),
                        in_=ytok[:, :].rearrange("p (a d) -> p a d", a=2)), "sdo", 16)
                blk.gpsimd(_)

            def _(sy):
                p.wait(sy, p.tok("sdwA"), p.tok("sdc0"), p.tok("sdc1"),
                       p.tok("sdp0"), p.tok("sdp1"), p.tok("sdsi"),
                       p.tok("sdt0"), p.tok("sdt1"), p.tok("sds0"), p.tok("sds1"),
                       p.tok("sdx0"), p.tok("sdx1"))
            blk.sync(_)

            def _(g):
                p.wait(g, p.tok("sdo"))
            blk.gpsimd(_)


def _host_layouts(inputs):
    f = np.ascontiguousarray
    L = {}
    for nm, key in (("wq", "L0_attn_Wq"), ("wk", "L0_attn_Wk"),
                    ("wv", "L0_attn_Wv"), ("wo", "L0_attn_Wo")):
        L[nm] = f(inputs[key].reshape(EC, 128, D).transpose(1, 0, 2).reshape(128, EC * D))
    L["bq"] = f(inputs["L0_attn_bq"].reshape(EC, 128).T)
    L["bk"] = f(inputs["L0_attn_bk"].reshape(EC, 128).T)
    L["bo"] = f(inputs["L0_attn_bo"].reshape(EC, 128).T)
    L["pjb"] = f(inputs["L0_ffn_proj_b"].reshape(EC, 128).T)
    L["bvb"] = f(np.tile(inputs["L0_attn_bv"][None, :], (128, 1)))
    L["lng"] = f(np.tile(inputs["L0_ffn_ln_g"][None, :], (128, 1)))
    L["lnb"] = f(np.tile(inputs["L0_ffn_ln_b"][None, :], (128, 1)))
    L["ce"] = f(inputs["L0_ffn_col_emb"])
    L["gw1"] = f(inputs["L0_ffn_gate_w1"].reshape(EC, 128, HG).transpose(1, 0, 2).reshape(128, EC * HG))
    L["gb1"] = f(inputs["L0_ffn_gate_b1"].reshape(HG // 128, 128).T)
    L["gw2"] = f(inputs["L0_ffn_gate_w2"].reshape(HG // 128, 128, C16).transpose(1, 0, 2).reshape(128, (HG // 128) * C16))
    L["gb2"] = f(inputs["L0_ffn_gate_b2"].reshape(C16, 1))
    L["wc"] = f(inputs["L0_ffn_W_cols"].reshape(C16, EC, 128, NN).transpose(0, 2, 1, 3).reshape(C16, 128, EC * NN))
    L["bc"] = f(inputs["L0_ffn_b_cols"].reshape(C16, 2, 128).transpose(2, 0, 1).reshape(128, C16 * 2))
    L["pj"] = f(inputs["L0_ffn_proj_w"].reshape(32, 128, EC, 128).transpose(2, 1, 0, 3).reshape(EC, 128, 32 * 128))
    L["ident"] = np.eye(128, dtype=np.float32)
    oh = np.zeros((C16, C16 * 128), dtype=np.float32)
    for c in range(C16):
        oh[c, c * 128:(c + 1) * 128] = 1.0
    L["oh"] = oh
    return L


def kernel(**inputs):
    nc = build_core_program()
    L = _host_layouts(inputs)
    x = np.ascontiguousarray(np.asarray(inputs["L0_attn_x"], dtype=np.float32))
    in_maps = []
    for ci in range(NC_):
        m = dict(L)
        m["x"] = np.ascontiguousarray(x[ci * BPC:(ci + 1) * BPC])
        in_maps.append(m)
    res = run_bass_kernel_spmd(nc, in_maps, core_ids=list(range(NC_)))
    return np.concatenate([r["out"] for r in res.results], axis=0)


# version 17
# speedup vs baseline: 1.0608x; 1.0608x over previous
import numpy as np
from contextlib import ExitStack

import concourse.bass as bass
import concourse.mybir as mybir
from concourse.alu_op_type import AluOpType
from concourse.bass_utils import run_bass_kernel_spmd

F32 = mybir.dt.float32
BF16 = mybir.dt.bfloat16
AX = mybir.AxisListType.X
AF = mybir.ActivationFunctionType
OP = AluOpType

B, S, D = 16, 2048, 1024
NC_ = 8
BPC = B // NC_
CH = 256
NCH = S // CH
EC = D // 128
C16 = 16
NN = 256
HG = 512
LN_EPS = 1e-5

PHASES = ("a", "f")
DEBUG_S = False
NSEQ = BPC
NF_TILES = NCH
NA_CHUNKS = NCH

PASSES = ((0, 0), (0, 1), (1, 0))


class Prog:
    def __init__(self, nc):
        self.nc = nc
        self.h = {}
        self.v = {}

    def sem(self, es, *names):
        for n in names:
            self.h[n] = es.enter_context(self.nc.semaphore(n))
            self.v[n] = 0

    def inc(self, instr, name, n=1):
        instr.then_inc(self.h[name], n)
        self.v[name] += n
        return (name, self.v[name])

    def tok(self, name):
        return (name, self.v[name])

    def wait(self, eng, *deps):
        for d in deps:
            if d is None:
                continue
            name, val = d
            if val > 0:
                eng.wait_ge(self.h[name], val)


class Buffers:
    pass


class ScalarQ:

    def __init__(self, p, blk):
        self.p = p
        self.blk = blk
        self.ops = []

    def wait(self, *deps):
        deps = tuple(d for d in deps if d is not None)
        if deps:
            self.ops.append(("w", deps))

    def op(self, make, sem="sa", n=1):
        tok = None
        if sem is not None:
            self.p.v[sem] += n
            tok = (sem, self.p.v[sem])
        self.ops.append(("i", make, sem, n))
        return tok

    def flush(self):
        if not self.ops:
            return
        ops, self.ops = self.ops, []
        p = self.p

        def body(sc):
            for it in ops:
                if it[0] == "w":
                    p.wait(sc, *it[1])
                else:
                    _, make, sem, n = it
                    ins = make(sc)
                    if sem is not None:
                        ins.then_inc(p.h[sem], n)
        self.blk.scalar(body)


def _ln_tile(p, blk, sq, x_in, xn_out, bf, pre_deps=(), gamma=None, beta=None):
    def _(v):
        p.wait(v, *pre_deps)
        v.bn_stats(bf.stats[:, 0:6], x_in[:, 0:512])
        v.bn_stats(bf.stats[:, 6:12], x_in[:, 512:1024])
        v.drain()
        v.bn_aggr(bf.mv[:, :], bf.stats[:, :])
        v.drain()
        p.inc(v.tensor_scalar_add(bf.tmp1[:, :], bf.mv[:, 1:2], LN_EPS), "sv", 1)
    blk.vector(_)
    var_done = p.tok("sv")

    sq.wait(var_done)
    sq.op(lambda sc: sc.activation(bf.rstd[:, :], bf.tmp1[:, :], AF.Ln), sem=None)
    sq.op(lambda sc: sc.drain(), sem=None)
    rstd_done = sq.op(lambda sc: sc.activation(bf.rstd[:, :], bf.rstd[:, :],
                                               AF.Exp, scale=-0.5))

    def _(v):
        p.wait(v, rstd_done)
        ins = v.tensor_scalar(xn_out, x_in, bf.mv[:, 0:1], bf.rstd[:, :],
                              op0=OP.subtract, op1=OP.mult)
        if gamma is not None:
            v.drain()
            v.tensor_mul(xn_out, xn_out, gamma[:, :])
            v.drain()
            ins = v.tensor_add(xn_out, xn_out, beta[:, :])
        p.inc(ins, "sv", 1)
    blk.vector(_)
    return p.tok("sv")


def _evac_hilo(p, blk, sq, bf, items):
    hi_toks = []
    for stok, ps, hi, lo, bias in items:
        sq.wait(stok)
        if bias is None:
            hi_toks.append(sq.op(lambda sc, ps=ps, hi=hi: sc.copy(hi, ps)))
        else:
            hi_toks.append(sq.op(lambda sc, ps=ps, hi=hi, bias=bias: sc.activation(
                hi, ps, AF.Identity, bias=bias)))

    def _(v):
        for i, (stok, ps, hi, lo, bias) in enumerate(items):
            p.wait(v, hi_toks[i])
            if bias is None:
                ins = v.tensor_sub(lo, ps, hi)
            else:
                ins = v.scalar_tensor_tensor(lo, ps, bias, hi,
                                             op0=OP.add, op1=OP.subtract)
            p.inc(ins, "sv", 1)
            v.drain()
    blk.vector(_)
    return p.tok("sv")


def build_core_program(phases=PHASES, debug_s=DEBUG_S):
    nc = bass.Bass("TRN2", target_bir_lowering=False)
    p = Prog(nc)

    x_d = nc.dram_tensor("x", [BPC, S, D], F32, kind="ExternalInput").ap()
    wq_d = nc.dram_tensor("wq", [128, EC * D], F32, kind="ExternalInput").ap()
    wk_d = nc.dram_tensor("wk", [128, EC * D], F32, kind="ExternalInput").ap()
    wv_d = nc.dram_tensor("wv", [128, EC * D], F32, kind="ExternalInput").ap()
    wo_d = nc.dram_tensor("wo", [128, EC * D], F32, kind="ExternalInput").ap()
    bq_d = nc.dram_tensor("bq", [128, EC], F32, kind="ExternalInput").ap()
    bk_d = nc.dram_tensor("bk", [128, EC], F32, kind="ExternalInput").ap()
    bo_d = nc.dram_tensor("bo", [128, EC], F32, kind="ExternalInput").ap()
    pb_d = nc.dram_tensor("pjb", [128, EC], F32, kind="ExternalInput").ap()
    bvb_d = nc.dram_tensor("bvb", [128, D], F32, kind="ExternalInput").ap()
    lng_d = nc.dram_tensor("lng", [128, D], F32, kind="ExternalInput").ap()
    lnb_d = nc.dram_tensor("lnb", [128, D], F32, kind="ExternalInput").ap()
    ce_d = nc.dram_tensor("ce", [C16, D], F32, kind="ExternalInput").ap()
    gw1_d = nc.dram_tensor("gw1", [128, EC * HG], F32, kind="ExternalInput").ap()
    gb1_d = nc.dram_tensor("gb1", [128, HG // 128], F32, kind="ExternalInput").ap()
    gw2_d = nc.dram_tensor("gw2", [128, (HG // 128) * C16], F32, kind="ExternalInput").ap()
    gb2_d = nc.dram_tensor("gb2", [C16, 1], F32, kind="ExternalInput").ap()
    wc_d = nc.dram_tensor("wc", [C16, 128, EC * NN], F32, kind="ExternalInput").ap()
    bc_d = nc.dram_tensor("bc", [128, C16 * 2], F32, kind="ExternalInput").ap()
    pj_d = nc.dram_tensor("pj", [EC, 128, 32 * 128], F32, kind="ExternalInput").ap()
    id_d = nc.dram_tensor("ident", [128, 128], F32, kind="ExternalInput").ap()
    oh_d = nc.dram_tensor("oh", [C16, C16 * 128], F32, kind="ExternalInput").ap()
    out_d = nc.dram_tensor("out", [BPC, S, D], F32, kind="ExternalOutput").ap()
    skind = {"kind": "ExternalOutput"} if debug_s else {}
    s_d = nc.dram_tensor("s_scr", [BPC, 128, EC, S], F32, **skind).ap()
    xnTh_d = nc.dram_tensor("xnTh_scr", [BPC, 128, EC, S], BF16).ap()
    xnTl_d = nc.dram_tensor("xnTl_scr", [BPC, 128, EC, S], BF16).ap()
    qTh_d = nc.dram_tensor("qTh_scr", [BPC, 128, EC, S], BF16).ap()
    qTl_d = nc.dram_tensor("qTl_scr", [BPC, 128, EC, S], BF16).ap()
    vTh_d = nc.dram_tensor("vTh_scr", [BPC, S // 128, 128, D], BF16).ap()
    vTl_d = nc.dram_tensor("vTl_scr", [BPC, S // 128, 128, D], BF16).ap()
    rn_d = nc.dram_tensor("rn_scr", [BPC, 1, S], F32).ap()
    wk2_d = nc.dram_tensor("wk2_scr", [128, 2 * EC * D], BF16).ap()
    wq2_d = nc.dram_tensor("wq2_scr", [128, 2 * EC * D], BF16).ap()
    wv2_d = nc.dram_tensor("wv2_scr", [128, 2 * EC * D], BF16).ap()
    wo2_d = nc.dram_tensor("wo2_scr", [128, 2 * EC * D], BF16).ap()
    gw12_d = nc.dram_tensor("gw12_scr", [128, 2 * EC * HG], BF16).ap()
    wc2_d = nc.dram_tensor("wc2_scr", [C16, 128, 2 * EC * NN], BF16).ap()
    pj2_d = nc.dram_tensor("pj2_scr", [EC, 128, 2 * 32 * 128], BF16).ap()

    es = ExitStack()
    with es:
        bf = Buffers()
        sb = lambda name, shape, dt=F32: es.enter_context(nc.sbuf_tensor(name, shape, dt))
        bf.bq_t = sb("bq_t", [128, EC])
        bf.bk_t = sb("bk_t", [128, EC])
        bf.bo_t = sb("bo_t", [128, EC])
        bf.pjb_t = sb("pjb_t", [128, EC])
        bf.gb1_t = sb("gb1_t", [128, HG // 128])
        bf.gb2_t = sb("gb2_t", [C16, 1])
        bf.bc_t = sb("bc_t", [128, C16 * 2])
        bf.id_t = sb("id_t", [128, 128])
        bf.ones1 = sb("ones1", [1, C16])
        bf.cnTh = sb("cnTh", [128, EC * C16], BF16)
        bf.cnTl = sb("cnTl", [128, EC * C16], BF16)
        bf.stats = sb("stats", [128, 12])
        bf.mv = sb("mv", [128, 2])
        bf.tmp1 = sb("tmp1", [128, 1])
        bf.rstd = sb("rstd", [128, 1])
        bf.nmx = sb("nmx", [128, 1])
        bf.mxv = sb("mxv", [128, 1])
        bf.sme = sb("sme", [128, 1])
        bf.rcp = sb("rcp", [128, 1])
        bf.rcp2 = sb("rcp2", [128, 1])
        bf.ss_t = sb("ss_t", [128, 1])
        bf.pbk = [es.enter_context(nc.psum_tensor(f"pb{i}", [128, 512], F32))
                  for i in range(8)]

        p.sem(es, "sdk", "sdx0", "sdx1", "sdwA", "sdwB", "sdc0", "sdc1",
              "sdp0", "sdp1", "sdsi", "sdt0", "sdt1", "sds0", "sds1",
              "st", "sa", "sv", "sgc", "sdso", "sdo", "sdsq", "sdsv", "sdsr")

        with nc.Block() as blk:
            sq = ScalarQ(p, blk)
            pro_es = ExitStack()
            psb = lambda name, shape, dt=F32: pro_es.enter_context(
                nc.sbuf_tensor(name, shape, dt))
            ce_t = psb("ce_t", [C16, D])
            cesc = psb("cesc", [C16, D])
            e16a = psb("e16a", [C16, 1])
            e16b = psb("e16b", [C16, 1])
            cnT32 = psb("cnT32", [128, EC * C16])
            w32 = [psb(f"prep32_{i}", [128, EC * D]) for i in range(2)]
            w16 = [psb(f"prep16_{i}", [128, 2 * EC * D], BF16) for i in range(2)]

            def _(sy):
                for dst, src in ((bf.bq_t, bq_d), (bf.bk_t, bk_d), (bf.bo_t, bo_d),
                                 (bf.pjb_t, pb_d), (bf.gb1_t, gb1_d),
                                 (bf.gb2_t, gb2_d), (bf.bc_t, bc_d),
                                 (bf.id_t, id_d), (ce_t, ce_d)):
                    p.inc(sy.dma_start(out=dst[:, :], in_=src), "sdk", 16)
            blk.sync(_)
            kload = p.tok("sdk")

            def _(g):
                p.inc(g.memset(bf.ones1[:, :], 1.0), "sgc", 1)
            blk.gpsimd(_)

            units = [("wk", wk_d, wk2_d, EC * D, None),
                     ("wq", wq_d, wq2_d, EC * D, 1.0 / 32.0),
                     ("wv", wv_d, wv2_d, EC * D, None),
                     ("wo", wo_d, wo2_d, EC * D, None),
                     ("gw1", gw1_d, gw12_d, EC * HG, None)]

            done_toks = [None, None]
            in_sems = ("sdwB", "sdsi")
            out_sems = ("sdso", "sdo")
            for ui, (nm, src, dst, width, scale) in enumerate(units):
                slot = ui % 2

                def _(sy, src=src, width=width, slot=slot, dep=done_toks[slot]):
                    p.wait(sy, dep)
                    p.inc(sy.dma_start(out=w32[slot][:, 0:width], in_=src),
                          in_sems[slot], 16)
                blk.sync(_)
                in_tok = p.tok(in_sems[slot])

                if scale is not None:
                    def _(v, slot=slot, width=width, dep=in_tok, scale=scale):
                        p.wait(v, dep)
                        p.inc(v.tensor_scalar_mul(w32[slot][:, 0:width],
                                                  w32[slot][:, 0:width], scale),
                              "sv", 1)
                    blk.vector(_)
                    in_tok = p.tok("sv")

                sq.wait(in_tok)
                hi_tok = sq.op(lambda sc, slot=slot, width=width: sc.copy(
                    w16[slot][:, 0:width], w32[slot][:, 0:width]))

                def _(v, slot=slot, width=width, dep=hi_tok):
                    p.wait(v, dep)
                    p.inc(v.tensor_sub(w16[slot][:, width:2 * width],
                                       w32[slot][:, 0:width],
                                       w16[slot][:, 0:width]), "sv", 1)
                blk.vector(_)
                lo_tok = p.tok("sv")

                def _(g, dst=dst, width=width, slot=slot, dep=lo_tok):
                    p.wait(g, dep)
                    p.inc(g.dma_start(out=dst, in_=w16[slot][:, 0:2 * width]),
                          out_sems[slot], 16)
                blk.gpsimd(_)
                done_toks[slot] = p.tok(out_sems[slot])

            sq.wait(kload)
            sq_done = sq.op(lambda sc: sc.activation(cesc[:, :], ce_t[:, :],
                                                     AF.Square, accum_out=e16a[:, :]))
            sq.flush()

            def _(v):
                p.wait(v, sq_done)
                p.inc(v.tensor_scalar_max(e16a[:, :], e16a[:, :], 1e-24), "sv", 1)
            blk.vector(_)
            mx_done = p.tok("sv")

            sq.wait(mx_done)
            sq.op(lambda sc: sc.activation(e16b[:, :], e16a[:, :], AF.Ln), sem=None)
            sq.op(lambda sc: sc.drain(), sem=None)
            nrm_done = sq.op(lambda sc: sc.activation(e16a[:, :], e16b[:, :],
                                                      AF.Exp, scale=-0.5))
            sq.flush()

            def _(v):
                p.wait(v, nrm_done)
                v.tensor_scalar(cesc[:, :], ce_t[:, :], e16a[:, :], None, op0=OP.mult)
                v.drain()
                v.tensor_scalar_mul(bf.bq_t[:, :], bf.bq_t[:, :], 1.0 / 32.0)
                v.drain()
                p.inc(v.tensor_scalar_mul(bf.gb2_t[:, :], bf.gb2_t[:, :], 0.5), "sv", 1)
            blk.vector(_)
            cn_done = p.tok("sv")

            sa_base = p.v["sa"]

            def _(t):
                p.wait(t, cn_done)
                for e in range(EC):
                    if e >= 2:
                        p.wait(t, ("sa", sa_base + e - 1))
                    p.inc(t.transpose(bf.pbk[e % 2][0:128, 0:C16],
                                      cesc[:, e * 128:(e + 1) * 128],
                                      bf.id_t[0:C16, 0:C16]), "st", 1)
            blk.tensor(_)
            st_base = p.v["st"] - EC

            for e in range(EC):
                sq.wait(("st", st_base + e + 1))
                sq.op(lambda sc, e=e: sc.copy(cnT32[:, e * C16:(e + 1) * C16],
                                              bf.pbk[e % 2][0:128, 0:C16]))
            sq.op(lambda sc: sc.drain(), sem=None)
            cnh_tok = sq.op(lambda sc: sc.copy(bf.cnTh[:, :], cnT32[:, :]))
            sq.flush()

            def _(v):
                p.wait(v, cnh_tok)
                p.inc(v.tensor_sub(bf.cnTl[:, :], cnT32[:, :], bf.cnTh[:, :]), "sv", 1)
            blk.vector(_)

            def _(sy):
                p.wait(sy, p.tok("sdwB"), p.tok("sdsi"), p.tok("sdk"))
            blk.sync(_)

            def _(g):
                p.wait(g, p.tok("sdso"), p.tok("sdo"))
            blk.gpsimd(_)
        pro_es.close()

        for sqn in range(NSEQ):
            if "a" in phases:
                _phase_a(nc, p, es, sqn, bf, x_d, s_d, bvb_d,
                         wk2_d, wq2_d, wv2_d, wo2_d,
                         xnTh_d, xnTl_d, qTh_d, qTl_d, vTh_d, vTl_d, rn_d,
                         wc_d, wc2_d, pj_d, pj2_d)
            if "f" in phases:
                _phase_f(nc, p, es, sqn, bf, x_d, s_d, out_d, gw12_d, gw2_d,
                         wc2_d, pj2_d, lng_d, lnb_d, oh_d,
                         xnTh_d, xnTl_d, rn_d)

        with nc.Block() as blk:
            def _(sy):
                for s in ("sdk", "sdx0", "sdx1", "sdwA", "sdwB", "sdc0", "sdc1",
                          "sdp0", "sdp1", "sdsi", "sdt0", "sdt1", "sds0", "sds1"):
                    p.wait(sy, p.tok(s))
            blk.sync(_)

            def _(g):
                p.wait(g, p.tok("sdso"), p.tok("sdo"))
            blk.gpsimd(_)
    return nc


def _phase_a(nc, p, es, sqn, bf, x_d, s_d, bvb_d, wk2_d, wq2_d, wv2_d, wo2_d,
             xnTh_d, xnTl_d, qTh_d, qTl_d, vTh_d, vTl_d, rn_d,
             wc_d, wc2_d, pj_d, pj2_d):
    with ExitStack() as pes:
        sbX = lambda name, shape, dt=F32: pes.enter_context(
            nc.sbuf_tensor(name, shape, dt))
        KTh = [sbX(f"KTh{e}_s{sqn}", [128, S], BF16) for e in range(EC)]
        KTl = [sbX(f"KTl{e}_s{sqn}", [128, S], BF16) for e in range(EC)]

        with ExitStack() as a1es:
            sb1 = lambda name, shape, dt=F32: a1es.enter_context(
                nc.sbuf_tensor(name, shape, dt))
            wk16 = sb1(f"wk16_{sqn}", [128, 2 * EC * D], BF16)
            wq16 = sb1(f"wq16_{sqn}", [128, 2 * EC * D], BF16)
            wv16 = sb1(f"wv16_{sqn}", [128, 2 * EC * D], BF16)
            bvb_t = sb1(f"bvb_s{sqn}", [128, D])
            xa = sb1(f"xa_s{sqn}", [128, 2 * D])
            xnTh = sb1(f"xnTh_s{sqn}", [128, EC * CH], BF16)
            xnTl = sb1(f"xnTl_s{sqn}", [128, EC * CH], BF16)
            QTh = sb1(f"QTh_s{sqn}", [128, EC * CH], BF16)
            QTl = sb1(f"QTl_s{sqn}", [128, EC * CH], BF16)
            VTh = sb1(f"VTh_s{sqn}", [128, 2 * D], BF16)
            VTl = sb1(f"VTl_s{sqn}", [128, 2 * D], BF16)
            V32 = sb1(f"V32_s{sqn}", [128, 2 * D])
            rn_row = sb1(f"rn_row_s{sqn}", [1, CH])
            rn_park = sb1(f"rn_park_s{sqn}", [128, 4])

            def wof(w16buf, e, m, base):
                return (w16buf[:, e * D + m * 128:e * D + (m + 1) * 128],
                        w16buf[:, base + e * D + m * 128:base + e * D + (m + 1) * 128])

            with nc.Block() as blk:
                sq = ScalarQ(p, blk)

                def _(sy):
                    p.inc(sy.dma_start(out=wk16[:, :], in_=wk2_d), "sdwA", 16)
                    p.inc(sy.dma_start(out=wq16[:, :], in_=wq2_d), "sdwA", 16)
                    p.inc(sy.dma_start(out=wv16[:, :], in_=wv2_d), "sdwA", 16)
                    p.inc(sy.dma_start(out=bvb_t[:, :], in_=bvb_d), "sdwA", 16)
                blk.sync(_)
                w_tok = p.tok("sdwA")

                lnst = {}

                def emit_ln(c, xfree):
                    t0c = c * CH
                    sx = "sdx0" if c % 2 == 0 else "sdx1"
                    rb = (c % 2) * 2

                    def _(sy, t0=t0c, sx=sx, fdep=xfree):
                        p.wait(sy, fdep)
                        p.inc(sy.dma_start(
                            out=xa[:, :].rearrange("p (a d) -> p a d", a=2),
                            in_=x_d[sqn, t0:t0 + CH, :].rearrange(
                                "(a p) d -> p a d", p=128)), sx, 16)
                    blk.sync(_)
                    x_tok = p.tok(sx)
                    lnst["x_tok"] = x_tok

                    ln_toks = []
                    rn_parked = []
                    for tt in range(2):
                        ln_toks.append(_ln_tile(p, blk, sq,
                                                xa[:, tt * D:(tt + 1) * D],
                                                xa[:, tt * D:(tt + 1) * D], bf,
                                                pre_deps=(x_tok, p.tok("sa"))))
                        def _(v, dep=ln_toks[-1]):
                            p.wait(v, dep)
                            v.tensor_mul(bf.rcp2[:, :], bf.mv[:, 1:2], bf.rstd[:, :])
                            v.drain()
                            v.tensor_mul(bf.rcp2[:, :], bf.rcp2[:, :], bf.rstd[:, :])
                            v.drain()
                            p.inc(v.tensor_scalar(bf.rcp2[:, :], bf.rcp2[:, :],
                                                  float(D), 1e-24,
                                                  op0=OP.mult, op1=OP.max), "sv", 1)
                        blk.vector(_)
                        ss_tok = p.tok("sv")

                        sq.wait(ss_tok)
                        sq.op(lambda sc: sc.activation(bf.rcp2[:, :], bf.rcp2[:, :],
                                                       AF.Ln), sem=None)
                        sq.op(lambda sc: sc.drain(), sem=None)
                        rcp_tok = sq.op(lambda sc: sc.activation(
                            bf.rcp2[:, :], bf.rcp2[:, :], AF.Exp, scale=-0.5))

                        def _(v, dep=rcp_tok, tt=tt, rb=rb):
                            p.wait(v, dep)
                            p.inc(v.tensor_copy(rn_park[:, rb + tt:rb + tt + 1],
                                                bf.rcp2[:, :]), "sv", 1)
                        blk.vector(_)
                        rn_parked.append(p.tok("sv"))
                    lnst["ln"] = ln_toks[-1]
                    lnst["rnp"] = rn_parked[-1]

                emit_ln(0, None)
                spp = {"x": None, "q": None, "v": None, "r": None}
                spn = {}
                for n in range(NA_CHUNKS):
                    t0 = n * CH
                    ln_dep = lnst["ln"]
                    rnp_dep = lnst["rnp"]
                    rn_done = None

                    sv_tb = p.v["sv"]

                    def _(t, dep=ln_dep):
                        p.wait(t, dep, p.tok("sv"))
                        for e in range(EC):
                            if e >= 2:
                                p.wait(t, ("sv", sv_tb + 2 * (e - 1)))
                            for tt in range(2):
                                p.inc(t.transpose(
                                    bf.pbk[6 + (e % 2)][:, tt * 128:tt * 128 + 128],
                                    xa[:, tt * D + e * 128:tt * D + (e + 1) * 128],
                                    bf.id_t[:, :]), "st", 1)
                    blk.tensor(_)
                    tr_base = p.v["st"] - 16

                    sq.wait(spp["x"])
                    xl_tok = _evac_hilo(p, blk, sq, bf,
                        [(("st", tr_base + 2 * e + tt + 1),
                          bf.pbk[6 + (e % 2)][:, tt * 128:tt * 128 + 128],
                          xnTh[:, e * CH + tt * 128:e * CH + tt * 128 + 128],
                          xnTl[:, e * CH + tt * 128:e * CH + tt * 128 + 128],
                          None)
                         for e in range(EC) for tt in range(2)])

                    def _(g, t0=t0, dep=xl_tok):
                        p.wait(g, dep)
                        for dst, srcb in ((xnTh_d, xnTh), (xnTl_d, xnTl)):
                            p.inc(g.dma_start(
                                out=dst[sqn, :, :, t0:t0 + CH],
                                in_=srcb[:, :].rearrange("p (e q) -> p e q", e=EC)),
                                "sdso", 16)
                    blk.gpsimd(_)
                    spn["x"] = p.tok("sdso")

                    sv_kb = p.v["sv"]

                    def _(t, dep=xl_tok):
                        p.wait(t, dep, w_tok)
                        for m in range(EC):
                            if m >= 2:
                                p.wait(t, ("sv", sv_kb + m - 1))
                            bank = bf.pbk[m % 2]
                            for e in range(EC):
                                wh, wl = wof(wk16, e, m, EC * D)
                                xh = xnTh[:, e * CH:(e + 1) * CH]
                                xl = xnTl[:, e * CH:(e + 1) * CH]
                                for pi, (ws, xs) in enumerate(PASSES):
                                    mm = t.matmul(bank[:, 0:CH],
                                                  wh if ws == 0 else wl,
                                                  xh if xs == 0 else xl,
                                                  start=(e == 0 and pi == 0),
                                                  stop=(e == EC - 1 and pi == 2))
                            p.inc(mm, "st", 1)
                    blk.tensor(_)
                    k_base = p.v["st"] - EC

                    _evac_hilo(p, blk, sq, bf,
                               [(("st", k_base + m + 1), bf.pbk[m % 2][:, 0:CH],
                                 KTh[m][:, t0:t0 + CH], KTl[m][:, t0:t0 + CH],
                                 bf.bk_t[:, m:m + 1]) for m in range(EC)])

                    sq.wait(spp["q"])
                    sv_qb = p.v["sv"]

                    def _(t, dep=xl_tok):
                        p.wait(t, dep, w_tok, ("sv", sv_qb))
                        for m in range(EC):
                            if m >= 2:
                                p.wait(t, ("sv", sv_qb + m - 1))
                            bank = bf.pbk[2 + (m % 2)]
                            for e in range(EC):
                                wh, wl = wof(wq16, e, m, EC * D)
                                xh = xnTh[:, e * CH:(e + 1) * CH]
                                xl = xnTl[:, e * CH:(e + 1) * CH]
                                for pi, (ws, xs) in enumerate(PASSES):
                                    mm = t.matmul(bank[:, 0:CH],
                                                  wh if ws == 0 else wl,
                                                  xh if xs == 0 else xl,
                                                  start=(e == 0 and pi == 0),
                                                  stop=(e == EC - 1 and pi == 2))
                            p.inc(mm, "st", 1)
                    blk.tensor(_)
                    q_base = p.v["st"] - EC

                    qe_tok = _evac_hilo(p, blk, sq, bf,
                        [(("st", q_base + m + 1), bf.pbk[2 + (m % 2)][:, 0:CH],
                          QTh[:, m * CH:(m + 1) * CH], QTl[:, m * CH:(m + 1) * CH],
                          bf.bq_t[:, m:m + 1]) for m in range(EC)])

                    def _(g, t0=t0, dep=qe_tok):
                        p.wait(g, dep)
                        for dst, srcb in ((qTh_d, QTh), (qTl_d, QTl)):
                            p.inc(g.dma_start(
                                out=dst[sqn, :, :, t0:t0 + CH],
                                in_=srcb[:, :].rearrange("p (e q) -> p e q", e=EC)),
                                "sdsq", 16)
                    blk.gpsimd(_)
                    spn["q"] = p.tok("sdsq")

                    v_guard = p.tok("sv")
                    if n + 1 < NA_CHUNKS:
                        emit_ln(n + 1, ("st", tr_base + 16))

                    sv_vb = p.v["sv"]

                    def _(t, dep=xl_tok):
                        p.wait(t, dep, v_guard)
                        idx = 0
                        for tt in range(2):
                            for hf in range(2):
                                idx += 1
                                if idx >= 3:
                                    p.wait(t, ("sv", sv_vb + idx - 2))
                                bank = bf.pbk[4 + hf]
                                for e in range(EC):
                                    xh = xnTh[:, e * CH + tt * 128:e * CH + (tt + 1) * 128]
                                    xl = xnTl[:, e * CH + tt * 128:e * CH + (tt + 1) * 128]
                                    whh = wv16[:, e * D + hf * 512:e * D + (hf + 1) * 512]
                                    wll = wv16[:, EC * D + e * D + hf * 512:
                                               EC * D + e * D + (hf + 1) * 512]
                                    for pi, (ws, xs) in enumerate(PASSES):
                                        mm = t.matmul(bank[:, 0:512],
                                                      xh if xs == 0 else xl,
                                                      whh if ws == 0 else wll,
                                                      start=(e == 0 and pi == 0),
                                                      stop=(e == EC - 1 and pi == 2))
                                p.inc(mm, "st", 1)
                    blk.tensor(_)
                    v_base = p.v["st"] - 4

                    v32_toks = []

                    def _(v, v_base=v_base, sp=spp["v"]):
                        p.wait(v, sp)
                        for idx in range(4):
                            tt, hf = divmod(idx, 2)
                            sl = slice(tt * D + hf * 512, tt * D + (hf + 1) * 512)
                            p.wait(v, ("st", v_base + idx + 1))
                            p.inc(v.tensor_add(V32[:, sl], bf.pbk[4 + hf][:, 0:512],
                                               bvb_t[:, hf * 512:(hf + 1) * 512]),
                                  "sv", 1)
                            v.drain()
                    blk.vector(_)
                    v32_base = p.v["sv"] - 4
                    for idx in range(4):
                        v32_toks.append(("sv", v32_base + idx + 1))

                    vh_toks = []
                    for idx in range(4):
                        tt, hf = divmod(idx, 2)
                        sl = slice(tt * D + hf * 512, tt * D + (hf + 1) * 512)
                        sq.wait(v32_toks[idx])
                        vh_toks.append(sq.op(lambda sc, sl=sl: sc.copy(
                            VTh[:, sl], V32[:, sl])))

                    def _(v):
                        for idx in range(4):
                            tt, hf = divmod(idx, 2)
                            sl = slice(tt * D + hf * 512, tt * D + (hf + 1) * 512)
                            p.wait(v, vh_toks[idx])
                            ins = v.tensor_sub(VTl[:, sl], V32[:, sl], VTh[:, sl])
                            v.drain()
                        p.inc(ins, "sv", 1)
                    blk.vector(_)
                    ve_tok = p.tok("sv")

                    def _(g, n=n, dep=ve_tok):
                        p.wait(g, dep)
                        for dst, srcb in ((vTh_d, VTh), (vTl_d, VTl)):
                            p.inc(g.dma_start(
                                out=dst[sqn, 2 * n:2 * n + 2, :, :].rearrange(
                                    "a p d -> p a d"),
                                in_=srcb[:, :].rearrange("p (a d) -> p a d", a=2)),
                                "sdsv", 16)
                    blk.gpsimd(_)
                    spn["v"] = p.tok("sdsv")

                    def _(t, dep=rnp_dep, rb=(n % 2) * 2):
                        p.wait(t, dep, p.tok("sv"))
                        for tt in range(2):
                            p.inc(t.transpose(bf.pbk[2 + tt][0:1, 0:128],
                                              rn_park[:, rb + tt:rb + tt + 1],
                                              bf.id_t[:, :]),
                                  "st", 1)
                    blk.tensor(_)
                    rt_base = p.v["st"] - 2

                    sq.wait(("st", rt_base + 2), spp["r"])
                    for tt in range(2):
                        rn_done = sq.op(lambda sc, tt=tt: sc.copy(
                            rn_row[0:1, tt * 128:(tt + 1) * 128],
                            bf.pbk[2 + tt][0:1, 0:128]))
                    sq.flush()

                    def _(g, t0=t0, dep=rn_done):
                        p.wait(g, dep)
                        p.inc(g.dma_start(out=rn_d[sqn, 0:1, t0:t0 + CH],
                                          in_=rn_row[0:1, :]), "sdsr", 16)
                    blk.gpsimd(_)
                    spn["r"] = p.tok("sdsr")
                    spp = dict(spn)

                def _(g):
                    p.wait(g, p.tok("sdso"), p.tok("sdsq"), p.tok("sdsv"),
                           p.tok("sdsr"))
                blk.gpsimd(_)

                def _(sy):
                    p.wait(sy, p.tok("sdwA"), p.tok("sdx0"), p.tok("sdx1"))
                blk.sync(_)

        with ExitStack() as a2es:
            sb2 = lambda name, shape, dt=F32: a2es.enter_context(
                nc.sbuf_tensor(name, shape, dt))
            wo16 = sb2(f"wo16_{sqn}", [128, 2 * EC * D], BF16)
            QThb = [sb2(f"aQTh{i}_s{sqn}", [128, EC * CH], BF16) for i in range(2)]
            QTlb = [sb2(f"aQTl{i}_s{sqn}", [128, EC * CH], BF16) for i in range(2)]
            VThb = [sb2(f"aVTh{i}_s{sqn}", [128, 2 * D], BF16) for i in range(2)]
            VTlb = [sb2(f"aVTl{i}_s{sqn}", [128, 2 * D], BF16) for i in range(2)]
            ET = sb2(f"ET_s{sqn}", [128, 2 * CH])
            ASb = [sb2(f"AS{i}_s{sqn}", [128, 2 * CH]) for i in range(2)]
            ATTh = sb2(f"ATTh_s{sqn}", [128, 2 * CH], BF16)
            ATTl = sb2(f"ATTl_s{sqn}", [128, 2 * CH], BF16)
            OTh = sb2(f"OTh_s{sqn}", [128, EC * CH], BF16)
            OTl = sb2(f"OTl_s{sqn}", [128, EC * CH], BF16)
            sT = sb2(f"sT_s{sqn}", [128, EC * CH])
            nmx8 = sb2(f"nmx8_s{sqn}", [128, NCH])
            sme8 = sb2(f"sme8_s{sqn}", [128, NCH])
            if sqn == 0:
                pq32 = [sb2(f"pq32_{i}", [128, 2048]) for i in range(2)]
                pq16 = [sb2(f"pq16_{i}", [128, 4096], BF16) for i in range(2)]
                pieces = []
                for c in range(C16):
                    pieces.append((wc_d[c, :, :],
                                   wc2_d[c, :, 0:2048], wc2_d[c, :, 2048:4096]))
                for m in range(EC):
                    for h in range(2):
                        pieces.append((pj_d[m, :, h * 2048:(h + 1) * 2048],
                                       pj2_d[m, :, h * 2048:(h + 1) * 2048],
                                       pj2_d[m, :, 4096 + h * 2048:
                                             4096 + (h + 1) * 2048]))

            with nc.Block() as blk:
                sq = ScalarQ(p, blk)
                pp_load = [None] * 32
                pp_lo = [None] * 32
                pp_sp = [None] * 32

                def prep_load(ks):
                    if sqn != 0:
                        return
                    for k in ks:
                        if k >= len(pieces):
                            continue
                        sl = k % 2

                        def _(sy, k=k, sl=sl,
                              dep=(pp_lo[k - 2] if k >= 2 else None)):
                            p.wait(sy, dep)
                            p.inc(sy.dma_start(out=pq32[sl][:, :],
                                               in_=pieces[k][0]), "sdsi", 16)
                        blk.sync(_)
                    tok = p.tok("sdsi")
                    for k in ks:
                        if k < len(pieces):
                            pp_load[k] = tok

                def prep_stage(ks):
                    if sqn != 0:
                        return
                    for k in ks:
                        if k >= len(pieces):
                            continue
                        sl = k % 2
                        sq.wait(pp_load[k])
                        if k >= 2:
                            sq.wait(pp_sp[k - 2])
                        hi_tok = sq.op(lambda sc, sl=sl: sc.copy(
                            pq16[sl][:, 0:2048], pq32[sl][:, :]))

                        def _(v, sl=sl, dep=hi_tok):
                            p.wait(v, dep)
                            p.inc(v.tensor_sub(pq16[sl][:, 2048:4096],
                                               pq32[sl][:, :],
                                               pq16[sl][:, 0:2048]), "sv", 1)
                        blk.vector(_)
                        pp_lo[k] = p.tok("sv")

                        def _(g, k=k, sl=sl, dep=pp_lo[k]):
                            p.wait(g, dep)
                            p.inc(g.dma_start(out=pieces[k][1],
                                              in_=pq16[sl][:, 0:2048]), "sdo", 16)
                            p.inc(g.dma_start(out=pieces[k][2],
                                              in_=pq16[sl][:, 2048:4096]), "sdo", 16)
                        blk.gpsimd(_)
                    tok = p.tok("sdo")
                    for k in ks:
                        if k < len(pieces):
                            pp_sp[k] = tok

                def _(sy):
                    p.inc(sy.dma_start(out=wo16[:, :], in_=wo2_d), "sdwB", 16)
                    for dst, src in ((QThb[0], qTh_d), (QTlb[0], qTl_d)):
                        p.inc(sy.dma_start(
                            out=dst[:, :].rearrange("p (e q) -> p e q", e=EC),
                            in_=src[sqn, :, :, 0:CH]), "sdc0", 16)
                    for dst, src in ((VThb[0], vTh_d), (VTlb[0], vTl_d)):
                        p.inc(sy.dma_start(
                            out=dst[:, :].rearrange("p (a d) -> p a d", a=2),
                            in_=src[sqn, 0:2, :, :].rearrange("a p d -> p a d")),
                            "sdc0", 16)
                blk.sync(_)
                wo_tok = p.tok("sdwB")
                q_toks = [p.tok("sdc0"), None]
                v_toks = [q_toks[0], None]
                q_use = [None, None]
                atg_hist = [None, None]
                as_prev = None
                prev = None

                def emit_atgroup(pv):
                    def _(t, dep=pv["as_tok"], ASp=pv["AS"]):
                        p.wait(t, dep, p.tok("sv"))
                        for qt in range(2):
                            for kt in range(2):
                                p.inc(t.transpose(
                                    bf.pbk[6 + qt][:, kt * 128:(kt + 1) * 128],
                                    ASp[:, qt * CH + kt * 128:qt * CH + (kt + 1) * 128],
                                    bf.id_t[:, :]), "st", 1)
                    blk.tensor(_)
                    atg_tok = p.tok("st")
                    atg_hist[pv["n"] % 2] = atg_tok
                    pv["att_tok"] = _evac_hilo(p, blk, sq, bf,
                        [(atg_tok, bf.pbk[6 + qt][:, kt * 128:(kt + 1) * 128],
                          ATTh[:, kt * CH + qt * 128:kt * CH + qt * 128 + 128],
                          ATTl[:, kt * CH + qt * 128:kt * CH + qt * 128 + 128],
                          None) for qt in range(2) for kt in range(2)])

                def emit_av(pv):
                    npv = pv["n"]
                    parp = npv % 2
                    VThp, VTlp = VThb[parp], VTlb[parp]
                    sv_av = p.v["sv"]

                    def _(t, dep=pv["att_tok"], vdep=v_toks[parp],
                          VTh=VThp, VTl=VTlp):
                        p.wait(t, dep, vdep)
                        for e in range(EC):
                            if e >= 2:
                                p.wait(t, ("sv", sv_av + e - 1))
                            bank = bf.pbk[6 + (e % 2)]
                            first = True
                            for kt in range(2):
                                vh = VTh[:, kt * D + e * 128:kt * D + (e + 1) * 128]
                                vl = VTl[:, kt * D + e * 128:kt * D + (e + 1) * 128]
                                ah = ATTh[:, kt * CH:(kt + 1) * CH]
                                al = ATTl[:, kt * CH:(kt + 1) * CH]
                                for ws, xs in PASSES:
                                    mm = t.matmul(bank[:, 0:CH],
                                                  vh if ws == 0 else vl,
                                                  ah if xs == 0 else al,
                                                  start=first,
                                                  stop=(kt == 1 and (ws, xs) == PASSES[-1]))
                                    first = False
                            p.inc(mm, "st", 1)
                    blk.tensor(_)
                    av_base = p.v["st"] - EC
                    av_tok = p.tok("st")

                    ot_tok = _evac_hilo(p, blk, sq, bf,
                        [(("st", av_base + e + 1), bf.pbk[6 + (e % 2)][:, 0:CH],
                          OTh[:, e * CH:(e + 1) * CH], OTl[:, e * CH:(e + 1) * CH],
                          None) for e in range(EC)])
                    return av_tok, ot_tok

                def emit_wo(pv, ot_tok):
                    npv = pv["n"]
                    t0p = npv * CH
                    sa_wo = p.v["sa"]
                    sdso_prev = p.tok("sdso")

                    def _(t, dep=ot_tok, wdep=wo_tok):
                        p.wait(t, dep, wdep)
                        for m in range(EC):
                            if m >= 2:
                                p.wait(t, ("sa", sa_wo + m - 1))
                            bank = bf.pbk[4 + (m % 2)]
                            for e in range(EC):
                                wh = wo16[:, e * D + m * 128:e * D + (m + 1) * 128]
                                wl = wo16[:, EC * D + e * D + m * 128:
                                          EC * D + e * D + (m + 1) * 128]
                                oh_ = OTh[:, e * CH:(e + 1) * CH]
                                ol_ = OTl[:, e * CH:(e + 1) * CH]
                                for pi, (ws, xs) in enumerate(PASSES):
                                    mm = t.matmul(bank[:, 0:CH],
                                                  wh if ws == 0 else wl,
                                                  oh_ if xs == 0 else ol_,
                                                  start=(e == 0 and pi == 0),
                                                  stop=(e == EC - 1 and pi == 2))
                            p.inc(mm, "st", 1)
                    blk.tensor(_)
                    wo_base = p.v["st"] - EC
                    wo_done = p.tok("st")

                    se_tok = None
                    sq.wait(sdso_prev)
                    for m in range(EC):
                        sq.wait(("st", wo_base + m + 1))
                        se_tok = sq.op(lambda sc, m=m: sc.activation(
                            sT[:, m * CH:(m + 1) * CH], bf.pbk[4 + (m % 2)][:, 0:CH],
                            AF.Identity, bias=bf.bo_t[:, m:m + 1]))
                    sq.flush()

                    def _(g, t0=t0p, dep=se_tok):
                        p.wait(g, dep)
                        p.inc(g.dma_start(out=s_d[sqn, :, :, t0:t0 + CH],
                                          in_=sT[:, :].rearrange("p (e q) -> p e q",
                                                                 e=EC)), "sdso", 16)
                    blk.gpsimd(_)
                    return wo_done

                for n in range(NA_CHUNKS):
                    t0 = n * CH
                    par = n % 2
                    QTh, QTl = QThb[par], QTlb[par]
                    AS = ASb[par]

                    if n >= 1:
                        prep_stage((4 * (n - 1) + 2, 4 * (n - 1) + 3))
                    prep_load((4 * n, 4 * n + 1))

                    def _(t, qdep=q_toks[par], adep=as_prev, QTh=QTh, QTl=QTl):
                        p.wait(t, qdep, adep)
                        for mp in range(4):
                            bank = bf.pbk[mp]
                            for e in range(EC):
                                qh = QTh[:, e * CH:e * CH + 128]
                                ql = QTl[:, e * CH:e * CH + 128]
                                kh = KTh[e][:, mp * 512:(mp + 1) * 512]
                                kl = KTl[e][:, mp * 512:(mp + 1) * 512]
                                for pi, (ws, xs) in enumerate(PASSES):
                                    mm = t.matmul(bank[:, 0:512],
                                                  qh if ws == 0 else ql,
                                                  kh if xs == 0 else kl,
                                                  start=(e == 0 and pi == 0),
                                                  stop=(e == EC - 1 and pi == 2))
                            p.inc(mm, "st", 1)
                    blk.tensor(_)
                    sc_base0 = p.v["st"] - 4

                    exq = [[], []]
                    as_toks = [None, None]

                    def emit_reduce(qt, sc_base):
                        def _(v, sc_base=sc_base):
                            for m in range(NCH):
                                mp, half = divmod(m, 2)
                                p.wait(v, ("st", sc_base + mp + 1))
                                p.inc(v.reduce_max(
                                    nmx8[:, m:m + 1],
                                    bf.pbk[mp][:, half * CH:(half + 1) * CH],
                                    axis=AX, negate=True), "sv", 1)
                        blk.vector(_)
                        return p.v["sv"] - NCH

                    def emit_expacc(qt, nm_base, n=n, AS=AS):
                        acc0_base = p.v["sv"]
                        for m in range(NCH):
                            mp, half = divmod(m, 2)
                            sq.wait(("sv", nm_base + m + 1))
                            if m >= 2:
                                sq.wait(("sv", acc0_base + m - 1))
                            exq[qt].append(sq.op(
                                lambda sc, mp=mp, half=half, m=m: sc.activation(
                                    ET[:, (m % 2) * CH:(m % 2 + 1) * CH],
                                    bf.pbk[mp][:, half * CH:(half + 1) * CH],
                                    AF.Exp, bias=nmx8[:, m:m + 1],
                                    accum_out=sme8[:, m:m + 1])))
                        ex_base = p.v["sa"] - NCH

                        def _(v, qt=qt, n=n, ex_base=ex_base, AS=AS,
                              gdep=(atg_hist[n % 2] if qt == 0 else None)):
                            p.wait(v, gdep)
                            for m in range(NCH):
                                p.wait(v, ("sa", ex_base + m + 1))
                                v.reciprocal(bf.rcp[:, :], sme8[:, m:m + 1])
                                v.drain()
                                r = bf.rcp
                                if m == n:
                                    v.tensor_scalar_mul(bf.rcp2[:, :],
                                                        bf.rcp[:, :], 2.0)
                                    v.drain()
                                    r = bf.rcp2
                                dst = AS[:, qt * CH:(qt + 1) * CH]
                                esl = ET[:, (m % 2) * CH:(m % 2 + 1) * CH]
                                if m == 0:
                                    ins = v.tensor_scalar(dst, esl, r[:, :], None,
                                                          op0=OP.mult)
                                else:
                                    ins = v.scalar_tensor_tensor(
                                        dst, esl, r[:, :], dst,
                                        op0=OP.mult, op1=OP.add)
                                p.inc(ins, "sv", 1)
                                if m < NCH - 1:
                                    v.drain()
                        blk.vector(_)
                        as_toks[qt] = p.tok("sv")

                    emit_expacc(0, emit_reduce(0, sc_base0))

                    if prev is not None:
                        emit_atgroup(prev)

                    def _(t, qdep=q_toks[par], QTh=QTh, QTl=QTl,
                          ex0=list(exq[0])):
                        p.wait(t, qdep)
                        for mp in range(4):
                            p.wait(t, ex0[2 * mp + 1])
                            bank = bf.pbk[mp]
                            for e in range(EC):
                                qh = QTh[:, e * CH + 128:e * CH + 256]
                                ql = QTl[:, e * CH + 128:e * CH + 256]
                                kh = KTh[e][:, mp * 512:(mp + 1) * 512]
                                kl = KTl[e][:, mp * 512:(mp + 1) * 512]
                                for pi, (ws, xs) in enumerate(PASSES):
                                    mm = t.matmul(bank[:, 0:512],
                                                  qh if ws == 0 else ql,
                                                  kh if xs == 0 else kl,
                                                  start=(e == 0 and pi == 0),
                                                  stop=(e == EC - 1 and pi == 2))
                            p.inc(mm, "st", 1)
                    blk.tensor(_)
                    sc_base1 = p.v["st"] - 4
                    q_use[par] = p.tok("st")

                    nm_base1 = emit_reduce(1, sc_base1)

                    if prev is not None:
                        av_tok, ot_tok = emit_av(prev)
                    emit_expacc(1, nm_base1)
                    as_prev = as_toks[1]

                    if prev is not None:
                        emit_wo(prev, ot_tok)
                        if n + 1 < NA_CHUNKS:
                            def _(sy, n=n, qfree=q_use[(n + 1) % 2], vfree=av_tok):
                                pr = (n + 1) % 2
                                sem_q = "sdc0" if pr == 0 else "sdc1"
                                p.wait(sy, qfree)
                                for dst, srcd in ((QThb[pr], qTh_d),
                                                  (QTlb[pr], qTl_d)):
                                    p.inc(sy.dma_start(
                                        out=dst[:, :].rearrange("p (e q) -> p e q",
                                                                e=EC),
                                        in_=srcd[sqn, :, :,
                                                 (n + 1) * CH:(n + 2) * CH]),
                                        sem_q, 16)
                                p.wait(sy, vfree)
                                for dst, srcd in ((VThb[pr], vTh_d),
                                                  (VTlb[pr], vTl_d)):
                                    p.inc(sy.dma_start(
                                        out=dst[:, :].rearrange("p (a d) -> p a d",
                                                                a=2),
                                        in_=srcd[sqn, 2 * n + 2:2 * n + 4, :, :]
                                        .rearrange("a p d -> p a d")), sem_q, 16)
                            blk.sync(_)
                            q_toks[(n + 1) % 2] = p.tok(
                                "sdc0" if (n + 1) % 2 == 0 else "sdc1")
                            v_toks[(n + 1) % 2] = q_toks[(n + 1) % 2]
                    elif n + 1 < NA_CHUNKS:
                        def _(sy, n=n):
                            pr = (n + 1) % 2
                            sem_q = "sdc0" if pr == 0 else "sdc1"
                            for dst, srcd in ((QThb[pr], qTh_d), (QTlb[pr], qTl_d)):
                                p.inc(sy.dma_start(
                                    out=dst[:, :].rearrange("p (e q) -> p e q",
                                                            e=EC),
                                    in_=srcd[sqn, :, :,
                                             (n + 1) * CH:(n + 2) * CH]),
                                    sem_q, 16)
                            for dst, srcd in ((VThb[pr], vTh_d), (VTlb[pr], vTl_d)):
                                p.inc(sy.dma_start(
                                    out=dst[:, :].rearrange("p (a d) -> p a d",
                                                            a=2),
                                    in_=srcd[sqn, 2 * n + 2:2 * n + 4, :, :]
                                    .rearrange("a p d -> p a d")), sem_q, 16)
                        blk.sync(_)
                        q_toks[(n + 1) % 2] = p.tok(
                            "sdc0" if (n + 1) % 2 == 0 else "sdc1")
                        v_toks[(n + 1) % 2] = q_toks[(n + 1) % 2]

                    prep_stage((4 * n, 4 * n + 1))
                    prep_load((4 * n + 2, 4 * n + 3))

                    prev = {"n": n, "par": par, "AS": AS, "as_tok": as_toks[1]}

                prep_stage((4 * (NA_CHUNKS - 1) + 2, 4 * (NA_CHUNKS - 1) + 3))
                emit_atgroup(prev)
                av_tok, ot_tok = emit_av(prev)
                emit_wo(prev, ot_tok)

                def _(g):
                    p.wait(g, p.tok("sdso"), p.tok("sdo"))
                blk.gpsimd(_)

                def _(sy):
                    p.wait(sy, p.tok("sdwB"), p.tok("sdc0"), p.tok("sdc1"),
                           p.tok("sdsi"))
                blk.sync(_)


def _phase_f(nc, p, es, sqn, bf, x_d, s_d, out_d, gw12_d, gw2_d, wc2_d, pj2_d,
             lng_d, lnb_d, oh_d, xnTh_d, xnTl_d, rn_d):
    with ExitStack() as pes:
        sbX = lambda name, shape, dt=F32: pes.enter_context(
            nc.sbuf_tensor(name, shape, dt))
        gw16 = sbX(f"gw16_{sqn}", [128, 2 * EC * HG], BF16)
        gw2_t = sbX(f"gw2_s{sqn}", [128, (HG // 128) * C16])
        lng_t = sbX(f"lng_s{sqn}", [128, D])
        lnb_t = sbX(f"lnb_s{sqn}", [128, D])
        oh_t = sbX(f"oh_s{sqn}", [C16, C16 * 128])
        wcb = [sbX(f"wcb{i}_s{sqn}", [128, 2 * EC * NN], BF16) for i in range(2)]
        pjb = [sbX(f"pjbuf{i}_s{sqn}", [128, 2 * 32 * 128], BF16) for i in range(2)]
        xt = sbX(f"xt_s{sqn}", [128, 2 * D])
        xnThb = [sbX(f"fxnTh{i}_s{sqn}", [128, EC * CH], BF16) for i in range(2)]
        xnTlb = [sbX(f"fxnTl{i}_s{sqn}", [128, EC * CH], BF16) for i in range(2)]
        sTf = sbX(f"sTf_s{sqn}", [128, EC * CH])
        lgT = sbX(f"lgT_s{sqn}", [C16, CH])
        gvT = sbX(f"gvT_s{sqn}", [C16, CH])
        rnbS = sbX(f"rnbS_s{sqn}", [C16, CH])
        ltok = sbX(f"ltok_s{sqn}", [128, 2 * C16])
        wtok = sbX(f"wtok_s{sqn}", [128, 2 * C16])
        mskT = sbX(f"mskT_s{sqn}", [128, 2 * C16])
        exT = sbX(f"exT_s{sqn}", [128, 2 * C16])
        wT16 = sbX(f"wT16_s{sqn}", [C16, CH])
        colsg = sbX(f"colsg_s{sqn}", [128, 32 * CH])
        colh = sbX(f"colh_s{sqn}", [128, 32 * CH], BF16)
        coll = sbX(f"coll_s{sqn}", [128, 32 * CH], BF16)
        ysT = sbX(f"ysT_s{sqn}", [128, EC * CH])
        hT = colsg[:, 24 * CH:32 * CH]
        ytok = sbX(f"ytok_s{sqn}", [128, 2 * D])
        stok = sbX(f"stok_s{sqn}", [128, 2 * D])
        rn_fc = sbX(f"rn_fc_s{sqn}", [1, CH])

        def gwof(e, m):
            return (gw16[:, e * HG + m * 128:e * HG + (m + 1) * 128],
                    gw16[:, EC * HG + e * HG + m * 128:EC * HG + e * HG + (m + 1) * 128])

        with nc.Block() as blk:
            sq = ScalarQ(p, blk)

            def _(sy):
                p.inc(sy.dma_start(out=gw16[:, :], in_=gw12_d), "sdwA", 16)
                p.inc(sy.dma_start(out=gw2_t[:, :], in_=gw2_d), "sdwA", 16)
                p.inc(sy.dma_start(out=lng_t[:, :], in_=lng_d), "sdwA", 16)
                p.inc(sy.dma_start(out=lnb_t[:, :], in_=lnb_d), "sdwA", 16)
                p.inc(sy.dma_start(out=oh_t[:, :], in_=oh_d), "sdwA", 16)
            blk.sync(_)
            gw_tok = p.tok("sdwA")

            xn_use = [None, None]
            xt_use = [None]
            st_use = [None]
            rn_use = [None]
            cols_prev = [None, None]
            pj_prev = [None]
            hfree = [None]
            hbank_prev = [None, None]
            xnp_toks = [None, None]
            for stile in range(NF_TILES):
                t0 = stile * CH
                par = stile % 2
                sdo_prev = p.tok("sdo")
                xnTh, xnTl = xnThb[par], xnTlb[par]

                if stile == 0:
                    def _(sy):
                        p.inc(sy.dma_start(
                            out=xnThb[0][:, :].rearrange("p (e q) -> p e q", e=EC),
                            in_=xnTh_d[sqn, :, :, 0:CH]), "sdt0", 16)
                        p.inc(sy.dma_start(
                            out=xnTlb[0][:, :].rearrange("p (e q) -> p e q", e=EC),
                            in_=xnTl_d[sqn, :, :, 0:CH]), "sdt1", 16)
                        p.wait(sy, p.tok("sdsr"))
                        p.inc(sy.dma_start(out=rn_fc[0:1, :],
                                           in_=rn_d[sqn, 0:1, 0:CH]), "sdt0", 16)
                    blk.sync(_)
                    xnp_toks[0] = p.tok("sdt0")
                    xnp_toks[1] = p.tok("sdt1")

                def _(sy, t0=t0, par=par, xdep=xt_use[0], sdep=st_use[0]):
                    sx = "sdx0" if par == 0 else "sdx1"
                    ss_ = "sds0" if par == 0 else "sds1"
                    p.wait(sy, sdep)
                    p.inc(sy.dma_start(
                        out=sTf[:, :].rearrange("p (e q) -> p e q", e=EC),
                        in_=s_d[sqn, :, :, t0:t0 + CH]), ss_, 16)
                    p.wait(sy, xdep)
                    p.inc(sy.dma_start(
                        out=xt[:, :].rearrange("p (a d) -> p a d", a=2),
                        in_=x_d[sqn, t0:t0 + CH, :].rearrange(
                            "(a p) d -> p a d", p=128)), sx, 16)
                blk.sync(_)
                x_tok = p.tok("sdx0" if par == 0 else "sdx1")
                xnTh_tok = xnp_toks[0]
                xnTl_tok = xnp_toks[1]
                s_tok_in = p.tok("sds0" if par == 0 else "sds1")

                sa_base = p.v["sa"]

                def _(t, dep0=xnTh_tok, dep1=xnTl_tok, wdep=gw_tok,
                      b0=hbank_prev[0], b1=hbank_prev[1],
                      xnTh=xnTh, xnTl=xnTl):
                    p.wait(t, dep0, dep1, wdep, b0, b1)
                    for m in range(HG // 128):
                        if m >= 2:
                            p.wait(t, ("sa", sa_base + m - 1))
                        bank = bf.pbk[m % 2]
                        for e in range(EC):
                            wh, wl = gwof(e, m)
                            xh = xnTh[:, e * CH:(e + 1) * CH]
                            xl = xnTl[:, e * CH:(e + 1) * CH]
                            for pi, (ws, xs) in enumerate(PASSES):
                                mm = t.matmul(bank[:, 0:CH],
                                              wh if ws == 0 else wl,
                                              xh if xs == 0 else xl,
                                              start=(e == 0 and pi == 0),
                                              stop=(e == EC - 1 and pi == 2))
                        p.inc(mm, "st", 1)
                blk.tensor(_)
                h_base = p.v["st"] - HG // 128

                hT_tok = None
                sq.wait(hfree[0])
                for m in range(HG // 128):
                    sq.wait(("st", h_base + m + 1))
                    hT_tok = sq.op(lambda sc, m=m: sc.activation(
                        hT[:, m * CH:(m + 1) * CH], bf.pbk[m % 2][:, 0:CH],
                        AF.Gelu, bias=bf.gb1_t[:, m:m + 1]))

                def _(t, dep=hT_tok, rdep=xnTh_tok, xnTh=xnTh, xnTl=xnTl):
                    p.wait(t, dep, rdep, p.tok("sv"))
                    for k in range(HG // 128):
                        mm = t.matmul(bf.pbk[2][0:C16, 0:CH],
                                      gw2_t[:, k * C16:(k + 1) * C16],
                                      hT[:, k * CH:(k + 1) * CH],
                                      start=(k == 0), stop=(k == HG // 128 - 1))
                    p.inc(mm, "st", 1)
                    first = True
                    for e in range(EC):
                        ch_ = bf.cnTh[:, e * C16:(e + 1) * C16]
                        cl_ = bf.cnTl[:, e * C16:(e + 1) * C16]
                        xh = xnTh[:, e * CH:(e + 1) * CH]
                        xl = xnTl[:, e * CH:(e + 1) * CH]
                        for ws, xs in PASSES:
                            mm = t.matmul(bf.pbk[3][0:C16, 0:CH],
                                          ch_ if ws == 0 else cl_,
                                          xh if xs == 0 else xl,
                                          start=first,
                                          stop=(e == EC - 1 and (ws, xs) == PASSES[-1]))
                            first = False
                    p.inc(mm, "st", 1)
                    p.wait(t, p.tok("sgc"))
                    p.inc(t.matmul(bf.pbk[3][0:C16, CH:2 * CH], bf.ones1[0:1, :],
                                   rn_fc[0:1, :], start=True, stop=True), "st", 1)
                blk.tensor(_)
                gv_tok = ("st", p.v["st"] - 2)
                rnb_tok = p.tok("st")
                rn_use[0] = rnb_tok

                sq.wait(gv_tok)
                gvtanh_tok = sq.op(lambda sc: sc.activation(
                    gvT[:, :], bf.pbk[2][0:C16, 0:CH],
                    AF.Tanh, bias=bf.gb2_t[:, :], scale=0.5))
                sq.wait(rnb_tok)
                gv_sb_tok = sq.op(lambda sc: sc.copy(rnbS[:, :],
                                                     bf.pbk[3][0:C16, CH:2 * CH]))

                def _(v, dep=gv_sb_tok):
                    p.wait(v, dep)
                    v.tensor_scalar(gvT[:, :], gvT[:, :], 0.5, 0.5,
                                    op0=OP.mult, op1=OP.add)
                    v.drain()
                    v.tensor_mul(lgT[:, :], bf.pbk[3][0:C16, 0:CH], rnbS[:, :])
                    v.drain()
                    p.inc(v.tensor_add(lgT[:, :], lgT[:, :], gvT[:, :]), "sv", 1)
                blk.vector(_)
                lg_tok = p.tok("sv")

                def _(sy, dep0=xnTh_tok, dep1=xnTl_tok):
                    p.wait(sy, dep0, dep1)
                    p.inc(sy.dma_start(out=wcb[0][:, :], in_=wc2_d[0, :, :]),
                          "sdc0", 16)
                blk.sync(_)
                wc_toks = [p.tok("sdc0"), None]

                def _(sy, dep=pj_prev[0]):
                    p.wait(sy, dep)
                    p.inc(sy.dma_start(out=pjb[0][:, :], in_=pj2_d[0, :, :]),
                          "sdp0", 16)
                blk.sync(_)
                pj_toks = [p.tok("sdp0"), None]

                split_toks = []
                rt_state = {}

                def emit_bcast(cp):
                    def _(t, cp=cp):
                        p.wait(t, rt_state["wT_tok"])
                        if cp >= 2:
                            p.wait(t, split_toks[2 * (cp - 2) + 1])
                        p.inc(t.matmul(bf.pbk[2 + (cp % 2)][:, 0:CH],
                                       oh_t[:, cp * 128:(cp + 1) * 128],
                                       wT16[:, :], start=True, stop=True), "st", 1)
                    blk.tensor(_)
                    bc_tok = p.tok("st")
                    for cm in range(2):
                        ti = cp * 2 + cm
                        csl = colsg[:, ti * CH:(ti + 1) * CH]
                        hsl = colh[:, ti * CH:(ti + 1) * CH]
                        lsl = coll[:, ti * CH:(ti + 1) * CH]

                        def _(v, cp=cp, csl=csl, dep1=gelu_toks[ti], dep2=bc_tok):
                            p.wait(v, dep1, dep2)
                            p.inc(v.tensor_mul(csl, csl,
                                               bf.pbk[2 + (cp % 2)][:, 0:CH]),
                                  "sv", 1)
                        blk.vector(_)
                        mul_tok = p.tok("sv")

                        sq.wait(mul_tok)
                        h_tok = sq.op(lambda sc, hsl=hsl, csl=csl: sc.copy(hsl, csl))

                        def _(v, csl=csl, hsl=hsl, lsl=lsl, dep=h_tok):
                            p.wait(v, dep)
                            p.inc(v.tensor_sub(lsl, csl, hsl), "sv", 1)
                        blk.vector(_)
                        split_toks.append(p.tok("sv"))

                cols_mm_toks = []
                gelu_toks = []
                for c in range(C16):
                    def _(sy, c=c):
                        if c + 1 < C16:
                            sem = "sdc0" if (c + 1) % 2 == 0 else "sdc1"
                            if c >= 1:
                                p.wait(sy, cols_mm_toks[c - 1])
                            p.inc(sy.dma_start(out=wcb[(c + 1) % 2][:, :],
                                               in_=wc2_d[c + 1, :, :]), sem, 16)
                    blk.sync(_)
                    if c + 1 < C16:
                        wc_toks[(c + 1) % 2] = p.tok("sdc0" if (c + 1) % 2 == 0 else "sdc1")

                    def _(t, c=c, dep=wc_toks[c % 2], xnTh=xnTh, xnTl=xnTl,
                          b4=(cols_prev[0] if c == 0 else gelu_toks[2 * (c - 1)]),
                          b5=(cols_prev[1] if c == 0 else gelu_toks[2 * (c - 1) + 1])):
                        p.wait(t, dep, b4, b5)
                        wb16 = wcb[c % 2]
                        for cm in range(2):
                            bank = bf.pbk[4 + cm]
                            for e in range(EC):
                                wh = wb16[:, e * NN + cm * 128:e * NN + (cm + 1) * 128]
                                wl = wb16[:, EC * NN + e * NN + cm * 128:
                                          EC * NN + e * NN + (cm + 1) * 128]
                                xh = xnTh[:, e * CH:(e + 1) * CH]
                                xl = xnTl[:, e * CH:(e + 1) * CH]
                                for pi, (ws, xs) in enumerate(PASSES):
                                    mm = t.matmul(bank[:, 0:CH],
                                                  wh if ws == 0 else wl,
                                                  xh if xs == 0 else xl,
                                                  start=(e == 0 and pi == 0),
                                                  stop=(e == EC - 1 and pi == 2))
                            p.inc(mm, "st", 1)
                    blk.tensor(_)
                    cc_base = p.v["st"] - 2
                    cols_mm_toks.append(p.tok("st"))

                    for cm in range(2):
                        sq.wait(("st", cc_base + cm + 1))
                        gelu_toks.append(sq.op(lambda sc, c=c, cm=cm: sc.activation(
                            colsg[:, (c * 2 + cm) * CH:(c * 2 + cm + 1) * CH],
                            bf.pbk[4 + cm][:, 0:CH], AF.Gelu,
                            bias=bf.bc_t[:, c * 2 + cm:c * 2 + cm + 1])))

                    if c == 0:
                        def _(t, dep=lg_tok, gdep=gvtanh_tok):
                            p.wait(t, dep, gdep)
                            for tt in range(2):
                                p.inc(t.transpose(
                                    bf.pbk[2][0:128, tt * C16:(tt + 1) * C16],
                                    lgT[0:C16, tt * 128:(tt + 1) * 128],
                                    bf.id_t[0:C16, 0:C16]), "st", 1)
                        blk.tensor(_)
                        lt_tok = p.tok("st")
                        sq.wait(lt_tok)
                        ltok_tok = None
                        for tt in range(2):
                            ltok_tok = sq.op(lambda sc, tt=tt: sc.copy(
                                ltok[:, tt * C16:(tt + 1) * C16],
                                bf.pbk[2][0:128, tt * C16:(tt + 1) * C16]))
                        rt_state["ltok_tok"] = ltok_tok
                    elif c == 1:
                        sq.flush()
                        w_toks = []
                        for tt in range(2):
                            sl = ltok[:, tt * C16:(tt + 1) * C16]
                            wl = wtok[:, tt * C16:(tt + 1) * C16]

                            def _(v, sl=sl, tt=tt, dep=rt_state["ltok_tok"]):
                                p.wait(v, dep)
                                v.reduce_max(bf.mxv[:, :], sl, axis=AX)
                                v.reduce_max(bf.nmx[:, :], sl, axis=AX, negate=True)
                                v.drain()
                                p.inc(v.tensor_scalar(mskT[:, tt * C16:(tt + 1) * C16],
                                                      sl, bf.mxv[:, :], None,
                                                      op0=OP.is_ge), "sv", 1)
                            blk.vector(_)
                            msk_tok = p.tok("sv")

                            sq.wait(msk_tok)
                            ex_tok = sq.op(lambda sc, sl=sl, tt=tt: sc.activation(
                                exT[:, tt * C16:(tt + 1) * C16], sl, AF.Exp,
                                bias=bf.nmx[:, :], accum_out=bf.sme[:, :]))

                            def _(v, wl=wl, tt=tt, dep=ex_tok):
                                p.wait(v, dep)
                                v.reciprocal(bf.rcp[:, :], bf.sme[:, :])
                                v.drain()
                                p.inc(v.scalar_tensor_tensor(
                                    wl, exT[:, tt * C16:(tt + 1) * C16], bf.rcp[:, :],
                                    mskT[:, tt * C16:(tt + 1) * C16],
                                    op0=OP.mult, op1=OP.mult), "sv", 1)
                            blk.vector(_)
                            w_toks.append(p.tok("sv"))
                        rt_state["w_tok"] = w_toks[-1]
                        sq.flush()
                    elif c == 2:
                        def _(t, dep=rt_state["w_tok"], ldep=rt_state["ltok_tok"]):
                            p.wait(t, dep, ldep)
                            for tt in range(2):
                                p.inc(t.transpose(
                                    bf.pbk[2][0:C16, tt * 128:(tt + 1) * 128],
                                    wtok[:, tt * C16:(tt + 1) * C16],
                                    bf.id_t[:, :]), "st", 1)
                        blk.tensor(_)
                        wt_tok = p.tok("st")
                        sq.wait(wt_tok)
                        wT_tok = None
                        for tt in range(2):
                            wT_tok = sq.op(lambda sc, tt=tt: sc.copy(
                                wT16[:, tt * 128:(tt + 1) * 128],
                                bf.pbk[2][0:C16, tt * 128:(tt + 1) * 128]))
                        rt_state["wT_tok"] = wT_tok
                    elif c == 3:
                        emit_bcast(0)
                        emit_bcast(1)
                    else:
                        emit_bcast(c - 2)
                        if c == C16 - 1:
                            emit_bcast(14)
                            emit_bcast(15)
                cols_tok = split_toks[-1]
                xn_use[par] = cols_tok
                hfree[0] = cols_tok
                cols_prev[0] = gelu_toks[-2]
                cols_prev[1] = gelu_toks[-1]
                sq.flush()

                if stile + 1 < NF_TILES:
                    def _(sy, t1=(stile + 1) * CH, npar=(stile + 1) % 2,
                          ndep=xn_use[(stile + 1) % 2], rdep=rn_use[0]):
                        p.wait(sy, ndep)
                        p.inc(sy.dma_start(
                            out=xnThb[npar][:, :].rearrange("p (e q) -> p e q",
                                                            e=EC),
                            in_=xnTh_d[sqn, :, :, t1:t1 + CH]), "sdt0", 16)
                        p.inc(sy.dma_start(
                            out=xnTlb[npar][:, :].rearrange("p (e q) -> p e q",
                                                            e=EC),
                            in_=xnTl_d[sqn, :, :, t1:t1 + CH]), "sdt1", 16)
                        p.wait(sy, rdep)
                        p.inc(sy.dma_start(out=rn_fc[0:1, :],
                                           in_=rn_d[sqn, 0:1, t1:t1 + CH]),
                              "sdt0", 16)
                    blk.sync(_)
                    xnp_toks[0] = p.tok("sdt0")
                    xnp_toks[1] = p.tok("sdt1")

                ye_toks = []
                for m in range(EC):
                    def _(sy, m=m):
                        if m + 1 < EC:
                            sem = "sdp0" if (m + 1) % 2 == 0 else "sdp1"
                            if m >= 1:
                                p.wait(sy, ye_toks[m - 1])
                            p.inc(sy.dma_start(out=pjb[(m + 1) % 2][:, :],
                                               in_=pj2_d[m + 1, :, :]), sem, 16)
                    blk.sync(_)
                    if m + 1 < EC:
                        pj_toks[(m + 1) % 2] = p.tok("sdp0" if (m + 1) % 2 == 0 else "sdp1")

                    def _(t, m=m, dep=pj_toks[m % 2], cdep=cols_tok,
                          hdep=hT_tok):
                        if m == 0:
                            p.wait(t, dep, hdep)
                        else:
                            p.wait(t, dep, cdep, p.tok("sa"), p.tok("sv"))
                        bank = bf.pbk[m % 4]
                        pb16 = pjb[m % 2]
                        for k in range(32):
                            if m == 0:
                                p.wait(t, split_toks[k])
                            wh = pb16[:, k * 128:(k + 1) * 128]
                            wlo = pb16[:, 32 * 128 + k * 128:32 * 128 + (k + 1) * 128]
                            xh = colh[:, k * CH:(k + 1) * CH]
                            xl = coll[:, k * CH:(k + 1) * CH]
                            for pi, (ws, xs) in enumerate(PASSES):
                                mm = t.matmul(bank[:, 0:CH],
                                              wh if ws == 0 else wlo,
                                              xh if xs == 0 else xl,
                                              start=(k == 0 and pi == 0),
                                              stop=(k == 31 and pi == 2))
                        p.inc(mm, "st", 1)
                    blk.tensor(_)
                    y_tok = p.tok("st")

                    if m == 0:
                        sq.wait(xt_use[0])
                    sq.wait(y_tok)
                    ye_toks.append(sq.op(lambda sc, m=m: sc.activation(
                        ysT[:, m * CH:(m + 1) * CH], bf.pbk[m % 4][:, 0:CH],
                        AF.Identity, bias=bf.pjb_t[:, m:m + 1])))
                if True:
                    pj_prev[0] = ye_toks[-2]
                    hbank_prev[0] = ye_toks[4]
                    hbank_prev[1] = ye_toks[5]

                def _(v, dep=ye_toks[-1], xnTh=xnTh, xnTl=xnTl):
                    p.wait(v, dep)
                    for e in range(EC):
                        v.tensor_add(ysT[:, e * CH:(e + 1) * CH],
                                     ysT[:, e * CH:(e + 1) * CH],
                                     xnTh[:, e * CH:(e + 1) * CH])
                        v.drain()
                        ins = v.tensor_add(ysT[:, e * CH:(e + 1) * CH],
                                           ysT[:, e * CH:(e + 1) * CH],
                                           xnTl[:, e * CH:(e + 1) * CH])
                        if e < EC - 1:
                            v.drain()
                    p.inc(ins, "sv", 1)
                blk.vector(_)
                yres_tok = p.tok("sv")

                sa_base4 = p.v["sa"]

                def _(t, dep=yres_tok):
                    p.wait(t, dep, p.tok("sa"))
                    for e in range(EC):
                        if e >= 2:
                            p.wait(t, ("sa", sa_base4 + 2 * (e - 1)))
                        for tt in range(2):
                            p.inc(t.transpose(bf.pbk[6 + (e % 2)][:, tt * 128:tt * 128 + 128],
                                              ysT[:, e * CH + tt * 128:e * CH + tt * 128 + 128],
                                              bf.id_t[:, :]), "st", 1)
                blk.tensor(_)
                ytr_base = p.v["st"] - 16

                ytok_tok = None
                sq.wait(sdo_prev)
                for e in range(EC):
                    sq.wait(("st", ytr_base + 2 * (e + 1)))
                    for tt in range(2):
                        ytok_tok = sq.op(lambda sc, e=e, tt=tt: sc.copy(
                            ytok[:, tt * D + e * 128:tt * D + (e + 1) * 128],
                            bf.pbk[6 + (e % 2)][:, tt * 128:tt * 128 + 128]))

                sa_base5 = p.v["sa"]

                def _(t, dep=ytok_tok, sdep=s_tok_in):
                    p.wait(t, dep, sdep)
                    for e in range(EC):
                        if e >= 2:
                            p.wait(t, ("sa", sa_base5 + 2 * (e - 1)))
                        for tt in range(2):
                            p.inc(t.transpose(bf.pbk[6 + (e % 2)][:, tt * 128:tt * 128 + 128],
                                              sTf[:, e * CH + tt * 128:e * CH + tt * 128 + 128],
                                              bf.id_t[:, :]), "st", 1)
                blk.tensor(_)
                str_base = p.v["st"] - 16

                stok_tok = None
                for e in range(EC):
                    sq.wait(("st", str_base + 2 * (e + 1)))
                    for tt in range(2):
                        stok_tok = sq.op(lambda sc, e=e, tt=tt: sc.copy(
                            stok[:, tt * D + e * 128:tt * D + (e + 1) * 128],
                            bf.pbk[6 + (e % 2)][:, tt * 128:tt * 128 + 128]))
                st_use[0] = stok_tok

                lnts = []
                for tt in range(2):
                    lnts.append(_ln_tile(p, blk, sq, ytok[:, tt * D:(tt + 1) * D],
                                         ytok[:, tt * D:(tt + 1) * D], bf,
                                         pre_deps=(ytok_tok, p.tok("sa")),
                                         gamma=lng_t, beta=lnb_t))
                for tt in range(2):
                    gsl = ytok[:, tt * D:(tt + 1) * D]
                    sq.wait(lnts[tt])
                    g_tok = sq.op(lambda sc, tt=tt, gsl=gsl: sc.activation(
                        ysT[:, tt * D:(tt + 1) * D], gsl, AF.Tanh, scale=0.5))

                    def _(v, tt=tt, gsl=gsl, dep=g_tok, sdep=stok_tok, xdep=x_tok):
                        p.wait(v, dep, sdep, xdep)
                        v.tensor_scalar(ysT[:, tt * D:(tt + 1) * D],
                                        ysT[:, tt * D:(tt + 1) * D], 0.5, 0.5,
                                        op0=OP.mult, op1=OP.add)
                        v.drain()
                        v.tensor_mul(gsl, gsl, ysT[:, tt * D:(tt + 1) * D])
                        v.drain()
                        v.tensor_mul(gsl, gsl, stok[:, tt * D:(tt + 1) * D])
                        v.drain()
                        p.inc(v.tensor_add(gsl, gsl, xt[:, tt * D:(tt + 1) * D]),
                              "sv", 1)
                    blk.vector(_)
                out_tok = p.tok("sv")
                xt_use[0] = out_tok
                sq.flush()

                def _(g, t0=t0, dep=out_tok):
                    p.wait(g, dep)
                    p.inc(g.dma_start(out=out_d[sqn, t0:t0 + CH, :].rearrange(
                        "(a p) d -> p a d", p=128),
                        in_=ytok[:, :].rearrange("p (a d) -> p a d", a=2)), "sdo", 16)
                blk.gpsimd(_)

            def _(sy):
                p.wait(sy, p.tok("sdwA"), p.tok("sdc0"), p.tok("sdc1"),
                       p.tok("sdp0"), p.tok("sdp1"), p.tok("sdsi"),
                       p.tok("sdt0"), p.tok("sdt1"), p.tok("sds0"), p.tok("sds1"),
                       p.tok("sdx0"), p.tok("sdx1"))
            blk.sync(_)

            def _(g):
                p.wait(g, p.tok("sdo"))
            blk.gpsimd(_)


def _host_layouts(inputs):
    f = np.ascontiguousarray
    L = {}
    for nm, key in (("wq", "L0_attn_Wq"), ("wk", "L0_attn_Wk"),
                    ("wv", "L0_attn_Wv"), ("wo", "L0_attn_Wo")):
        L[nm] = f(inputs[key].reshape(EC, 128, D).transpose(1, 0, 2).reshape(128, EC * D))
    L["bq"] = f(inputs["L0_attn_bq"].reshape(EC, 128).T)
    L["bk"] = f(inputs["L0_attn_bk"].reshape(EC, 128).T)
    L["bo"] = f(inputs["L0_attn_bo"].reshape(EC, 128).T)
    L["pjb"] = f(inputs["L0_ffn_proj_b"].reshape(EC, 128).T)
    L["bvb"] = f(np.tile(inputs["L0_attn_bv"][None, :], (128, 1)))
    L["lng"] = f(np.tile(inputs["L0_ffn_ln_g"][None, :], (128, 1)))
    L["lnb"] = f(np.tile(inputs["L0_ffn_ln_b"][None, :], (128, 1)))
    L["ce"] = f(inputs["L0_ffn_col_emb"])
    L["gw1"] = f(inputs["L0_ffn_gate_w1"].reshape(EC, 128, HG).transpose(1, 0, 2).reshape(128, EC * HG))
    L["gb1"] = f(inputs["L0_ffn_gate_b1"].reshape(HG // 128, 128).T)
    L["gw2"] = f(inputs["L0_ffn_gate_w2"].reshape(HG // 128, 128, C16).transpose(1, 0, 2).reshape(128, (HG // 128) * C16))
    L["gb2"] = f(inputs["L0_ffn_gate_b2"].reshape(C16, 1))
    L["wc"] = f(inputs["L0_ffn_W_cols"].reshape(C16, EC, 128, NN).transpose(0, 2, 1, 3).reshape(C16, 128, EC * NN))
    L["bc"] = f(inputs["L0_ffn_b_cols"].reshape(C16, 2, 128).transpose(2, 0, 1).reshape(128, C16 * 2))
    L["pj"] = f(inputs["L0_ffn_proj_w"].reshape(32, 128, EC, 128).transpose(2, 1, 0, 3).reshape(EC, 128, 32 * 128))
    L["ident"] = np.eye(128, dtype=np.float32)
    oh = np.zeros((C16, C16 * 128), dtype=np.float32)
    for c in range(C16):
        oh[c, c * 128:(c + 1) * 128] = 1.0
    L["oh"] = oh
    return L


def kernel(**inputs):
    nc = build_core_program()
    L = _host_layouts(inputs)
    x = np.ascontiguousarray(np.asarray(inputs["L0_attn_x"], dtype=np.float32))
    in_maps = []
    for ci in range(NC_):
        m = dict(L)
        m["x"] = np.ascontiguousarray(x[ci * BPC:(ci + 1) * BPC])
        in_maps.append(m)
    res = run_bass_kernel_spmd(nc, in_maps, core_ids=list(range(NC_)))
    return np.concatenate([r["out"] for r in res.results], axis=0)


# version 18 (final)
# speedup vs baseline: 1.0864x; 1.0241x over previous
import numpy as np
from contextlib import ExitStack

import concourse.bass as bass
import concourse.mybir as mybir
from concourse.alu_op_type import AluOpType
from concourse.bass_utils import run_bass_kernel_spmd

F32 = mybir.dt.float32
BF16 = mybir.dt.bfloat16
AX = mybir.AxisListType.X
AF = mybir.ActivationFunctionType
OP = AluOpType

B, S, D = 16, 2048, 1024
NC_ = 8
BPC = B // NC_
CH = 256
NCH = S // CH
EC = D // 128
C16 = 16
NN = 256
HG = 512
LN_EPS = 1e-5

PHASES = ("a", "f")
DEBUG_S = False
NSEQ = BPC
NF_TILES = NCH
NA_CHUNKS = NCH

PASSES = ((0, 0), (0, 1), (1, 0))


class Prog:
    def __init__(self, nc):
        self.nc = nc
        self.h = {}
        self.v = {}

    def sem(self, es, *names):
        for n in names:
            self.h[n] = es.enter_context(self.nc.semaphore(n))
            self.v[n] = 0

    def inc(self, instr, name, n=1):
        instr.then_inc(self.h[name], n)
        self.v[name] += n
        return (name, self.v[name])

    def tok(self, name):
        return (name, self.v[name])

    def wait(self, eng, *deps):
        for d in deps:
            if d is None:
                continue
            name, val = d
            if val > 0:
                eng.wait_ge(self.h[name], val)


class Buffers:
    pass


class ScalarQ:

    def __init__(self, p, blk):
        self.p = p
        self.blk = blk
        self.ops = []

    def wait(self, *deps):
        deps = tuple(d for d in deps if d is not None)
        if deps:
            self.ops.append(("w", deps))

    def op(self, make, sem="sa", n=1):
        tok = None
        if sem is not None:
            self.p.v[sem] += n
            tok = (sem, self.p.v[sem])
        self.ops.append(("i", make, sem, n))
        return tok

    def flush(self):
        if not self.ops:
            return
        ops, self.ops = self.ops, []
        p = self.p

        def body(sc):
            for it in ops:
                if it[0] == "w":
                    p.wait(sc, *it[1])
                else:
                    _, make, sem, n = it
                    ins = make(sc)
                    if sem is not None:
                        ins.then_inc(p.h[sem], n)
        self.blk.scalar(body)


def _ln_tile(p, blk, sq, x_in, xn_out, bf, pre_deps=(), gamma=None, beta=None):
    def _(v):
        p.wait(v, *pre_deps)
        v.bn_stats(bf.stats[:, 0:6], x_in[:, 0:512])
        v.bn_stats(bf.stats[:, 6:12], x_in[:, 512:1024])
        v.drain()
        v.bn_aggr(bf.mv[:, :], bf.stats[:, :])
        v.drain()
        p.inc(v.tensor_scalar_add(bf.tmp1[:, :], bf.mv[:, 1:2], LN_EPS), "sv", 1)
    blk.vector(_)
    var_done = p.tok("sv")

    sq.wait(var_done)
    sq.op(lambda sc: sc.activation(bf.rstd[:, :], bf.tmp1[:, :], AF.Ln), sem=None)
    sq.op(lambda sc: sc.drain(), sem=None)
    rstd_done = sq.op(lambda sc: sc.activation(bf.rstd[:, :], bf.rstd[:, :],
                                               AF.Exp, scale=-0.5))

    def _(v):
        p.wait(v, rstd_done)
        ins = v.tensor_scalar(xn_out, x_in, bf.mv[:, 0:1], bf.rstd[:, :],
                              op0=OP.subtract, op1=OP.mult)
        if gamma is not None:
            v.drain()
            v.tensor_mul(xn_out, xn_out, gamma[:, :])
            v.drain()
            ins = v.tensor_add(xn_out, xn_out, beta[:, :])
        p.inc(ins, "sv", 1)
    blk.vector(_)
    return p.tok("sv")


def _evac_hilo(p, blk, sq, bf, items):
    hi_toks = []
    for stok, ps, hi, lo, bias in items:
        sq.wait(stok)
        if bias is None:
            hi_toks.append(sq.op(lambda sc, ps=ps, hi=hi: sc.copy(hi, ps)))
        else:
            hi_toks.append(sq.op(lambda sc, ps=ps, hi=hi, bias=bias: sc.activation(
                hi, ps, AF.Identity, bias=bias)))

    def _(v):
        for i, (stok, ps, hi, lo, bias) in enumerate(items):
            p.wait(v, hi_toks[i])
            if bias is None:
                ins = v.tensor_sub(lo, ps, hi)
            else:
                ins = v.scalar_tensor_tensor(lo, ps, bias, hi,
                                             op0=OP.add, op1=OP.subtract)
            p.inc(ins, "sv", 1)
            v.drain()
    blk.vector(_)
    return p.tok("sv")


def build_core_program(phases=PHASES, debug_s=DEBUG_S):
    nc = bass.Bass("TRN2", target_bir_lowering=False)
    p = Prog(nc)

    x_d = nc.dram_tensor("x", [BPC, S, D], F32, kind="ExternalInput").ap()
    wq_d = nc.dram_tensor("wq", [128, EC * D], F32, kind="ExternalInput").ap()
    wk_d = nc.dram_tensor("wk", [128, EC * D], F32, kind="ExternalInput").ap()
    wv_d = nc.dram_tensor("wv", [128, EC * D], F32, kind="ExternalInput").ap()
    wo_d = nc.dram_tensor("wo", [128, EC * D], F32, kind="ExternalInput").ap()
    bq_d = nc.dram_tensor("bq", [128, EC], F32, kind="ExternalInput").ap()
    bk_d = nc.dram_tensor("bk", [128, EC], F32, kind="ExternalInput").ap()
    bo_d = nc.dram_tensor("bo", [128, EC], F32, kind="ExternalInput").ap()
    pb_d = nc.dram_tensor("pjb", [128, EC], F32, kind="ExternalInput").ap()
    bvb_d = nc.dram_tensor("bvb", [128, D], F32, kind="ExternalInput").ap()
    lng_d = nc.dram_tensor("lng", [128, D], F32, kind="ExternalInput").ap()
    lnb_d = nc.dram_tensor("lnb", [128, D], F32, kind="ExternalInput").ap()
    ce_d = nc.dram_tensor("ce", [C16, D], F32, kind="ExternalInput").ap()
    gw1_d = nc.dram_tensor("gw1", [128, EC * HG], F32, kind="ExternalInput").ap()
    gb1_d = nc.dram_tensor("gb1", [128, HG // 128], F32, kind="ExternalInput").ap()
    gw2_d = nc.dram_tensor("gw2", [128, (HG // 128) * C16], F32, kind="ExternalInput").ap()
    gb2_d = nc.dram_tensor("gb2", [C16, 1], F32, kind="ExternalInput").ap()
    wc_d = nc.dram_tensor("wc", [C16, 128, EC * NN], F32, kind="ExternalInput").ap()
    bc_d = nc.dram_tensor("bc", [128, C16 * 2], F32, kind="ExternalInput").ap()
    pj_d = nc.dram_tensor("pj", [EC, 128, 32 * 128], F32, kind="ExternalInput").ap()
    id_d = nc.dram_tensor("ident", [128, 128], F32, kind="ExternalInput").ap()
    oh_d = nc.dram_tensor("oh", [C16, C16 * 128], F32, kind="ExternalInput").ap()
    out_d = nc.dram_tensor("out", [BPC, S, D], F32, kind="ExternalOutput").ap()
    skind = {"kind": "ExternalOutput"} if debug_s else {}
    s_d = nc.dram_tensor("s_scr", [BPC, 128, EC, S], F32, **skind).ap()
    xnTh_d = nc.dram_tensor("xnTh_scr", [BPC, 128, EC, S], BF16).ap()
    xnTl_d = nc.dram_tensor("xnTl_scr", [BPC, 128, EC, S], BF16).ap()
    qTh_d = nc.dram_tensor("qTh_scr", [BPC, 128, EC, S], BF16).ap()
    qTl_d = nc.dram_tensor("qTl_scr", [BPC, 128, EC, S], BF16).ap()
    vTh_d = nc.dram_tensor("vTh_scr", [BPC, S // 128, 128, D], BF16).ap()
    vTl_d = nc.dram_tensor("vTl_scr", [BPC, S // 128, 128, D], BF16).ap()
    rn_d = nc.dram_tensor("rn_scr", [BPC, 1, S], F32).ap()
    wk2_d = nc.dram_tensor("wk2_scr", [128, 2 * EC * D], BF16).ap()
    wq2_d = nc.dram_tensor("wq2_scr", [128, 2 * EC * D], BF16).ap()
    wv2_d = nc.dram_tensor("wv2_scr", [128, 2 * EC * D], BF16).ap()
    wo2_d = nc.dram_tensor("wo2_scr", [128, 2 * EC * D], BF16).ap()
    gw12_d = nc.dram_tensor("gw12_scr", [128, 2 * EC * HG], BF16).ap()
    wc2_d = nc.dram_tensor("wc2_scr", [C16, 128, 2 * EC * NN], BF16).ap()
    pj2_d = nc.dram_tensor("pj2_scr", [EC, 128, 2 * 32 * 128], BF16).ap()

    es = ExitStack()
    with es:
        bf = Buffers()
        sb = lambda name, shape, dt=F32: es.enter_context(nc.sbuf_tensor(name, shape, dt))
        bf.bq_t = sb("bq_t", [128, EC])
        bf.bk_t = sb("bk_t", [128, EC])
        bf.bo_t = sb("bo_t", [128, EC])
        bf.pjb_t = sb("pjb_t", [128, EC])
        bf.gb1_t = sb("gb1_t", [128, HG // 128])
        bf.gb2_t = sb("gb2_t", [C16, 1])
        bf.bc_t = sb("bc_t", [128, C16 * 2])
        bf.id_t = sb("id_t", [128, 128])
        bf.ones1 = sb("ones1", [1, C16])
        bf.cnTh = sb("cnTh", [128, EC * C16], BF16)
        bf.cnTl = sb("cnTl", [128, EC * C16], BF16)
        bf.stats = sb("stats", [128, 12])
        bf.mv = sb("mv", [128, 2])
        bf.tmp1 = sb("tmp1", [128, 1])
        bf.rstd = sb("rstd", [128, 1])
        bf.nmx = sb("nmx", [128, 1])
        bf.mxv = sb("mxv", [128, 1])
        bf.sme = sb("sme", [128, 1])
        bf.rcp = sb("rcp", [128, 1])
        bf.rcp2 = sb("rcp2", [128, 1])
        bf.ss_t = sb("ss_t", [128, 1])
        bf.pbk = [es.enter_context(nc.psum_tensor(f"pb{i}", [128, 512], F32))
                  for i in range(8)]

        p.sem(es, "sdk", "sdx0", "sdx1", "sdwA", "sdwB", "sdc0", "sdc1",
              "sdp0", "sdp1", "sdsi", "sdt0", "sdt1", "sds0", "sds1",
              "st", "sa", "sv", "sgc", "sdso", "sdo", "sdsq", "sdsv", "sdsr")

        with nc.Block() as blk:
            sq = ScalarQ(p, blk)
            pro_es = ExitStack()
            psb = lambda name, shape, dt=F32: pro_es.enter_context(
                nc.sbuf_tensor(name, shape, dt))
            ce_t = psb("ce_t", [C16, D])
            cesc = psb("cesc", [C16, D])
            e16a = psb("e16a", [C16, 1])
            e16b = psb("e16b", [C16, 1])
            cnT32 = psb("cnT32", [128, EC * C16])
            w32 = [psb(f"prep32_{i}", [128, EC * D]) for i in range(2)]
            w16 = [psb(f"prep16_{i}", [128, 2 * EC * D], BF16) for i in range(2)]

            def _(sy):
                for dst, src in ((bf.bq_t, bq_d), (bf.bk_t, bk_d), (bf.bo_t, bo_d),
                                 (bf.pjb_t, pb_d), (bf.gb1_t, gb1_d),
                                 (bf.gb2_t, gb2_d), (bf.bc_t, bc_d),
                                 (bf.id_t, id_d), (ce_t, ce_d)):
                    p.inc(sy.dma_start(out=dst[:, :], in_=src), "sdk", 16)
            blk.sync(_)
            kload = p.tok("sdk")

            def _(g):
                p.inc(g.memset(bf.ones1[:, :], 1.0), "sgc", 1)
            blk.gpsimd(_)

            units = [("wk", wk_d, wk2_d, EC * D, None),
                     ("wq", wq_d, wq2_d, EC * D, 1.0 / 32.0),
                     ("wv", wv_d, wv2_d, EC * D, None),
                     ("wo", wo_d, wo2_d, EC * D, None),
                     ("gw1", gw1_d, gw12_d, EC * HG, None)]

            done_toks = [None, None]
            in_sems = ("sdwB", "sdsi")
            out_sems = ("sdso", "sdo")
            for ui, (nm, src, dst, width, scale) in enumerate(units):
                slot = ui % 2

                def _(sy, src=src, width=width, slot=slot, dep=done_toks[slot]):
                    p.wait(sy, dep)
                    p.inc(sy.dma_start(out=w32[slot][:, 0:width], in_=src),
                          in_sems[slot], 16)
                blk.sync(_)
                in_tok = p.tok(in_sems[slot])

                if scale is not None:
                    def _(v, slot=slot, width=width, dep=in_tok, scale=scale):
                        p.wait(v, dep)
                        p.inc(v.tensor_scalar_mul(w32[slot][:, 0:width],
                                                  w32[slot][:, 0:width], scale),
                              "sv", 1)
                    blk.vector(_)
                    in_tok = p.tok("sv")

                sq.wait(in_tok)
                hi_tok = sq.op(lambda sc, slot=slot, width=width: sc.copy(
                    w16[slot][:, 0:width], w32[slot][:, 0:width]))

                def _(v, slot=slot, width=width, dep=hi_tok):
                    p.wait(v, dep)
                    p.inc(v.tensor_sub(w16[slot][:, width:2 * width],
                                       w32[slot][:, 0:width],
                                       w16[slot][:, 0:width]), "sv", 1)
                blk.vector(_)
                lo_tok = p.tok("sv")

                def _(g, dst=dst, width=width, slot=slot, dep=lo_tok):
                    p.wait(g, dep)
                    p.inc(g.dma_start(out=dst, in_=w16[slot][:, 0:2 * width]),
                          out_sems[slot], 16)
                blk.gpsimd(_)
                done_toks[slot] = p.tok(out_sems[slot])

            sq.wait(kload)
            sq_done = sq.op(lambda sc: sc.activation(cesc[:, :], ce_t[:, :],
                                                     AF.Square, accum_out=e16a[:, :]))
            sq.flush()

            def _(v):
                p.wait(v, sq_done)
                p.inc(v.tensor_scalar_max(e16a[:, :], e16a[:, :], 1e-24), "sv", 1)
            blk.vector(_)
            mx_done = p.tok("sv")

            sq.wait(mx_done)
            sq.op(lambda sc: sc.activation(e16b[:, :], e16a[:, :], AF.Ln), sem=None)
            sq.op(lambda sc: sc.drain(), sem=None)
            nrm_done = sq.op(lambda sc: sc.activation(e16a[:, :], e16b[:, :],
                                                      AF.Exp, scale=-0.5))
            sq.flush()

            def _(v):
                p.wait(v, nrm_done)
                v.tensor_scalar(cesc[:, :], ce_t[:, :], e16a[:, :], None, op0=OP.mult)
                v.drain()
                v.tensor_scalar_mul(bf.bq_t[:, :], bf.bq_t[:, :], 1.0 / 32.0)
                v.drain()
                p.inc(v.tensor_scalar_mul(bf.gb2_t[:, :], bf.gb2_t[:, :], 0.5), "sv", 1)
            blk.vector(_)
            cn_done = p.tok("sv")

            sa_base = p.v["sa"]

            def _(t):
                p.wait(t, cn_done)
                for e in range(EC):
                    if e >= 2:
                        p.wait(t, ("sa", sa_base + e - 1))
                    p.inc(t.transpose(bf.pbk[e % 2][0:128, 0:C16],
                                      cesc[:, e * 128:(e + 1) * 128],
                                      bf.id_t[0:C16, 0:C16]), "st", 1)
            blk.tensor(_)
            st_base = p.v["st"] - EC

            for e in range(EC):
                sq.wait(("st", st_base + e + 1))
                sq.op(lambda sc, e=e: sc.copy(cnT32[:, e * C16:(e + 1) * C16],
                                              bf.pbk[e % 2][0:128, 0:C16]))
            sq.op(lambda sc: sc.drain(), sem=None)
            cnh_tok = sq.op(lambda sc: sc.copy(bf.cnTh[:, :], cnT32[:, :]))
            sq.flush()

            def _(v):
                p.wait(v, cnh_tok)
                p.inc(v.tensor_sub(bf.cnTl[:, :], cnT32[:, :], bf.cnTh[:, :]), "sv", 1)
            blk.vector(_)

            def _(sy):
                p.wait(sy, p.tok("sdwB"), p.tok("sdsi"), p.tok("sdk"))
            blk.sync(_)

            def _(g):
                p.wait(g, p.tok("sdso"), p.tok("sdo"))
            blk.gpsimd(_)
        pro_es.close()

        for sqn in range(NSEQ):
            if "a" in phases:
                _phase_a(nc, p, es, sqn, bf, x_d, s_d, bvb_d,
                         wk2_d, wq2_d, wv2_d, wo2_d,
                         xnTh_d, xnTl_d, qTh_d, qTl_d, vTh_d, vTl_d, rn_d,
                         wc_d, wc2_d, pj_d, pj2_d)
            if "f" in phases:
                _phase_f(nc, p, es, sqn, bf, x_d, s_d, out_d, gw12_d, gw2_d,
                         wc2_d, pj2_d, lng_d, lnb_d, oh_d,
                         xnTh_d, xnTl_d, rn_d)

        with nc.Block() as blk:
            def _(sy):
                for s in ("sdk", "sdx0", "sdx1", "sdwA", "sdwB", "sdc0", "sdc1",
                          "sdp0", "sdp1", "sdsi", "sdt0", "sdt1", "sds0", "sds1"):
                    p.wait(sy, p.tok(s))
            blk.sync(_)

            def _(g):
                p.wait(g, p.tok("sdso"), p.tok("sdo"))
            blk.gpsimd(_)
    return nc


def _phase_a(nc, p, es, sqn, bf, x_d, s_d, bvb_d, wk2_d, wq2_d, wv2_d, wo2_d,
             xnTh_d, xnTl_d, qTh_d, qTl_d, vTh_d, vTl_d, rn_d,
             wc_d, wc2_d, pj_d, pj2_d):
    with ExitStack() as pes:
        sbX = lambda name, shape, dt=F32: pes.enter_context(
            nc.sbuf_tensor(name, shape, dt))
        KTh = [sbX(f"KTh{e}_s{sqn}", [128, S], BF16) for e in range(EC)]
        KTl = [sbX(f"KTl{e}_s{sqn}", [128, S], BF16) for e in range(EC)]

        with ExitStack() as a1es:
            sb1 = lambda name, shape, dt=F32: a1es.enter_context(
                nc.sbuf_tensor(name, shape, dt))
            wk16 = sb1(f"wk16_{sqn}", [128, 2 * EC * D], BF16)
            wq16 = sb1(f"wq16_{sqn}", [128, 2 * EC * D], BF16)
            wv16 = sb1(f"wv16_{sqn}", [128, 2 * EC * D], BF16)
            bvb_t = sb1(f"bvb_s{sqn}", [128, D])
            xa = sb1(f"xa_s{sqn}", [128, 2 * D])
            xnTh = sb1(f"xnTh_s{sqn}", [128, EC * CH], BF16)
            xnTl = sb1(f"xnTl_s{sqn}", [128, EC * CH], BF16)
            QTh = sb1(f"QTh_s{sqn}", [128, EC * CH], BF16)
            QTl = sb1(f"QTl_s{sqn}", [128, EC * CH], BF16)
            VTh = sb1(f"VTh_s{sqn}", [128, 2 * D], BF16)
            VTl = sb1(f"VTl_s{sqn}", [128, 2 * D], BF16)
            V32 = sb1(f"V32_s{sqn}", [128, 2 * D])
            rn_row = sb1(f"rn_row_s{sqn}", [1, CH])
            rn_park = sb1(f"rn_park_s{sqn}", [128, 4])

            def wof(w16buf, e, m, base):
                return (w16buf[:, e * D + m * 128:e * D + (m + 1) * 128],
                        w16buf[:, base + e * D + m * 128:base + e * D + (m + 1) * 128])

            with nc.Block() as blk:
                sq = ScalarQ(p, blk)

                def _(sy):
                    p.inc(sy.dma_start(out=wk16[:, :], in_=wk2_d), "sdwA", 16)
                    p.inc(sy.dma_start(out=wq16[:, :], in_=wq2_d), "sdwB", 16)
                    p.inc(sy.dma_start(out=wv16[:, :], in_=wv2_d), "sdsi", 16)
                    p.inc(sy.dma_start(out=bvb_t[:, :], in_=bvb_d), "sdsi", 16)
                blk.sync(_)
                wk_tok = p.tok("sdwA")
                wq_tok = p.tok("sdwB")
                wv_tok = p.tok("sdsi")

                lnst = {}

                def emit_ln(c, xfree):
                    t0c = c * CH
                    sx = "sdx0" if c % 2 == 0 else "sdx1"
                    rb = (c % 2) * 2

                    def _(sy, t0=t0c, sx=sx, fdep=xfree):
                        p.wait(sy, fdep)
                        p.inc(sy.dma_start(
                            out=xa[:, :].rearrange("p (a d) -> p a d", a=2),
                            in_=x_d[sqn, t0:t0 + CH, :].rearrange(
                                "(a p) d -> p a d", p=128)), sx, 16)
                    blk.sync(_)
                    x_tok = p.tok(sx)
                    lnst["x_tok"] = x_tok

                    ln_toks = []
                    rn_parked = []
                    for tt in range(2):
                        ln_toks.append(_ln_tile(p, blk, sq,
                                                xa[:, tt * D:(tt + 1) * D],
                                                xa[:, tt * D:(tt + 1) * D], bf,
                                                pre_deps=(x_tok, p.tok("sa"))))
                        def _(v, dep=ln_toks[-1]):
                            p.wait(v, dep)
                            v.tensor_mul(bf.rcp2[:, :], bf.mv[:, 1:2], bf.rstd[:, :])
                            v.drain()
                            v.tensor_mul(bf.rcp2[:, :], bf.rcp2[:, :], bf.rstd[:, :])
                            v.drain()
                            p.inc(v.tensor_scalar(bf.rcp2[:, :], bf.rcp2[:, :],
                                                  float(D), 1e-24,
                                                  op0=OP.mult, op1=OP.max), "sv", 1)
                        blk.vector(_)
                        ss_tok = p.tok("sv")

                        sq.wait(ss_tok)
                        sq.op(lambda sc: sc.activation(bf.rcp2[:, :], bf.rcp2[:, :],
                                                       AF.Ln), sem=None)
                        sq.op(lambda sc: sc.drain(), sem=None)
                        rcp_tok = sq.op(lambda sc: sc.activation(
                            bf.rcp2[:, :], bf.rcp2[:, :], AF.Exp, scale=-0.5))

                        def _(v, dep=rcp_tok, tt=tt, rb=rb):
                            p.wait(v, dep)
                            p.inc(v.tensor_copy(rn_park[:, rb + tt:rb + tt + 1],
                                                bf.rcp2[:, :]), "sv", 1)
                        blk.vector(_)
                        rn_parked.append(p.tok("sv"))
                    lnst["ln"] = ln_toks[-1]
                    lnst["rnp"] = rn_parked[-1]

                emit_ln(0, None)
                spp = {"x": None, "q": None, "v": None, "r": None}
                spn = {}
                for n in range(NA_CHUNKS):
                    t0 = n * CH
                    ln_dep = lnst["ln"]
                    rnp_dep = lnst["rnp"]
                    rn_done = None

                    sv_tb = p.v["sv"]

                    def _(t, dep=ln_dep):
                        p.wait(t, dep, p.tok("sv"))
                        for e in range(EC):
                            if e >= 2:
                                p.wait(t, ("sv", sv_tb + 2 * (e - 1)))
                            for tt in range(2):
                                p.inc(t.transpose(
                                    bf.pbk[6 + (e % 2)][:, tt * 128:tt * 128 + 128],
                                    xa[:, tt * D + e * 128:tt * D + (e + 1) * 128],
                                    bf.id_t[:, :]), "st", 1)
                    blk.tensor(_)
                    tr_base = p.v["st"] - 16

                    sq.wait(spp["x"])
                    xl_tok = _evac_hilo(p, blk, sq, bf,
                        [(("st", tr_base + 2 * e + tt + 1),
                          bf.pbk[6 + (e % 2)][:, tt * 128:tt * 128 + 128],
                          xnTh[:, e * CH + tt * 128:e * CH + tt * 128 + 128],
                          xnTl[:, e * CH + tt * 128:e * CH + tt * 128 + 128],
                          None)
                         for e in range(EC) for tt in range(2)])

                    def _(g, t0=t0, dep=xl_tok):
                        p.wait(g, dep)
                        for dst, srcb in ((xnTh_d, xnTh), (xnTl_d, xnTl)):
                            p.inc(g.dma_start(
                                out=dst[sqn, :, :, t0:t0 + CH],
                                in_=srcb[:, :].rearrange("p (e q) -> p e q", e=EC)),
                                "sdso", 16)
                    blk.gpsimd(_)
                    spn["x"] = p.tok("sdso")

                    sv_kb = p.v["sv"]

                    def _(t, dep=xl_tok):
                        p.wait(t, dep, wk_tok)
                        for m in range(EC):
                            if m >= 2:
                                p.wait(t, ("sv", sv_kb + m - 1))
                            bank = bf.pbk[m % 2]
                            for e in range(EC):
                                wh, wl = wof(wk16, e, m, EC * D)
                                xh = xnTh[:, e * CH:(e + 1) * CH]
                                xl = xnTl[:, e * CH:(e + 1) * CH]
                                for pi, (ws, xs) in enumerate(PASSES):
                                    mm = t.matmul(bank[:, 0:CH],
                                                  wh if ws == 0 else wl,
                                                  xh if xs == 0 else xl,
                                                  start=(e == 0 and pi == 0),
                                                  stop=(e == EC - 1 and pi == 2))
                            p.inc(mm, "st", 1)
                    blk.tensor(_)
                    k_base = p.v["st"] - EC

                    _evac_hilo(p, blk, sq, bf,
                               [(("st", k_base + m + 1), bf.pbk[m % 2][:, 0:CH],
                                 KTh[m][:, t0:t0 + CH], KTl[m][:, t0:t0 + CH],
                                 bf.bk_t[:, m:m + 1]) for m in range(EC)])

                    sq.wait(spp["q"])
                    sv_qb = p.v["sv"]

                    def _(t, dep=xl_tok):
                        p.wait(t, dep, wq_tok, ("sv", sv_qb))
                        for m in range(EC):
                            if m >= 2:
                                p.wait(t, ("sv", sv_qb + m - 1))
                            bank = bf.pbk[2 + (m % 2)]
                            for e in range(EC):
                                wh, wl = wof(wq16, e, m, EC * D)
                                xh = xnTh[:, e * CH:(e + 1) * CH]
                                xl = xnTl[:, e * CH:(e + 1) * CH]
                                for pi, (ws, xs) in enumerate(PASSES):
                                    mm = t.matmul(bank[:, 0:CH],
                                                  wh if ws == 0 else wl,
                                                  xh if xs == 0 else xl,
                                                  start=(e == 0 and pi == 0),
                                                  stop=(e == EC - 1 and pi == 2))
                            p.inc(mm, "st", 1)
                    blk.tensor(_)
                    q_base = p.v["st"] - EC

                    qe_tok = _evac_hilo(p, blk, sq, bf,
                        [(("st", q_base + m + 1), bf.pbk[2 + (m % 2)][:, 0:CH],
                          QTh[:, m * CH:(m + 1) * CH], QTl[:, m * CH:(m + 1) * CH],
                          bf.bq_t[:, m:m + 1]) for m in range(EC)])

                    def _(g, t0=t0, dep=qe_tok):
                        p.wait(g, dep)
                        for dst, srcb in ((qTh_d, QTh), (qTl_d, QTl)):
                            p.inc(g.dma_start(
                                out=dst[sqn, :, :, t0:t0 + CH],
                                in_=srcb[:, :].rearrange("p (e q) -> p e q", e=EC)),
                                "sdsq", 16)
                    blk.gpsimd(_)
                    spn["q"] = p.tok("sdsq")

                    v_guard = p.tok("sv")
                    if n + 1 < NA_CHUNKS:
                        emit_ln(n + 1, ("st", tr_base + 16))

                    sv_vb = p.v["sv"]

                    def _(t, dep=xl_tok):
                        p.wait(t, dep, wv_tok, v_guard)
                        idx = 0
                        for tt in range(2):
                            for hf in range(2):
                                idx += 1
                                if idx >= 3:
                                    p.wait(t, ("sv", sv_vb + idx - 2))
                                bank = bf.pbk[4 + hf]
                                for e in range(EC):
                                    xh = xnTh[:, e * CH + tt * 128:e * CH + (tt + 1) * 128]
                                    xl = xnTl[:, e * CH + tt * 128:e * CH + (tt + 1) * 128]
                                    whh = wv16[:, e * D + hf * 512:e * D + (hf + 1) * 512]
                                    wll = wv16[:, EC * D + e * D + hf * 512:
                                               EC * D + e * D + (hf + 1) * 512]
                                    for pi, (ws, xs) in enumerate(PASSES):
                                        mm = t.matmul(bank[:, 0:512],
                                                      xh if xs == 0 else xl,
                                                      whh if ws == 0 else wll,
                                                      start=(e == 0 and pi == 0),
                                                      stop=(e == EC - 1 and pi == 2))
                                p.inc(mm, "st", 1)
                    blk.tensor(_)
                    v_base = p.v["st"] - 4

                    v32_toks = []

                    def _(v, v_base=v_base, sp=spp["v"]):
                        p.wait(v, sp)
                        for idx in range(4):
                            tt, hf = divmod(idx, 2)
                            sl = slice(tt * D + hf * 512, tt * D + (hf + 1) * 512)
                            p.wait(v, ("st", v_base + idx + 1))
                            p.inc(v.tensor_add(V32[:, sl], bf.pbk[4 + hf][:, 0:512],
                                               bvb_t[:, hf * 512:(hf + 1) * 512]),
                                  "sv", 1)
                            v.drain()
                    blk.vector(_)
                    v32_base = p.v["sv"] - 4
                    for idx in range(4):
                        v32_toks.append(("sv", v32_base + idx + 1))

                    vh_toks = []
                    for idx in range(4):
                        tt, hf = divmod(idx, 2)
                        sl = slice(tt * D + hf * 512, tt * D + (hf + 1) * 512)
                        sq.wait(v32_toks[idx])
                        vh_toks.append(sq.op(lambda sc, sl=sl: sc.copy(
                            VTh[:, sl], V32[:, sl])))

                    def _(v):
                        for idx in range(4):
                            tt, hf = divmod(idx, 2)
                            sl = slice(tt * D + hf * 512, tt * D + (hf + 1) * 512)
                            p.wait(v, vh_toks[idx])
                            ins = v.tensor_sub(VTl[:, sl], V32[:, sl], VTh[:, sl])
                            v.drain()
                        p.inc(ins, "sv", 1)
                    blk.vector(_)
                    ve_tok = p.tok("sv")

                    def _(g, n=n, dep=ve_tok):
                        p.wait(g, dep)
                        for dst, srcb in ((vTh_d, VTh), (vTl_d, VTl)):
                            p.inc(g.dma_start(
                                out=dst[sqn, 2 * n:2 * n + 2, :, :].rearrange(
                                    "a p d -> p a d"),
                                in_=srcb[:, :].rearrange("p (a d) -> p a d", a=2)),
                                "sdsv", 16)
                    blk.gpsimd(_)
                    spn["v"] = p.tok("sdsv")

                    def _(t, dep=rnp_dep, rb=(n % 2) * 2):
                        p.wait(t, dep, p.tok("sv"))
                        for tt in range(2):
                            p.inc(t.transpose(bf.pbk[2 + tt][0:1, 0:128],
                                              rn_park[:, rb + tt:rb + tt + 1],
                                              bf.id_t[:, :]),
                                  "st", 1)
                    blk.tensor(_)
                    rt_base = p.v["st"] - 2

                    sq.wait(("st", rt_base + 2), spp["r"])
                    for tt in range(2):
                        rn_done = sq.op(lambda sc, tt=tt: sc.copy(
                            rn_row[0:1, tt * 128:(tt + 1) * 128],
                            bf.pbk[2 + tt][0:1, 0:128]))
                    sq.flush()

                    def _(g, t0=t0, dep=rn_done):
                        p.wait(g, dep)
                        p.inc(g.dma_start(out=rn_d[sqn, 0:1, t0:t0 + CH],
                                          in_=rn_row[0:1, :]), "sdsr", 16)
                    blk.gpsimd(_)
                    spn["r"] = p.tok("sdsr")
                    spp = dict(spn)

                def _(g):
                    p.wait(g, p.tok("sdso"), p.tok("sdsq"), p.tok("sdsv"),
                           p.tok("sdsr"))
                blk.gpsimd(_)

                def _(sy):
                    p.wait(sy, p.tok("sdwA"), p.tok("sdwB"), p.tok("sdsi"),
                           p.tok("sdx0"), p.tok("sdx1"))
                blk.sync(_)

        with ExitStack() as a2es:
            sb2 = lambda name, shape, dt=F32: a2es.enter_context(
                nc.sbuf_tensor(name, shape, dt))
            wo16 = sb2(f"wo16_{sqn}", [128, 2 * EC * D], BF16)
            QThb = [sb2(f"aQTh{i}_s{sqn}", [128, EC * CH], BF16) for i in range(2)]
            QTlb = [sb2(f"aQTl{i}_s{sqn}", [128, EC * CH], BF16) for i in range(2)]
            VThb = [sb2(f"aVTh{i}_s{sqn}", [128, 2 * D], BF16) for i in range(2)]
            VTlb = [sb2(f"aVTl{i}_s{sqn}", [128, 2 * D], BF16) for i in range(2)]
            ET = sb2(f"ET_s{sqn}", [128, 2 * CH])
            ASb = [sb2(f"AS{i}_s{sqn}", [128, 2 * CH]) for i in range(2)]
            ATTh = sb2(f"ATTh_s{sqn}", [128, 2 * CH], BF16)
            ATTl = sb2(f"ATTl_s{sqn}", [128, 2 * CH], BF16)
            OTh = sb2(f"OTh_s{sqn}", [128, EC * CH], BF16)
            OTl = sb2(f"OTl_s{sqn}", [128, EC * CH], BF16)
            sT = sb2(f"sT_s{sqn}", [128, EC * CH])
            nmx8 = sb2(f"nmx8_s{sqn}", [128, NCH])
            sme8 = sb2(f"sme8_s{sqn}", [128, NCH])
            if sqn == 0:
                pq32 = [sb2(f"pq32_{i}", [128, 2048]) for i in range(2)]
                pq16 = [sb2(f"pq16_{i}", [128, 4096], BF16) for i in range(2)]
                pieces = []
                for c in range(C16):
                    pieces.append((wc_d[c, :, :],
                                   wc2_d[c, :, 0:2048], wc2_d[c, :, 2048:4096]))
                for m in range(EC):
                    for h in range(2):
                        pieces.append((pj_d[m, :, h * 2048:(h + 1) * 2048],
                                       pj2_d[m, :, h * 2048:(h + 1) * 2048],
                                       pj2_d[m, :, 4096 + h * 2048:
                                             4096 + (h + 1) * 2048]))

            with nc.Block() as blk:
                sq = ScalarQ(p, blk)
                pp_load = [None] * 32
                pp_lo = [None] * 32
                pp_sp = [None] * 32

                def prep_load(ks):
                    if sqn != 0:
                        return
                    for k in ks:
                        if k >= len(pieces):
                            continue
                        sl = k % 2

                        def _(sy, k=k, sl=sl,
                              dep=(pp_lo[k - 2] if k >= 2 else None)):
                            p.wait(sy, dep)
                            p.inc(sy.dma_start(out=pq32[sl][:, :],
                                               in_=pieces[k][0]), "sdsi", 16)
                        blk.sync(_)
                    tok = p.tok("sdsi")
                    for k in ks:
                        if k < len(pieces):
                            pp_load[k] = tok

                def prep_stage(ks):
                    if sqn != 0:
                        return
                    for k in ks:
                        if k >= len(pieces):
                            continue
                        sl = k % 2
                        sq.wait(pp_load[k])
                        if k >= 2:
                            sq.wait(pp_sp[k - 2])
                        hi_tok = sq.op(lambda sc, sl=sl: sc.copy(
                            pq16[sl][:, 0:2048], pq32[sl][:, :]))

                        def _(v, sl=sl, dep=hi_tok):
                            p.wait(v, dep)
                            p.inc(v.tensor_sub(pq16[sl][:, 2048:4096],
                                               pq32[sl][:, :],
                                               pq16[sl][:, 0:2048]), "sv", 1)
                        blk.vector(_)
                        pp_lo[k] = p.tok("sv")

                        def _(g, k=k, sl=sl, dep=pp_lo[k]):
                            p.wait(g, dep)
                            p.inc(g.dma_start(out=pieces[k][1],
                                              in_=pq16[sl][:, 0:2048]), "sdo", 16)
                            p.inc(g.dma_start(out=pieces[k][2],
                                              in_=pq16[sl][:, 2048:4096]), "sdo", 16)
                        blk.gpsimd(_)
                    tok = p.tok("sdo")
                    for k in ks:
                        if k < len(pieces):
                            pp_sp[k] = tok

                def _(sy):
                    p.inc(sy.dma_start(out=wo16[:, :], in_=wo2_d), "sdwB", 16)
                    for dst, src in ((QThb[0], qTh_d), (QTlb[0], qTl_d)):
                        p.inc(sy.dma_start(
                            out=dst[:, :].rearrange("p (e q) -> p e q", e=EC),
                            in_=src[sqn, :, :, 0:CH]), "sdc0", 16)
                    for dst, src in ((VThb[0], vTh_d), (VTlb[0], vTl_d)):
                        p.inc(sy.dma_start(
                            out=dst[:, :].rearrange("p (a d) -> p a d", a=2),
                            in_=src[sqn, 0:2, :, :].rearrange("a p d -> p a d")),
                            "sdp0", 16)
                blk.sync(_)
                wo_tok = p.tok("sdwB")
                q_toks = [p.tok("sdc0"), None]
                v_toks = [p.tok("sdp0"), None]
                prep_load((0, 1))
                q_use = [None, None]
                atg_hist = [None, None]
                as_prev = None
                prev = None

                def emit_atgroup(pv):
                    def _(t, dep=pv["as_tok"], ASp=pv["AS"]):
                        p.wait(t, dep, p.tok("sv"))
                        for qt in range(2):
                            for kt in range(2):
                                p.inc(t.transpose(
                                    bf.pbk[6 + qt][:, kt * 128:(kt + 1) * 128],
                                    ASp[:, qt * CH + kt * 128:qt * CH + (kt + 1) * 128],
                                    bf.id_t[:, :]), "st", 1)
                    blk.tensor(_)
                    atg_tok = p.tok("st")
                    atg_hist[pv["n"] % 2] = atg_tok
                    pv["att_tok"] = _evac_hilo(p, blk, sq, bf,
                        [(atg_tok, bf.pbk[6 + qt][:, kt * 128:(kt + 1) * 128],
                          ATTh[:, kt * CH + qt * 128:kt * CH + qt * 128 + 128],
                          ATTl[:, kt * CH + qt * 128:kt * CH + qt * 128 + 128],
                          None) for qt in range(2) for kt in range(2)])

                def emit_av(pv):
                    npv = pv["n"]
                    parp = npv % 2
                    VThp, VTlp = VThb[parp], VTlb[parp]
                    sv_av = p.v["sv"]

                    def _(t, dep=pv["att_tok"], vdep=v_toks[parp],
                          VTh=VThp, VTl=VTlp):
                        p.wait(t, dep, vdep)
                        for e in range(EC):
                            if e >= 2:
                                p.wait(t, ("sv", sv_av + e - 1))
                            bank = bf.pbk[6 + (e % 2)]
                            first = True
                            for kt in range(2):
                                vh = VTh[:, kt * D + e * 128:kt * D + (e + 1) * 128]
                                vl = VTl[:, kt * D + e * 128:kt * D + (e + 1) * 128]
                                ah = ATTh[:, kt * CH:(kt + 1) * CH]
                                al = ATTl[:, kt * CH:(kt + 1) * CH]
                                for ws, xs in PASSES:
                                    mm = t.matmul(bank[:, 0:CH],
                                                  vh if ws == 0 else vl,
                                                  ah if xs == 0 else al,
                                                  start=first,
                                                  stop=(kt == 1 and (ws, xs) == PASSES[-1]))
                                    first = False
                            p.inc(mm, "st", 1)
                    blk.tensor(_)
                    av_base = p.v["st"] - EC
                    av_tok = p.tok("st")

                    ot_tok = _evac_hilo(p, blk, sq, bf,
                        [(("st", av_base + e + 1), bf.pbk[6 + (e % 2)][:, 0:CH],
                          OTh[:, e * CH:(e + 1) * CH], OTl[:, e * CH:(e + 1) * CH],
                          None) for e in range(EC)])
                    return av_tok, ot_tok

                def emit_wo(pv, ot_tok):
                    npv = pv["n"]
                    t0p = npv * CH
                    sa_wo = p.v["sa"]
                    sdso_prev = p.tok("sdso")

                    def _(t, dep=ot_tok, wdep=wo_tok):
                        p.wait(t, dep, wdep)
                        for m in range(EC):
                            if m >= 2:
                                p.wait(t, ("sa", sa_wo + m - 1))
                            bank = bf.pbk[4 + (m % 2)]
                            for e in range(EC):
                                wh = wo16[:, e * D + m * 128:e * D + (m + 1) * 128]
                                wl = wo16[:, EC * D + e * D + m * 128:
                                          EC * D + e * D + (m + 1) * 128]
                                oh_ = OTh[:, e * CH:(e + 1) * CH]
                                ol_ = OTl[:, e * CH:(e + 1) * CH]
                                for pi, (ws, xs) in enumerate(PASSES):
                                    mm = t.matmul(bank[:, 0:CH],
                                                  wh if ws == 0 else wl,
                                                  oh_ if xs == 0 else ol_,
                                                  start=(e == 0 and pi == 0),
                                                  stop=(e == EC - 1 and pi == 2))
                            p.inc(mm, "st", 1)
                    blk.tensor(_)
                    wo_base = p.v["st"] - EC
                    wo_done = p.tok("st")

                    se_tok = None
                    sq.wait(sdso_prev)
                    for m in range(EC):
                        sq.wait(("st", wo_base + m + 1))
                        se_tok = sq.op(lambda sc, m=m: sc.activation(
                            sT[:, m * CH:(m + 1) * CH], bf.pbk[4 + (m % 2)][:, 0:CH],
                            AF.Identity, bias=bf.bo_t[:, m:m + 1]))
                    sq.flush()

                    def _(g, t0=t0p, dep=se_tok):
                        p.wait(g, dep)
                        p.inc(g.dma_start(out=s_d[sqn, :, :, t0:t0 + CH],
                                          in_=sT[:, :].rearrange("p (e q) -> p e q",
                                                                 e=EC)), "sdso", 16)
                    blk.gpsimd(_)
                    return wo_done

                for n in range(NA_CHUNKS):
                    t0 = n * CH
                    par = n % 2
                    QTh, QTl = QThb[par], QTlb[par]
                    AS = ASb[par]

                    prep_stage((4 * n, 4 * n + 1))
                    prep_load((4 * n + 2, 4 * n + 3))

                    def _(t, qdep=q_toks[par], adep=as_prev, QTh=QTh, QTl=QTl):
                        p.wait(t, qdep, adep)
                        for mp in range(4):
                            bank = bf.pbk[mp]
                            for e in range(EC):
                                qh = QTh[:, e * CH:e * CH + 128]
                                ql = QTl[:, e * CH:e * CH + 128]
                                kh = KTh[e][:, mp * 512:(mp + 1) * 512]
                                kl = KTl[e][:, mp * 512:(mp + 1) * 512]
                                for pi, (ws, xs) in enumerate(PASSES):
                                    mm = t.matmul(bank[:, 0:512],
                                                  qh if ws == 0 else ql,
                                                  kh if xs == 0 else kl,
                                                  start=(e == 0 and pi == 0),
                                                  stop=(e == EC - 1 and pi == 2))
                            p.inc(mm, "st", 1)
                    blk.tensor(_)
                    sc_base0 = p.v["st"] - 4

                    exq = [[], []]
                    as_toks = [None, None]

                    def emit_reduce(qt, sc_base):
                        def _(v, sc_base=sc_base):
                            for m in range(NCH):
                                mp, half = divmod(m, 2)
                                p.wait(v, ("st", sc_base + mp + 1))
                                p.inc(v.reduce_max(
                                    nmx8[:, m:m + 1],
                                    bf.pbk[mp][:, half * CH:(half + 1) * CH],
                                    axis=AX, negate=True), "sv", 1)
                        blk.vector(_)
                        return p.v["sv"] - NCH

                    def emit_expacc(qt, nm_base, n=n, AS=AS):
                        acc0_base = p.v["sv"]
                        for m in range(NCH):
                            mp, half = divmod(m, 2)
                            sq.wait(("sv", nm_base + m + 1))
                            if m >= 2:
                                sq.wait(("sv", acc0_base + m - 1))
                            exq[qt].append(sq.op(
                                lambda sc, mp=mp, half=half, m=m: sc.activation(
                                    ET[:, (m % 2) * CH:(m % 2 + 1) * CH],
                                    bf.pbk[mp][:, half * CH:(half + 1) * CH],
                                    AF.Exp, bias=nmx8[:, m:m + 1],
                                    accum_out=sme8[:, m:m + 1])))
                        ex_base = p.v["sa"] - NCH

                        def _(v, qt=qt, n=n, ex_base=ex_base, AS=AS,
                              gdep=(atg_hist[n % 2] if qt == 0 else None)):
                            p.wait(v, gdep)
                            for m in range(NCH):
                                p.wait(v, ("sa", ex_base + m + 1))
                                v.reciprocal(bf.rcp[:, :], sme8[:, m:m + 1])
                                v.drain()
                                r = bf.rcp
                                if m == n:
                                    v.tensor_scalar_mul(bf.rcp2[:, :],
                                                        bf.rcp[:, :], 2.0)
                                    v.drain()
                                    r = bf.rcp2
                                dst = AS[:, qt * CH:(qt + 1) * CH]
                                esl = ET[:, (m % 2) * CH:(m % 2 + 1) * CH]
                                if m == 0:
                                    ins = v.tensor_scalar(dst, esl, r[:, :], None,
                                                          op0=OP.mult)
                                else:
                                    ins = v.scalar_tensor_tensor(
                                        dst, esl, r[:, :], dst,
                                        op0=OP.mult, op1=OP.add)
                                p.inc(ins, "sv", 1)
                                if m < NCH - 1:
                                    v.drain()
                        blk.vector(_)
                        as_toks[qt] = p.tok("sv")

                    emit_expacc(0, emit_reduce(0, sc_base0))

                    if prev is not None:
                        emit_atgroup(prev)

                    def _(t, qdep=q_toks[par], QTh=QTh, QTl=QTl,
                          ex0=list(exq[0])):
                        p.wait(t, qdep)
                        for mp in range(4):
                            p.wait(t, ex0[2 * mp + 1])
                            bank = bf.pbk[mp]
                            for e in range(EC):
                                qh = QTh[:, e * CH + 128:e * CH + 256]
                                ql = QTl[:, e * CH + 128:e * CH + 256]
                                kh = KTh[e][:, mp * 512:(mp + 1) * 512]
                                kl = KTl[e][:, mp * 512:(mp + 1) * 512]
                                for pi, (ws, xs) in enumerate(PASSES):
                                    mm = t.matmul(bank[:, 0:512],
                                                  qh if ws == 0 else ql,
                                                  kh if xs == 0 else kl,
                                                  start=(e == 0 and pi == 0),
                                                  stop=(e == EC - 1 and pi == 2))
                            p.inc(mm, "st", 1)
                    blk.tensor(_)
                    sc_base1 = p.v["st"] - 4
                    q_use[par] = p.tok("st")

                    nm_base1 = emit_reduce(1, sc_base1)

                    if prev is not None:
                        av_tok, ot_tok = emit_av(prev)
                    emit_expacc(1, nm_base1)
                    as_prev = as_toks[1]

                    if prev is not None:
                        emit_wo(prev, ot_tok)
                        if n + 1 < NA_CHUNKS:
                            def _(sy, n=n, qfree=q_use[(n + 1) % 2], vfree=av_tok):
                                pr = (n + 1) % 2
                                sem_q = "sdc0" if pr == 0 else "sdc1"
                                p.wait(sy, qfree)
                                for dst, srcd in ((QThb[pr], qTh_d),
                                                  (QTlb[pr], qTl_d)):
                                    p.inc(sy.dma_start(
                                        out=dst[:, :].rearrange("p (e q) -> p e q",
                                                                e=EC),
                                        in_=srcd[sqn, :, :,
                                                 (n + 1) * CH:(n + 2) * CH]),
                                        sem_q, 16)
                                p.wait(sy, vfree)
                                for dst, srcd in ((VThb[pr], vTh_d),
                                                  (VTlb[pr], vTl_d)):
                                    p.inc(sy.dma_start(
                                        out=dst[:, :].rearrange("p (a d) -> p a d",
                                                                a=2),
                                        in_=srcd[sqn, 2 * n + 2:2 * n + 4, :, :]
                                        .rearrange("a p d -> p a d")), sem_q, 16)
                            blk.sync(_)
                            q_toks[(n + 1) % 2] = p.tok(
                                "sdc0" if (n + 1) % 2 == 0 else "sdc1")
                            v_toks[(n + 1) % 2] = q_toks[(n + 1) % 2]
                    elif n + 1 < NA_CHUNKS:
                        def _(sy, n=n):
                            pr = (n + 1) % 2
                            sem_q = "sdc0" if pr == 0 else "sdc1"
                            for dst, srcd in ((QThb[pr], qTh_d), (QTlb[pr], qTl_d)):
                                p.inc(sy.dma_start(
                                    out=dst[:, :].rearrange("p (e q) -> p e q",
                                                            e=EC),
                                    in_=srcd[sqn, :, :,
                                             (n + 1) * CH:(n + 2) * CH]),
                                    sem_q, 16)
                            for dst, srcd in ((VThb[pr], vTh_d), (VTlb[pr], vTl_d)):
                                p.inc(sy.dma_start(
                                    out=dst[:, :].rearrange("p (a d) -> p a d",
                                                            a=2),
                                    in_=srcd[sqn, 2 * n + 2:2 * n + 4, :, :]
                                    .rearrange("a p d -> p a d")), sem_q, 16)
                        blk.sync(_)
                        q_toks[(n + 1) % 2] = p.tok(
                            "sdc0" if (n + 1) % 2 == 0 else "sdc1")
                        v_toks[(n + 1) % 2] = q_toks[(n + 1) % 2]

                    prep_stage((4 * n + 2, 4 * n + 3))
                    prep_load((4 * n + 4, 4 * n + 5))

                    prev = {"n": n, "par": par, "AS": AS, "as_tok": as_toks[1]}

                emit_atgroup(prev)
                av_tok, ot_tok = emit_av(prev)
                emit_wo(prev, ot_tok)

                def _(g):
                    p.wait(g, p.tok("sdso"), p.tok("sdo"))
                blk.gpsimd(_)

                def _(sy):
                    p.wait(sy, p.tok("sdwB"), p.tok("sdc0"), p.tok("sdc1"),
                           p.tok("sdsi"), p.tok("sdp0"))
                blk.sync(_)


def _phase_f(nc, p, es, sqn, bf, x_d, s_d, out_d, gw12_d, gw2_d, wc2_d, pj2_d,
             lng_d, lnb_d, oh_d, xnTh_d, xnTl_d, rn_d):
    with ExitStack() as pes:
        sbX = lambda name, shape, dt=F32: pes.enter_context(
            nc.sbuf_tensor(name, shape, dt))
        gw16 = sbX(f"gw16_{sqn}", [128, 2 * EC * HG], BF16)
        gw2_t = sbX(f"gw2_s{sqn}", [128, (HG // 128) * C16])
        lng_t = sbX(f"lng_s{sqn}", [128, D])
        lnb_t = sbX(f"lnb_s{sqn}", [128, D])
        oh_t = sbX(f"oh_s{sqn}", [C16, C16 * 128])
        wcb = [sbX(f"wcb{i}_s{sqn}", [128, 2 * EC * NN], BF16) for i in range(2)]
        pjb = [sbX(f"pjbuf{i}_s{sqn}", [128, 2 * 32 * 128], BF16) for i in range(2)]
        xt = sbX(f"xt_s{sqn}", [128, 2 * D])
        xnThb = [sbX(f"fxnTh{i}_s{sqn}", [128, EC * CH], BF16) for i in range(2)]
        xnTlb = [sbX(f"fxnTl{i}_s{sqn}", [128, EC * CH], BF16) for i in range(2)]
        sTf = sbX(f"sTf_s{sqn}", [128, EC * CH])
        lgT = sbX(f"lgT_s{sqn}", [C16, CH])
        gvT = sbX(f"gvT_s{sqn}", [C16, CH])
        rnbS = sbX(f"rnbS_s{sqn}", [C16, CH])
        ltok = sbX(f"ltok_s{sqn}", [128, 2 * C16])
        wtok = sbX(f"wtok_s{sqn}", [128, 2 * C16])
        mskT = sbX(f"mskT_s{sqn}", [128, 2 * C16])
        exT = sbX(f"exT_s{sqn}", [128, 2 * C16])
        wT16 = sbX(f"wT16_s{sqn}", [C16, CH])
        colsg = sbX(f"colsg_s{sqn}", [128, 32 * CH])
        colh = sbX(f"colh_s{sqn}", [128, 32 * CH], BF16)
        coll = sbX(f"coll_s{sqn}", [128, 32 * CH], BF16)
        ysT = sbX(f"ysT_s{sqn}", [128, EC * CH])
        hT = colsg[:, 24 * CH:32 * CH]
        ytok = sbX(f"ytok_s{sqn}", [128, 2 * D])
        stok = sbX(f"stok_s{sqn}", [128, 2 * D])
        rn_fc = sbX(f"rn_fc_s{sqn}", [1, CH])

        def gwof(e, m):
            return (gw16[:, e * HG + m * 128:e * HG + (m + 1) * 128],
                    gw16[:, EC * HG + e * HG + m * 128:EC * HG + e * HG + (m + 1) * 128])

        with nc.Block() as blk:
            sq = ScalarQ(p, blk)

            HW = EC * HG
            def _(sy):
                p.inc(sy.dma_start(out=gw16[:, 0:HW], in_=gw12_d[:, 0:HW]),
                      "sdwA", 16)
                p.inc(sy.dma_start(out=gw16[:, HW:2 * HW],
                                   in_=gw12_d[:, HW:2 * HW]), "sdk", 16)
                p.inc(sy.dma_start(out=gw2_t[:, :], in_=gw2_d), "sdwA", 16)
                p.inc(sy.dma_start(out=lng_t[:, :], in_=lng_d), "sdwA", 16)
                p.inc(sy.dma_start(out=lnb_t[:, :], in_=lnb_d), "sdwA", 16)
                p.inc(sy.dma_start(out=oh_t[:, :], in_=oh_d), "sdwA", 16)
            blk.sync(_)
            gw_tok = p.tok("sdwA")
            gwk_tok = p.tok("sdk")

            xn_use = [None, None]
            xt_use = [None]
            st_use = [None]
            rn_use = [None]
            cols_prev = [None, None]
            pj_prev = [None]
            hfree = [None]
            hbank_prev = [None, None]
            xnp_toks = [None, None]
            for stile in range(NF_TILES):
                t0 = stile * CH
                par = stile % 2
                sdo_prev = p.tok("sdo")
                xnTh, xnTl = xnThb[par], xnTlb[par]

                if stile == 0:
                    def _(sy):
                        p.inc(sy.dma_start(
                            out=xnThb[0][:, :].rearrange("p (e q) -> p e q", e=EC),
                            in_=xnTh_d[sqn, :, :, 0:CH]), "sdt0", 16)
                        p.inc(sy.dma_start(
                            out=xnTlb[0][:, :].rearrange("p (e q) -> p e q", e=EC),
                            in_=xnTl_d[sqn, :, :, 0:CH]), "sdt1", 16)
                        p.wait(sy, p.tok("sdsr"))
                        p.inc(sy.dma_start(out=rn_fc[0:1, :],
                                           in_=rn_d[sqn, 0:1, 0:CH]), "sdt0", 16)
                    blk.sync(_)
                    xnp_toks[0] = p.tok("sdt0")
                    xnp_toks[1] = p.tok("sdt1")

                def _(sy, t0=t0, par=par, xdep=xt_use[0], sdep=st_use[0]):
                    sx = "sdx0" if par == 0 else "sdx1"
                    ss_ = "sds0" if par == 0 else "sds1"
                    p.wait(sy, sdep)
                    p.inc(sy.dma_start(
                        out=sTf[:, :].rearrange("p (e q) -> p e q", e=EC),
                        in_=s_d[sqn, :, :, t0:t0 + CH]), ss_, 16)
                    p.wait(sy, xdep)
                    p.inc(sy.dma_start(
                        out=xt[:, :].rearrange("p (a d) -> p a d", a=2),
                        in_=x_d[sqn, t0:t0 + CH, :].rearrange(
                            "(a p) d -> p a d", p=128)), sx, 16)
                blk.sync(_)
                x_tok = p.tok("sdx0" if par == 0 else "sdx1")
                xnTh_tok = xnp_toks[0]
                xnTl_tok = xnp_toks[1]
                s_tok_in = p.tok("sds0" if par == 0 else "sds1")

                sa_base = p.v["sa"]

                def _(t, dep0=xnTh_tok, dep1=xnTl_tok, wdep=gw_tok,
                      wdep2=gwk_tok, b0=hbank_prev[0], b1=hbank_prev[1],
                      xnTh=xnTh, xnTl=xnTl):
                    p.wait(t, dep0, dep1, wdep, wdep2, b0, b1)
                    for m in range(HG // 128):
                        if m >= 2:
                            p.wait(t, ("sa", sa_base + m - 1))
                        bank = bf.pbk[m % 2]
                        for e in range(EC):
                            wh, wl = gwof(e, m)
                            xh = xnTh[:, e * CH:(e + 1) * CH]
                            xl = xnTl[:, e * CH:(e + 1) * CH]
                            for pi, (ws, xs) in enumerate(PASSES):
                                mm = t.matmul(bank[:, 0:CH],
                                              wh if ws == 0 else wl,
                                              xh if xs == 0 else xl,
                                              start=(e == 0 and pi == 0),
                                              stop=(e == EC - 1 and pi == 2))
                        p.inc(mm, "st", 1)
                blk.tensor(_)
                h_base = p.v["st"] - HG // 128

                hT_tok = None
                sq.wait(hfree[0])
                for m in range(HG // 128):
                    sq.wait(("st", h_base + m + 1))
                    hT_tok = sq.op(lambda sc, m=m: sc.activation(
                        hT[:, m * CH:(m + 1) * CH], bf.pbk[m % 2][:, 0:CH],
                        AF.Gelu, bias=bf.gb1_t[:, m:m + 1]))

                def _(t, dep=hT_tok, rdep=xnTh_tok, xnTh=xnTh, xnTl=xnTl):
                    p.wait(t, dep, rdep, p.tok("sv"))
                    for k in range(HG // 128):
                        mm = t.matmul(bf.pbk[2][0:C16, 0:CH],
                                      gw2_t[:, k * C16:(k + 1) * C16],
                                      hT[:, k * CH:(k + 1) * CH],
                                      start=(k == 0), stop=(k == HG // 128 - 1))
                    p.inc(mm, "st", 1)
                    first = True
                    for e in range(EC):
                        ch_ = bf.cnTh[:, e * C16:(e + 1) * C16]
                        cl_ = bf.cnTl[:, e * C16:(e + 1) * C16]
                        xh = xnTh[:, e * CH:(e + 1) * CH]
                        xl = xnTl[:, e * CH:(e + 1) * CH]
                        for ws, xs in PASSES:
                            mm = t.matmul(bf.pbk[3][0:C16, 0:CH],
                                          ch_ if ws == 0 else cl_,
                                          xh if xs == 0 else xl,
                                          start=first,
                                          stop=(e == EC - 1 and (ws, xs) == PASSES[-1]))
                            first = False
                    p.inc(mm, "st", 1)
                    p.wait(t, p.tok("sgc"))
                    p.inc(t.matmul(bf.pbk[3][0:C16, CH:2 * CH], bf.ones1[0:1, :],
                                   rn_fc[0:1, :], start=True, stop=True), "st", 1)
                blk.tensor(_)
                gv_tok = ("st", p.v["st"] - 2)
                rnb_tok = p.tok("st")
                rn_use[0] = rnb_tok

                sq.wait(gv_tok)
                gvtanh_tok = sq.op(lambda sc: sc.activation(
                    gvT[:, :], bf.pbk[2][0:C16, 0:CH],
                    AF.Tanh, bias=bf.gb2_t[:, :], scale=0.5))
                sq.wait(rnb_tok)
                gv_sb_tok = sq.op(lambda sc: sc.copy(rnbS[:, :],
                                                     bf.pbk[3][0:C16, CH:2 * CH]))

                def _(v, dep=gv_sb_tok):
                    p.wait(v, dep)
                    v.tensor_scalar(gvT[:, :], gvT[:, :], 0.5, 0.5,
                                    op0=OP.mult, op1=OP.add)
                    v.drain()
                    v.tensor_mul(lgT[:, :], bf.pbk[3][0:C16, 0:CH], rnbS[:, :])
                    v.drain()
                    p.inc(v.tensor_add(lgT[:, :], lgT[:, :], gvT[:, :]), "sv", 1)
                blk.vector(_)
                lg_tok = p.tok("sv")

                def _(sy, dep0=xnTh_tok, dep1=xnTl_tok):
                    p.wait(sy, dep0, dep1)
                    p.inc(sy.dma_start(out=wcb[0][:, :], in_=wc2_d[0, :, :]),
                          "sdc0", 16)
                blk.sync(_)
                wc_toks = [p.tok("sdc0"), None]

                def _(sy, dep=pj_prev[0]):
                    p.wait(sy, dep)
                    p.inc(sy.dma_start(out=pjb[0][:, :], in_=pj2_d[0, :, :]),
                          "sdp0", 16)
                blk.sync(_)
                pj_toks = [p.tok("sdp0"), None]

                split_toks = []
                rt_state = {}

                def emit_bcast(cp):
                    def _(t, cp=cp):
                        p.wait(t, rt_state["wT_tok"])
                        if cp >= 2:
                            p.wait(t, split_toks[2 * (cp - 2) + 1])
                        p.inc(t.matmul(bf.pbk[2 + (cp % 2)][:, 0:CH],
                                       oh_t[:, cp * 128:(cp + 1) * 128],
                                       wT16[:, :], start=True, stop=True), "st", 1)
                    blk.tensor(_)
                    bc_tok = p.tok("st")
                    for cm in range(2):
                        ti = cp * 2 + cm
                        csl = colsg[:, ti * CH:(ti + 1) * CH]
                        hsl = colh[:, ti * CH:(ti + 1) * CH]
                        lsl = coll[:, ti * CH:(ti + 1) * CH]

                        def _(v, cp=cp, csl=csl, dep1=gelu_toks[ti], dep2=bc_tok):
                            p.wait(v, dep1, dep2)
                            p.inc(v.tensor_mul(csl, csl,
                                               bf.pbk[2 + (cp % 2)][:, 0:CH]),
                                  "sv", 1)
                        blk.vector(_)
                        mul_tok = p.tok("sv")

                        sq.wait(mul_tok)
                        h_tok = sq.op(lambda sc, hsl=hsl, csl=csl: sc.copy(hsl, csl))

                        def _(v, csl=csl, hsl=hsl, lsl=lsl, dep=h_tok):
                            p.wait(v, dep)
                            p.inc(v.tensor_sub(lsl, csl, hsl), "sv", 1)
                        blk.vector(_)
                        split_toks.append(p.tok("sv"))

                cols_mm_toks = []
                gelu_toks = []
                for c in range(C16):
                    def _(sy, c=c):
                        if c + 1 < C16:
                            sem = "sdc0" if (c + 1) % 2 == 0 else "sdc1"
                            if c >= 1:
                                p.wait(sy, cols_mm_toks[c - 1])
                            p.inc(sy.dma_start(out=wcb[(c + 1) % 2][:, :],
                                               in_=wc2_d[c + 1, :, :]), sem, 16)
                    blk.sync(_)
                    if c + 1 < C16:
                        wc_toks[(c + 1) % 2] = p.tok("sdc0" if (c + 1) % 2 == 0 else "sdc1")

                    def _(t, c=c, dep=wc_toks[c % 2], xnTh=xnTh, xnTl=xnTl,
                          b4=(cols_prev[0] if c == 0 else gelu_toks[2 * (c - 1)]),
                          b5=(cols_prev[1] if c == 0 else gelu_toks[2 * (c - 1) + 1])):
                        p.wait(t, dep, b4, b5)
                        wb16 = wcb[c % 2]
                        for cm in range(2):
                            bank = bf.pbk[4 + cm]
                            for e in range(EC):
                                wh = wb16[:, e * NN + cm * 128:e * NN + (cm + 1) * 128]
                                wl = wb16[:, EC * NN + e * NN + cm * 128:
                                          EC * NN + e * NN + (cm + 1) * 128]
                                xh = xnTh[:, e * CH:(e + 1) * CH]
                                xl = xnTl[:, e * CH:(e + 1) * CH]
                                for pi, (ws, xs) in enumerate(PASSES):
                                    mm = t.matmul(bank[:, 0:CH],
                                                  wh if ws == 0 else wl,
                                                  xh if xs == 0 else xl,
                                                  start=(e == 0 and pi == 0),
                                                  stop=(e == EC - 1 and pi == 2))
                            p.inc(mm, "st", 1)
                    blk.tensor(_)
                    cc_base = p.v["st"] - 2
                    cols_mm_toks.append(p.tok("st"))

                    for cm in range(2):
                        sq.wait(("st", cc_base + cm + 1))
                        gelu_toks.append(sq.op(lambda sc, c=c, cm=cm: sc.activation(
                            colsg[:, (c * 2 + cm) * CH:(c * 2 + cm + 1) * CH],
                            bf.pbk[4 + cm][:, 0:CH], AF.Gelu,
                            bias=bf.bc_t[:, c * 2 + cm:c * 2 + cm + 1])))

                    if c == 0:
                        def _(t, dep=lg_tok, gdep=gvtanh_tok):
                            p.wait(t, dep, gdep)
                            for tt in range(2):
                                p.inc(t.transpose(
                                    bf.pbk[2][0:128, tt * C16:(tt + 1) * C16],
                                    lgT[0:C16, tt * 128:(tt + 1) * 128],
                                    bf.id_t[0:C16, 0:C16]), "st", 1)
                        blk.tensor(_)
                        lt_tok = p.tok("st")
                        sq.wait(lt_tok)
                        ltok_tok = None
                        for tt in range(2):
                            ltok_tok = sq.op(lambda sc, tt=tt: sc.copy(
                                ltok[:, tt * C16:(tt + 1) * C16],
                                bf.pbk[2][0:128, tt * C16:(tt + 1) * C16]))
                        rt_state["ltok_tok"] = ltok_tok
                    elif c == 1:
                        sq.flush()
                        w_toks = []
                        for tt in range(2):
                            sl = ltok[:, tt * C16:(tt + 1) * C16]
                            wl = wtok[:, tt * C16:(tt + 1) * C16]

                            def _(v, sl=sl, tt=tt, dep=rt_state["ltok_tok"]):
                                p.wait(v, dep)
                                v.reduce_max(bf.mxv[:, :], sl, axis=AX)
                                v.reduce_max(bf.nmx[:, :], sl, axis=AX, negate=True)
                                v.drain()
                                p.inc(v.tensor_scalar(mskT[:, tt * C16:(tt + 1) * C16],
                                                      sl, bf.mxv[:, :], None,
                                                      op0=OP.is_ge), "sv", 1)
                            blk.vector(_)
                            msk_tok = p.tok("sv")

                            sq.wait(msk_tok)
                            ex_tok = sq.op(lambda sc, sl=sl, tt=tt: sc.activation(
                                exT[:, tt * C16:(tt + 1) * C16], sl, AF.Exp,
                                bias=bf.nmx[:, :], accum_out=bf.sme[:, :]))

                            def _(v, wl=wl, tt=tt, dep=ex_tok):
                                p.wait(v, dep)
                                v.reciprocal(bf.rcp[:, :], bf.sme[:, :])
                                v.drain()
                                p.inc(v.scalar_tensor_tensor(
                                    wl, exT[:, tt * C16:(tt + 1) * C16], bf.rcp[:, :],
                                    mskT[:, tt * C16:(tt + 1) * C16],
                                    op0=OP.mult, op1=OP.mult), "sv", 1)
                            blk.vector(_)
                            w_toks.append(p.tok("sv"))
                        rt_state["w_tok"] = w_toks[-1]
                        sq.flush()
                    elif c == 2:
                        def _(t, dep=rt_state["w_tok"], ldep=rt_state["ltok_tok"]):
                            p.wait(t, dep, ldep)
                            for tt in range(2):
                                p.inc(t.transpose(
                                    bf.pbk[2][0:C16, tt * 128:(tt + 1) * 128],
                                    wtok[:, tt * C16:(tt + 1) * C16],
                                    bf.id_t[:, :]), "st", 1)
                        blk.tensor(_)
                        wt_tok = p.tok("st")
                        sq.wait(wt_tok)
                        wT_tok = None
                        for tt in range(2):
                            wT_tok = sq.op(lambda sc, tt=tt: sc.copy(
                                wT16[:, tt * 128:(tt + 1) * 128],
                                bf.pbk[2][0:C16, tt * 128:(tt + 1) * 128]))
                        rt_state["wT_tok"] = wT_tok
                    elif c == 3:
                        emit_bcast(0)
                        emit_bcast(1)
                    else:
                        emit_bcast(c - 2)
                        if c == C16 - 1:
                            emit_bcast(14)
                            emit_bcast(15)
                cols_tok = split_toks[-1]
                xn_use[par] = cols_tok
                hfree[0] = cols_tok
                cols_prev[0] = gelu_toks[-2]
                cols_prev[1] = gelu_toks[-1]
                sq.flush()

                if stile + 1 < NF_TILES:
                    def _(sy, t1=(stile + 1) * CH, npar=(stile + 1) % 2,
                          ndep=xn_use[(stile + 1) % 2], rdep=rn_use[0]):
                        p.wait(sy, ndep)
                        p.inc(sy.dma_start(
                            out=xnThb[npar][:, :].rearrange("p (e q) -> p e q",
                                                            e=EC),
                            in_=xnTh_d[sqn, :, :, t1:t1 + CH]), "sdt0", 16)
                        p.inc(sy.dma_start(
                            out=xnTlb[npar][:, :].rearrange("p (e q) -> p e q",
                                                            e=EC),
                            in_=xnTl_d[sqn, :, :, t1:t1 + CH]), "sdt1", 16)
                        p.wait(sy, rdep)
                        p.inc(sy.dma_start(out=rn_fc[0:1, :],
                                           in_=rn_d[sqn, 0:1, t1:t1 + CH]),
                              "sdt0", 16)
                    blk.sync(_)
                    xnp_toks[0] = p.tok("sdt0")
                    xnp_toks[1] = p.tok("sdt1")

                ye_toks = []
                for m in range(EC):
                    def _(sy, m=m):
                        if m + 1 < EC:
                            sem = "sdp0" if (m + 1) % 2 == 0 else "sdp1"
                            if m >= 1:
                                p.wait(sy, ye_toks[m - 1])
                            p.inc(sy.dma_start(out=pjb[(m + 1) % 2][:, :],
                                               in_=pj2_d[m + 1, :, :]), sem, 16)
                    blk.sync(_)
                    if m + 1 < EC:
                        pj_toks[(m + 1) % 2] = p.tok("sdp0" if (m + 1) % 2 == 0 else "sdp1")

                    def _(t, m=m, dep=pj_toks[m % 2], cdep=cols_tok,
                          hdep=hT_tok):
                        if m == 0:
                            p.wait(t, dep, hdep)
                        else:
                            p.wait(t, dep, cdep, p.tok("sa"), p.tok("sv"))
                        bank = bf.pbk[m % 4]
                        pb16 = pjb[m % 2]
                        for k in range(32):
                            if m == 0:
                                p.wait(t, split_toks[k])
                            wh = pb16[:, k * 128:(k + 1) * 128]
                            wlo = pb16[:, 32 * 128 + k * 128:32 * 128 + (k + 1) * 128]
                            xh = colh[:, k * CH:(k + 1) * CH]
                            xl = coll[:, k * CH:(k + 1) * CH]
                            for pi, (ws, xs) in enumerate(PASSES):
                                mm = t.matmul(bank[:, 0:CH],
                                              wh if ws == 0 else wlo,
                                              xh if xs == 0 else xl,
                                              start=(k == 0 and pi == 0),
                                              stop=(k == 31 and pi == 2))
                        p.inc(mm, "st", 1)
                    blk.tensor(_)
                    y_tok = p.tok("st")

                    if m == 0:
                        sq.wait(xt_use[0])
                    sq.wait(y_tok)
                    ye_toks.append(sq.op(lambda sc, m=m: sc.activation(
                        ysT[:, m * CH:(m + 1) * CH], bf.pbk[m % 4][:, 0:CH],
                        AF.Identity, bias=bf.pjb_t[:, m:m + 1])))
                if True:
                    pj_prev[0] = ye_toks[-2]
                    hbank_prev[0] = ye_toks[4]
                    hbank_prev[1] = ye_toks[5]

                def _(v, dep=ye_toks[-1], xnTh=xnTh, xnTl=xnTl):
                    p.wait(v, dep)
                    for e in range(EC):
                        v.tensor_add(ysT[:, e * CH:(e + 1) * CH],
                                     ysT[:, e * CH:(e + 1) * CH],
                                     xnTh[:, e * CH:(e + 1) * CH])
                        v.drain()
                        ins = v.tensor_add(ysT[:, e * CH:(e + 1) * CH],
                                           ysT[:, e * CH:(e + 1) * CH],
                                           xnTl[:, e * CH:(e + 1) * CH])
                        if e < EC - 1:
                            v.drain()
                    p.inc(ins, "sv", 1)
                blk.vector(_)
                yres_tok = p.tok("sv")

                sa_base4 = p.v["sa"]

                def _(t, dep=yres_tok):
                    p.wait(t, dep, p.tok("sa"))
                    for e in range(EC):
                        if e >= 2:
                            p.wait(t, ("sa", sa_base4 + 2 * (e - 1)))
                        for tt in range(2):
                            p.inc(t.transpose(bf.pbk[6 + (e % 2)][:, tt * 128:tt * 128 + 128],
                                              ysT[:, e * CH + tt * 128:e * CH + tt * 128 + 128],
                                              bf.id_t[:, :]), "st", 1)
                blk.tensor(_)
                ytr_base = p.v["st"] - 16

                ytok_tok = None
                sq.wait(sdo_prev)
                for e in range(EC):
                    sq.wait(("st", ytr_base + 2 * (e + 1)))
                    for tt in range(2):
                        ytok_tok = sq.op(lambda sc, e=e, tt=tt: sc.copy(
                            ytok[:, tt * D + e * 128:tt * D + (e + 1) * 128],
                            bf.pbk[6 + (e % 2)][:, tt * 128:tt * 128 + 128]))

                sa_base5 = p.v["sa"]

                def _(t, dep=ytok_tok, sdep=s_tok_in):
                    p.wait(t, dep, sdep)
                    for e in range(EC):
                        if e >= 2:
                            p.wait(t, ("sa", sa_base5 + 2 * (e - 1)))
                        for tt in range(2):
                            p.inc(t.transpose(bf.pbk[6 + (e % 2)][:, tt * 128:tt * 128 + 128],
                                              sTf[:, e * CH + tt * 128:e * CH + tt * 128 + 128],
                                              bf.id_t[:, :]), "st", 1)
                blk.tensor(_)
                str_base = p.v["st"] - 16

                stok_tok = None
                for e in range(EC):
                    sq.wait(("st", str_base + 2 * (e + 1)))
                    for tt in range(2):
                        stok_tok = sq.op(lambda sc, e=e, tt=tt: sc.copy(
                            stok[:, tt * D + e * 128:tt * D + (e + 1) * 128],
                            bf.pbk[6 + (e % 2)][:, tt * 128:tt * 128 + 128]))
                st_use[0] = stok_tok

                lnts = []
                for tt in range(2):
                    lnts.append(_ln_tile(p, blk, sq, ytok[:, tt * D:(tt + 1) * D],
                                         ytok[:, tt * D:(tt + 1) * D], bf,
                                         pre_deps=(ytok_tok, p.tok("sa")),
                                         gamma=lng_t, beta=lnb_t))
                for tt in range(2):
                    gsl = ytok[:, tt * D:(tt + 1) * D]
                    sq.wait(lnts[tt])
                    g_tok = sq.op(lambda sc, tt=tt, gsl=gsl: sc.activation(
                        ysT[:, tt * D:(tt + 1) * D], gsl, AF.Tanh, scale=0.5))

                    def _(v, tt=tt, gsl=gsl, dep=g_tok, sdep=stok_tok, xdep=x_tok):
                        p.wait(v, dep, sdep, xdep)
                        v.tensor_scalar(ysT[:, tt * D:(tt + 1) * D],
                                        ysT[:, tt * D:(tt + 1) * D], 0.5, 0.5,
                                        op0=OP.mult, op1=OP.add)
                        v.drain()
                        v.tensor_mul(gsl, gsl, ysT[:, tt * D:(tt + 1) * D])
                        v.drain()
                        v.tensor_mul(gsl, gsl, stok[:, tt * D:(tt + 1) * D])
                        v.drain()
                        p.inc(v.tensor_add(gsl, gsl, xt[:, tt * D:(tt + 1) * D]),
                              "sv", 1)
                    blk.vector(_)
                out_tok = p.tok("sv")
                xt_use[0] = out_tok
                sq.flush()

                def _(g, t0=t0, dep=out_tok):
                    p.wait(g, dep)
                    p.inc(g.dma_start(out=out_d[sqn, t0:t0 + CH, :].rearrange(
                        "(a p) d -> p a d", p=128),
                        in_=ytok[:, :].rearrange("p (a d) -> p a d", a=2)), "sdo", 16)
                blk.gpsimd(_)

            def _(sy):
                p.wait(sy, p.tok("sdwA"), p.tok("sdk"), p.tok("sdc0"),
                       p.tok("sdc1"), p.tok("sdp0"), p.tok("sdp1"), p.tok("sdsi"),
                       p.tok("sdt0"), p.tok("sdt1"), p.tok("sds0"), p.tok("sds1"),
                       p.tok("sdx0"), p.tok("sdx1"))
            blk.sync(_)

            def _(g):
                p.wait(g, p.tok("sdo"))
            blk.gpsimd(_)


def _host_layouts(inputs):
    f = np.ascontiguousarray
    L = {}
    for nm, key in (("wq", "L0_attn_Wq"), ("wk", "L0_attn_Wk"),
                    ("wv", "L0_attn_Wv"), ("wo", "L0_attn_Wo")):
        L[nm] = f(inputs[key].reshape(EC, 128, D).transpose(1, 0, 2).reshape(128, EC * D))
    L["bq"] = f(inputs["L0_attn_bq"].reshape(EC, 128).T)
    L["bk"] = f(inputs["L0_attn_bk"].reshape(EC, 128).T)
    L["bo"] = f(inputs["L0_attn_bo"].reshape(EC, 128).T)
    L["pjb"] = f(inputs["L0_ffn_proj_b"].reshape(EC, 128).T)
    L["bvb"] = f(np.tile(inputs["L0_attn_bv"][None, :], (128, 1)))
    L["lng"] = f(np.tile(inputs["L0_ffn_ln_g"][None, :], (128, 1)))
    L["lnb"] = f(np.tile(inputs["L0_ffn_ln_b"][None, :], (128, 1)))
    L["ce"] = f(inputs["L0_ffn_col_emb"])
    L["gw1"] = f(inputs["L0_ffn_gate_w1"].reshape(EC, 128, HG).transpose(1, 0, 2).reshape(128, EC * HG))
    L["gb1"] = f(inputs["L0_ffn_gate_b1"].reshape(HG // 128, 128).T)
    L["gw2"] = f(inputs["L0_ffn_gate_w2"].reshape(HG // 128, 128, C16).transpose(1, 0, 2).reshape(128, (HG // 128) * C16))
    L["gb2"] = f(inputs["L0_ffn_gate_b2"].reshape(C16, 1))
    L["wc"] = f(inputs["L0_ffn_W_cols"].reshape(C16, EC, 128, NN).transpose(0, 2, 1, 3).reshape(C16, 128, EC * NN))
    L["bc"] = f(inputs["L0_ffn_b_cols"].reshape(C16, 2, 128).transpose(2, 0, 1).reshape(128, C16 * 2))
    L["pj"] = f(inputs["L0_ffn_proj_w"].reshape(32, 128, EC, 128).transpose(2, 1, 0, 3).reshape(EC, 128, 32 * 128))
    L["ident"] = np.eye(128, dtype=np.float32)
    oh = np.zeros((C16, C16 * 128), dtype=np.float32)
    for c in range(C16):
        oh[c, c * 128:(c + 1) * 128] = 1.0
    L["oh"] = oh
    return L


def kernel(**inputs):
    nc = build_core_program()
    L = _host_layouts(inputs)
    x = np.ascontiguousarray(np.asarray(inputs["L0_attn_x"], dtype=np.float32))
    in_maps = []
    for ci in range(NC_):
        m = dict(L)
        m["x"] = np.ascontiguousarray(x[ci * BPC:(ci + 1) * BPC])
        in_maps.append(m)
    res = run_bass_kernel_spmd(nc, in_maps, core_ids=list(range(NC_)))
    return np.concatenate([r["out"] for r in res.results], axis=0)
